# version 05
# baseline (speedup 1.0000x reference)
import numpy as np
import ml_dtypes
import concourse.bass as bass
import concourse.mybir as mybir
from concourse.bass_utils import run_bass_kernel_spmd

F32 = mybir.dt.float32
BF16 = mybir.dt.bfloat16
AF = mybir.ActivationFunctionType
ALU = mybir.AluOpType
AX = mybir.AxisListType.X

T = 2048
D = 1024
NCORE = 8
NPBF = ml_dtypes.bfloat16

ALL_BLOCKS = ["l0a", "l0f", "l1a", "l1f", "l2a", "l2f", "l3a", "l3f"]
L1A_PHASES = 7
L1A_CHUNKS = 8
L1A_THETA = 3
L3A_CKS = 8
L3A_SMAX = 16
FUSED = True
ARENA_N = 27648

ENGS = ("T", "V", "S", "Q", "G")


def _coltile(w):
    K, M = w.shape
    assert K % 128 == 0 and M % 128 == 0, (K, M)
    t = w.reshape(K // 128, 128, M // 128, 128)
    return np.ascontiguousarray(t.transpose(2, 1, 0, 3))


def _cols(v):
    n = v.shape[0] // 128
    return np.ascontiguousarray(v.reshape(n, 128).T)


def prep_inputs(inputs):
    f = lambda a: np.ascontiguousarray(np.asarray(a, np.float32))
    bf = lambda a: np.ascontiguousarray(np.asarray(a, NPBF))
    I = inputs
    m = {}
    m["c_ident"] = np.eye(128, dtype=np.float32)
    m["c_identb"] = bf(np.eye(128))
    m["c_onescol"] = np.ones((128, 1), np.float32)
    m["c_onescolb"] = bf(np.ones((128, 1)))
    m["c_onesrow"] = np.ones((96, 128), np.float32)
    m["c_onesrowb"] = bf(np.ones((96, 128)))
    hd = 64
    inv_freq = 1.0 / (10000.0 ** (np.arange(0, hd, 2, dtype=np.float64) / hd))
    ang = np.arange(T, dtype=np.float64)[None, :] * inv_freq[:, None]
    Cf = np.empty((64, T)); Sf = np.empty((64, T))
    Cf[0::2] = np.cos(ang); Cf[1::2] = np.cos(ang)
    Sf[0::2] = -np.sin(ang); Sf[1::2] = np.sin(ang)
    m["c_ropeC"] = bf(np.tile(Cf, (2, 1)))
    m["c_ropeS"] = bf(np.tile(Sf, (2, 1)))
    P = np.zeros((128, 128))
    for dd in range(128):
        P[dd + 1 if dd % 2 == 0 else dd - 1, dd] = 1.0
    m["c_swap"] = bf(P)
    st = np.zeros((128, 896))
    for i in range(128):
        st[i, i + 384:] = 1.0
    m["c_stair"] = bf(st)
    hs = np.zeros((128, 128)); hs[0:64, 0:64] = 1.0; hs[64:128, 64:128] = 1.0
    m["c_hsum"] = bf(hs)
    ep = np.zeros((8, 128), np.float32)
    for e in range(8):
        ep[e, e * 16:(e + 1) * 16] = 1.0
    m["c_erpat"] = ep

    for nm in ("wq", "wk", "wo"):
        m[f"l0_{nm}"] = _coltile(f(I[f"L0_attn_{nm}"]))
    m["l0_wv_rows"] = f(I["L0_attn_wv"])
    m["l0f_wg"] = np.ascontiguousarray(
        f(I["L0_ffn_Wg"]).reshape(8, 128, 8).transpose(1, 0, 2).reshape(128, 64))
    m["l0f_a"] = np.ascontiguousarray(
        f(I["L0_ffn_A"]).transpose(1, 0, 2).reshape(8, 128, 128)
        .transpose(1, 0, 2).reshape(128, 1024))
    m["l0f_b"] = f(I["L0_ffn_B"]).reshape(128, D)
    for nm in ("Wq", "Wk", "Wo"):
        m[f"l1_{nm.lower()}"] = _coltile(np.ascontiguousarray(f(I[f"L1_attn_{nm}"]).T))
    m["l1_wv_rows"] = np.ascontiguousarray(f(I["L1_attn_Wv"]).T)
    m["l1_bq"] = _cols(f(I["L1_attn_bq"]))
    m["l1_bk"] = _cols(f(I["L1_attn_bk"]))
    m["l1_bv_row"] = f(I["L1_attn_bv"]).reshape(1, D)
    m["l1_bo"] = _cols(f(I["L1_attn_bo"]))
    m["l1f_wg"] = np.ascontiguousarray(
        f(I["L1_ffn_gate_w"]).reshape(8, 128, 8).transpose(1, 0, 2).reshape(128, 64))
    m["l1f_w1"] = np.stack([_coltile(f(I["L1_ffn_w1"][e])) for e in range(8)])
    m["l1f_w2"] = np.stack([_coltile(f(I["L1_ffn_w2"][e])) for e in range(8)])
    m["l1f_b1"] = np.stack([_cols(f(I["L1_ffn_b1"][e])) for e in range(8)])
    m["l1f_b2"] = np.stack([_cols(f(I["L1_ffn_b2"][e])) for e in range(8)])
    for nm in ("Wq", "Wk", "Wv", "Wo"):
        m[f"l2_{nm.lower()}"] = _coltile(f(I[f"L2_attn_{nm}"]))
    for nm in ("bq", "bk", "bv", "bo"):
        m[f"l2_{nm}"] = _cols(f(I[f"L2_attn_{nm}"]))
    m["l2f_wg"] = np.ascontiguousarray(
        f(I["L2_ffn_Wg"]).T.reshape(8, 128, 8).transpose(1, 0, 2).reshape(128, 64))
    m["l2f_w1"] = np.stack([_coltile(f(I["L2_ffn_w1"][e])) for e in range(8)])
    m["l2f_w2"] = np.stack([_coltile(f(I["L2_ffn_w2"][e])) for e in range(8)])
    m["l2f_b1"] = np.stack([_cols(f(I["L2_ffn_b1"][e])) for e in range(8)])
    m["l2f_b2"] = np.stack([_cols(f(I["L2_ffn_b2"][e])) for e in range(8)])
    m["l3_inproj"] = _coltile(f(I["L3_attn_in_proj_w"]))
    m["l3_inproj_b"] = _cols(f(I["L3_attn_in_proj_b"]))
    m["l3_convw"] = np.ascontiguousarray(
        f(I["L3_attn_conv_w"]).reshape(16, 128, 4).transpose(1, 0, 2).reshape(128, 64))
    m["l3_convb"] = _cols(f(I["L3_attn_conv_b"]))
    xp = f(I["L3_attn_x_proj_w"])
    m["l3_xp_head"] = np.ascontiguousarray(xp[:, 0:32].reshape(16, 128, 32).transpose(1, 0, 2))
    m["l3_xp_u"] = _coltile(np.ascontiguousarray(xp[:, 32:2080]))
    xb = f(I["L3_attn_x_proj_b"])
    m["l3_xpb_head"] = np.ascontiguousarray(xb[0:32].reshape(32, 1))
    m["l3_xpb_u"] = _cols(xb[32:2080])
    m["l3_dt"] = f(I["L3_attn_dt_proj_w"])
    m["l3_dtb"] = _cols(f(I["L3_attn_dt_proj_b"]))
    m["l3_D"] = _cols(f(I["L3_attn_D"]))
    m["l3_outproj"] = _coltile(f(I["L3_attn_out_proj_w"]))
    m["l3_outproj_b"] = _cols(f(I["L3_attn_out_proj_b"]))
    for nm in ("w1", "w2", "h1", "h2", "w3"):
        m[f"l3f_{nm}"] = _coltile(np.ascontiguousarray(f(I[f"L3_ffn_{nm}_w"]).T))
    m["l3f_s1"] = _cols(f(I["L3_ffn_w1_s"]))
    m["l3f_s2"] = _cols(f(I["L3_ffn_w2_s"]))
    m["l3f_sh1"] = _cols(f(I["L3_ffn_h1_s"]))
    m["l3f_sh2"] = _cols(f(I["L3_ffn_h2_s"]))
    m["l3f_s3"] = _cols(f(I["L3_ffn_w3_s"]))

    x = f(I["L0_attn_x"])
    xts = [np.ascontiguousarray(x[c].T) for c in range(NCORE)]
    return m, xts


class Reg:
    __slots__ = ("arena", "lo", "hi", "writers", "readers", "excl_read")

    def __init__(self, arena, lo, hi, excl_read=False):
        self.arena, self.lo, self.hi = arena, lo, hi
        self.writers = {}
        self.readers = {}
        self.excl_read = excl_read


class V:
    __slots__ = ("ap", "regs")

    def __init__(self, ap, regs):
        self.ap = ap
        self.regs = regs if isinstance(regs, list) else [regs]

    def __getitem__(self, idx):
        return V(self.ap[idx], self.regs)


class Arena:
    def __init__(self, P, name, f32_elems):
        self.P = P
        self.name = name
        self.tf = P.nc.alloc_sbuf_tensor(name, [128, f32_elems], F32)
        self.tb = self.tf.bitcast(BF16)
        self.size = f32_elems * 4
        self.regs = []

    def v(self, off_f32, parts, free, dtype=F32):
        esz = 4 if dtype == F32 else 2
        lo = off_f32 * 4
        hi = lo + free * esz
        assert hi <= self.size, (self.name, off_f32, free, dtype)
        r = Reg(self, lo, hi)
        self.regs.append(r)
        if dtype == F32:
            ap = self.tf[0:parts, off_f32:off_f32 + free]
        else:
            ap = self.tb[0:parts, off_f32 * 2:off_f32 * 2 + free]
        return V(ap, r)

    def consolidate(self):
        if not self.regs:
            return
        r = Reg(self, 0, self.size)
        for o in self.regs:
            for e, c in o.writers.items():
                r.writers[e] = max(r.writers.get(e, 0), c)
            for e, c in o.readers.items():
                r.readers[e] = max(r.readers.get(e, 0), c)
        self.regs = [r]


class _Solo:
    regs = None


class Prog:
    NSLOT = 8

    def __init__(self, nc, same_engine_waits=True):
        self.nc = nc
        keys = list(ENGS)
        for qn in ("Q", "G", "S", "V"):
            keys += [f"{qn}#{i}" for i in range(self.NSLOT)]
        self.sem = {e: nc.alloc_semaphore(f"sem_{e.replace('#','_')}") for e in keys}
        self.cnt = {e: 0 for e in keys}
        self.q = {e: [] for e in ENGS}
        self.waited = {e: {e2: 0 for e2 in keys} for e in ENGS}
        self._slot_rr = {"Q": 0, "G": 0, "S": 0, "V": 0}
        self.sew = same_engine_waits
        self.psum = []
        self.psreg = []
        for i in range(8):
            self.psum.append(nc.alloc_psum_tensor(f"ps{i}", [128, 512], F32))
            self.psreg.append(Reg(_Solo(), 0, 1, excl_read=True))
        self._bank_rr = 0
        self._dma_rr = 0
        self.rr_banks = [0, 1, 2, 3, 4, 5, 6]

    def ps(self, i, parts=128, free=512, off=0):
        return V(self.psum[i][0:parts, off:off + free], self.psreg[i])

    def take_bank(self, n=1):
        out = []
        for _ in range(n):
            out.append(self.rr_banks[self._bank_rr % len(self.rr_banks)])
            self._bank_rr += 1
        return out if n > 1 else out[0]

    def stat(self, name, shape, dtype):
        t = self.nc.alloc_sbuf_tensor(name, shape, dtype)
        return V(t[:, :], Reg(_Solo(), 0, 1)), t

    def _expand(self, regs):
        out = []
        for r in regs:
            ar = r.arena
            if isinstance(ar, Arena):
                for o in ar.regs:
                    if r.lo < o.hi and o.lo < r.hi:
                        out.append(o)
            else:
                out.append(r)
        return out

    def op(self, eng, emit, reads=(), writes=(), inc=1, key=None):
        key = key or eng
        rr = [r for x in reads for r in (x.regs if isinstance(x, V) else [x])]
        ww = [r for x in writes for r in (x.regs if isinstance(x, V) else [x])]
        err = self._expand(rr)
        eww = self._expand(ww)
        deps = {}
        for r in err:
            for e, c in r.writers.items():
                deps[e] = max(deps.get(e, 0), c)
            if r.excl_read:
                for e, c in r.readers.items():
                    if e != eng:
                        deps[e] = max(deps.get(e, 0), c)
        for w in eww:
            for e, c in w.writers.items():
                deps[e] = max(deps.get(e, 0), c)
            for e, c in w.readers.items():
                deps[e] = max(deps.get(e, 0), c)
        waits = []
        for e, c in deps.items():
            if e == eng and not self.sew:
                continue
            if self.waited[eng][e] >= c:
                continue
            self.waited[eng][e] = c
            waits.append((self.sem[e], c))
        sem_self = self.sem[key]
        cnt_after = self.cnt[key] + inc

        def closure(E, waits=waits, emit=emit, inc=inc):
            for s, c in waits:
                E.wait_ge(s, c)
            ins = emit(E)
            if inc and ins is not None:
                ins.then_inc(sem_self, inc)

        self.q[eng].append(closure)
        if inc:
            self.cnt[key] = cnt_after
            for r in eww:
                r.writers = {key: cnt_after}
                r.readers = {}
            for r in err:
                r.readers[key] = cnt_after
        return cnt_after

    def raw_wait(self, eng, sem_key, cnt):
        if cnt <= 0 or self.waited[eng][sem_key] >= cnt:
            return
        self.waited[eng][sem_key] = cnt
        s = self.sem[sem_key]
        self.q[eng].append(lambda E, s=s, c=cnt: E.wait_ge(s, c))

    mm_count = 0

    def t_mm(self, bank_v, lhsT, rhs, start, stop):
        self.mm_count = getattr(self, 'mm_count', 0) + 1
        return self.op(
            "T",
            lambda E, o=bank_v.ap, l=lhsT.ap, r=rhs.ap, st=start, sp=stop:
                E.matmul(o, l, r, start=st, stop=sp),
            reads=[lhsT, rhs], writes=[bank_v] if (start or stop) else [], inc=1)

    def t_transpose(self, bank_v, in_v, ident_v):
        return self.op(
            "T",
            lambda E, o=bank_v.ap, i=in_v.ap, d=ident_v.ap: E.transpose(o, i, d),
            reads=[in_v, ident_v], writes=[bank_v], inc=1)

    def s_act(self, out, in_, func, bias=0.0, scale=1.0, accum=None):
        bap = bias.ap if isinstance(bias, V) else bias
        sap = scale.ap if isinstance(scale, V) else scale
        reads = [in_] + [x for x in (bias, scale) if isinstance(x, V)]
        writes = [out] + ([accum] if accum is not None else [])
        acc = accum.ap if accum is not None else None
        return self.op(
            "S",
            lambda E, o=out.ap, i=in_.ap, ff=func, b=bap, s=sap, a=acc:
                E.activation(o, i, ff, bias=b, scale=s, accum_out=a),
            reads=reads, writes=writes, inc=1)

    def s_copy(self, out, in_):
        return self.op("S", lambda E, o=out.ap, i=in_.ap: E.copy(o, i),
                       reads=[in_], writes=[out], inc=1)

    def v_tt(self, out, a, b, alu, eng="V"):
        return self.op(eng, lambda E, o=out.ap, x=a.ap, y=b.ap, u=alu:
                       E.tensor_tensor(o, x, y, u),
                       reads=[a, b], writes=[out], inc=1)

    def v_ts(self, out, in_, s1, s2, op0, op1=ALU.bypass, eng="V"):
        s1a = s1.ap if isinstance(s1, V) else s1
        s2a = s2.ap if isinstance(s2, V) else s2
        reads = [in_] + [x for x in (s1, s2) if isinstance(x, V)]
        return self.op(eng, lambda E, o=out.ap, i=in_.ap, a=s1a, b=s2a, u0=op0, u1=op1:
                       E.tensor_scalar(o, i, a, b, u0, u1),
                       reads=reads, writes=[out], inc=1)

    def v_stt(self, out, in0, scalar, in1, op0, op1, eng="V"):
        sa = scalar.ap if isinstance(scalar, V) else scalar
        reads = [in0, in1] + ([scalar] if isinstance(scalar, V) else [])
        return self.op(eng, lambda E, o=out.ap, i=in0.ap, s=sa, j=in1.ap, u0=op0, u1=op1:
                       E.scalar_tensor_tensor(o, i, s, j, u0, u1),
                       reads=reads, writes=[out], inc=1)

    def v_scan(self, out, d0, d1, init):
        ia = init.ap if isinstance(init, V) else init
        reads = [d0, d1] + ([init] if isinstance(init, V) else [])
        return self.op("V", lambda E, o=out.ap, a=d0.ap, b=d1.ap, i=ia:
                       E.tensor_tensor_scan(o, a, b, i, ALU.mult, ALU.add),
                       reads=reads, writes=[out], inc=1)

    def v_red(self, out, in_, alu=ALU.max, negate=False, eng="V"):
        return self.op(eng, lambda E, o=out.ap, i=in_.ap, u=alu, n=negate:
                       E.tensor_reduce(o, i, AX, u, negate=n),
                       reads=[in_], writes=[out], inc=1)

    def v_copy(self, out, in_, eng="V"):
        if eng == "S":
            return self.op(eng, lambda E, o=out.ap, i=in_.ap: E.copy(o, i),
                           reads=[in_], writes=[out], inc=1)
        return self.op(eng, lambda E, o=out.ap, i=in_.ap: E.tensor_copy(o, i),
                       reads=[in_], writes=[out], inc=1)

    def v_recip(self, out, in_):
        return self.op("V", lambda E, o=out.ap, i=in_.ap: E.reciprocal(o, i),
                       reads=[in_], writes=[out], inc=1)

    def memset(self, out, val, eng="V"):
        return self.op(eng, lambda E, o=out.ap, v=val: E.memset(o, v),
                       reads=[], writes=[out], inc=1)

    def dmaq(self, dst, src):
        return self.dma("Q", dst, src)

    def dma(self, qeng, dst, src):
        reads = [src] if isinstance(src, V) else []
        writes = [dst] if isinstance(dst, V) else []
        da = dst.ap if isinstance(dst, V) else dst
        sa = src.ap if isinstance(src, V) else src
        slot = self._slot_rr[qeng] % self.NSLOT
        self._slot_rr[qeng] += 1
        key = f"{qeng}#{slot}"
        self.raw_wait(qeng, key, self.cnt[key])
        return self.op(qeng, lambda E, d=da, s=sa: E.dma_start(d, s),
                       reads=reads, writes=writes, inc=16, key=key)

    def build(self):
        nc = self.nc
        with nc.Block() as blk:
            blk.tensor(lambda E: [fn(E) for fn in self.q["T"]])
            blk.vector(lambda E: [fn(E) for fn in self.q["V"]])
            blk.scalar(lambda E: [fn(E) for fn in self.q["S"]])
            blk.sync(lambda E: [fn(E) for fn in self.q["Q"]])
            blk.gpsimd(lambda E: [fn(E) for fn in self.q["G"]])
        return nc


class Ctx:
    pass


PREFIX = {"l0a": "l0_", "l0f": "l0f_", "l1a": "l1_", "l1f": "l1f_",
          "l2a": "l2_", "l2f": "l2f_", "l3a": "l3_", "l3f": "l3f_"}


def setup_ctx(nc, P, marrs, blocks):
    C = Ctx()
    C.P = P
    C.d = {}

    def din(name):
        a = marrs[name]
        dt = BF16 if a.dtype == NPBF else F32
        C.d[name] = nc.dram_tensor(name, list(a.shape), dt, kind="ExternalInput").ap()

    names = set()
    for k in marrs:
        if k.startswith("c_"):
            names.add(k)
        else:
            for b in blocks:
                if k.startswith(PREFIX[b]):
                    names.add(k)
    for k in sorted(names):
        din(k)
    C.innames = sorted(names) + ["xT"]
    C.xT = nc.dram_tensor("xT", [D, T], F32, kind="ExternalInput").ap()
    C.yT = nc.dram_tensor("yT", [D, T], F32, kind="ExternalOutput").ap()

    C.resA = Arena(P, "resA", 8 * T)
    C.res = [C.resA.v(i * T, 128, T, F32) for i in range(8)]
    C.ar = Arena(P, "arena", ARENA_N)

    def stat_load(name, parts, free, dtype=F32):
        vv, _ = P.stat("sb_" + name, [parts, free], dtype)
        P.dma("Q", vv, C.d[name])
        return vv

    C.ident = stat_load("c_ident", 128, 128)
    C.identb = stat_load("c_identb", 128, 128, BF16)
    C.onescol = stat_load("c_onescol", 128, 1)
    C.onescolb = stat_load("c_onescolb", 128, 1, BF16)
    C.onesrow = stat_load("c_onesrow", 96, 128)
    C.onesrowb = stat_load("c_onesrowb", 96, 128, BF16)
    C.rowAB, rowt = P.stat("rowAB", [96, 6144], F32)
    C.rowABb = rowt.bitcast(BF16)
    C.eps, _ = P.stat("c_eps", [96, 1], F32)
    P.memset(C.eps, 1e-5)
    C.scr0, _ = P.stat("scr0", [128, 512], F32)
    C.scr1, _ = P.stat("scr1", [128, 512], F32)
    C.r_mean = V(C.rowAB.ap[0:1, 0:T], C.rowAB.regs)
    C.r_rstd = V(C.rowAB.ap[0:1, T:2 * T], C.rowAB.regs)
    C.r_sumsq = V(C.rowAB.ap[0:1, 2 * T:3 * T], C.rowAB.regs)
    C.stat_load = stat_load
    return C


def load_block_statics(C, blocks):
    sl = C.stat_load
    if "l1a" in blocks:
        C.l1_bq = sl("l1_bq", 128, 8)
        C.l1_bk = sl("l1_bk", 128, 8)
        C.l1_bo = sl("l1_bo", 128, 8)
    if "l1f" in blocks:
        C.l1f_wgf = sl("l1f_wg", 128, 64)
        C.l1f_b1 = sl("l1f_b1", 128, 256)
        C.l1f_b2 = sl("l1f_b2", 128, 64)
    if "l2a" in blocks:
        for nm in ("bq", "bk", "bv", "bo"):
            setattr(C, "l2_" + nm, sl("l2_" + nm, 128, 8))
    if "l2f" in blocks:
        C.l2f_wgf = sl("l2f_wg", 128, 64)
        C.l2f_b1 = sl("l2f_b1", 128, 256)
        C.l2f_b2 = sl("l2f_b2", 128, 64)
    if "l3a" in blocks:
        C.l3_inproj_b = sl("l3_inproj_b", 128, 32)
        C.l3_convw = sl("l3_convw", 128, 64)
        C.l3_convb = sl("l3_convb", 128, 16)
        C.l3_xpb_head = sl("l3_xpb_head", 32, 1)
        C.l3_xpb_u = sl("l3_xpb_u", 128, 16)
        C.l3_dtb = sl("l3_dtb", 128, 16)
        C.l3_D = sl("l3_D", 128, 16)
        C.l3_outproj_b = sl("l3_outproj_b", 128, 8)
    if "l3f" in blocks:
        for nm in ("s1", "s2", "sh1", "sh2"):
            setattr(C, "l3f_" + nm, sl("l3f_" + nm, 128, 32))
        C.l3f_s3 = sl("l3f_s3", 128, 8)


CONVERT_ORDER = [
    ("ct", "l0_wq"), ("ct", "l0_wo"),
    ("ct", "l1_wk"), ("rw", "l1_wv_rows"), ("ct", "l1_wq"), ("ct", "l1_wo"),
    ("ex", "l1f_w1"), ("ex", "l1f_w2"),
    ("ct", "l2_wq"), ("ct", "l2_wk"), ("ct", "l2_wv"), ("ct", "l2_wo"),
    ("ex", "l2f_w1"), ("ex", "l2f_w2"),
    ("ct", "l3_inproj"), ("ct", "l3_xp_u"), ("ct", "l3_outproj"),
    ("ct", "l3f_w1"), ("ct", "l3f_w2"), ("ct", "l3f_h1"), ("ct", "l3f_h2"),
    ("ct", "l3f_w3")]


def emit_convert(C, blocks):
    P = C.P
    ar = C.ar
    nc = P.nc
    C.dbf = {}
    items = []

    def scratch(name, shape):
        t = nc.dram_tensor(name + "_bf", list(shape), BF16)
        vv = V(t.ap(), Reg(_Solo(), 0, 1))
        C.dbf[name] = vv
        return vv

    for kind, name in CONVERT_ORDER:
        if name not in C.d:
            continue
        sh = C.d[name].shape
        dst = scratch(name, sh)
        if kind == "ct":
            nmt, _, ktn, _ = sh
            for mt in range(nmt):
                items.append((C.d[name][mt].rearrange("p k j -> p (k j)"),
                              dst.ap[mt].rearrange("p k j -> p (k j)"), ktn * 128))
        elif kind == "rw":
            for kt in range(sh[0] // 128):
                items.append((C.d[name][kt * 128:(kt + 1) * 128, :],
                              dst.ap[kt * 128:(kt + 1) * 128, :], sh[1]))
        else:
            for e in range(sh[0]):
                for mt in range(sh[1]):
                    items.append((C.d[name][e, mt].rearrange("p k j -> p (k j)"),
                                  dst.ap[e, mt].rearrange("p k j -> p (k j)"),
                                  sh[3] * 128))
    NI, NO = 2, 2
    inb = [C.resA.v(i * 4096, 128, 4096, F32) for i in range(NI)]
    outb = [C.resA.v(NI * 4096 + i * 2048, 128, 4096, BF16) for i in range(NO)]
    n = len(items)
    for k in range(min(NI - 1, n)):
        sap, dap, X = items[k]
        P.dma("G", V(inb[k % NI].ap[:, 0:X], inb[k % NI].regs), sap)
    for k in range(n):
        sap, dap, X = items[k]
        kin = k + NI - 1
        if kin < n:
            s2, d2, X2 = items[kin]
            P.dma("G", V(inb[kin % NI].ap[:, 0:X2], inb[kin % NI].regs), s2)
        P.v_copy(V(outb[k % NO].ap[:, 0:X], outb[k % NO].regs),
                 V(inb[k % NI].ap[:, 0:X], inb[k % NI].regs), eng="V")
        P.dma("S", dap, V(outb[k % NO].ap[:, 0:X], outb[k % NO].regs))
    C.resA.consolidate()


def emit_ln(C):
    P = C.P
    b0, b1 = P.take_bank(), P.take_bank()
    for c in range(4):
        cs = slice(c * 512, (c + 1) * 512)
        psS = P.ps(b0, 1, 512)
        psQ = P.ps(b1, 1, 512)
        for i in range(8):
            P.t_mm(psS, C.onescol, C.res[i][:, cs], start=(i == 0), stop=(i == 7))
        for i in range(8):
            sq = C.scr0 if i % 2 == 0 else C.scr1
            P.s_act(sq, C.res[i][:, cs], AF.Square)
            P.t_mm(psQ, C.onescol, sq, start=(i == 0), stop=(i == 7))
        P.s_copy(V(C.r_mean.ap[:, cs], C.r_mean.regs), psS)
        P.s_copy(V(C.r_sumsq.ap[:, cs], C.r_sumsq.regs), psQ)
    P.v_ts(C.r_mean, C.r_mean, 1.0 / D, None, ALU.mult)
    P.v_ts(C.r_sumsq, C.r_sumsq, 1.0 / D, None, ALU.mult)
    P.v_tt(C.r_rstd, C.r_mean, C.r_mean, ALU.mult)
    P.v_tt(C.r_sumsq, C.r_sumsq, C.r_rstd, ALU.subtract)
    P.s_act(C.r_sumsq, C.r_sumsq, AF.Ln, bias=C.eps[0:1, :])
    P.s_act(C.r_rstd, C.r_sumsq, AF.Exp, scale=-0.5)


def emit_xhat_chunk(C, t0, width, dst_tiles, bm0, bm1, tmp=None):
    P = C.P
    cs = slice(t0, t0 + width)
    psM = P.ps(bm0, 128, width)
    psR = P.ps(bm1, 128, width)
    P.t_mm(psM, C.onesrow[0:1, :], V(C.r_mean.ap[:, cs], C.r_mean.regs), True, True)
    P.t_mm(psR, C.onesrow[0:1, :], V(C.r_rstd.ap[:, cs], C.r_rstd.regs), True, True)
    tmp = tmp if tmp is not None else C.scr0
    for i in range(8):
        tv = V(tmp.ap[:, 0:width], tmp.regs)
        P.v_tt(tv, C.res[i][:, cs], psM, ALU.subtract)
        P.v_tt(dst_tiles[i], tv, psR, ALU.mult)


def stream_coltile(C, name, mt, kt_n, stage_f32, dst_bf16, cast_eng="S"):
    P = C.P
    P.dmaq(dst_bf16, C.dbf[name].ap[mt].rearrange("p k j -> p (k j)"))


def emit_l0a(C):
    P = C.P
    ar = C.ar
    emit_ln(C)
    KR = [ar.v(2048 + i * 1024, 128, T, BF16) for i in range(8)]
    VT = ar.v(10240, 128, 16 * 1024, BF16)
    WST = ar.v(22528, 128, 1024, F32)
    ropeC = ar.v(23552, 128, T, BF16)
    ropeS = ar.v(24576, 128, T, BF16)
    stair = ar.v(25600, 128, 896, BF16)
    swapb = ar.v(26048, 128, 128, BF16)
    P.dma("Q", ropeC, C.d["c_ropeC"])
    P.dma("Q", ropeS, C.d["c_ropeS"])
    P.dma("Q", stair, C.d["c_stair"])
    P.dma("Q", swapb, C.d["c_swap"])

    rsc1 = ar.v(26112, 128, 256, F32)
    rsc2 = ar.v(26368, 128, 256, F32)

    def rope_tile(src_bf, psb, dst, tslice):
        w = tslice.stop - tslice.start
        psw = P.ps(psb, 128, w)
        P.t_mm(psw, swapb, src_bf, True, True)
        P.v_tt(V(rsc1.ap[:, 0:w], rsc1.regs), src_bf,
               V(ropeC.ap[:, tslice], ropeC.regs), ALU.mult)
        P.v_tt(V(rsc2.ap[:, 0:w], rsc2.regs), psw,
               V(ropeS.ap[:, tslice], ropeS.regs), ALU.mult)
        P.v_tt(dst, V(rsc1.ap[:, 0:w], rsc1.regs),
               V(rsc2.ap[:, 0:w], rsc2.regs), ALU.add)

    WKB = [ar.v(18560 + m * 512, 128, 1024, BF16) for m in range(8)]
    for m in range(8):
        P.dma("Q", WST, C.d["l0_wk"][m].rearrange("p k j -> p (k j)"))
        P.v_copy(WKB[m], WST, eng="S" if m % 2 == 0 else "V")
    qtmp = ar.v(26624, 128, 512, BF16)
    for c in range(4):
        cs = slice(c * 512, (c + 1) * 512)
        XHC = [ar.v(i * 256, 128, 512, BF16) for i in range(8)]
        emit_xhat_chunk(C, c * 512, 512, XHC, P.take_bank(), P.take_bank())
        for m in range(8):
            psK = P.ps(P.take_bank(), 128, 512)
            for kt in range(8):
                P.t_mm(psK, V(WKB[m].ap[:, kt * 128:(kt + 1) * 128], WKB[m].regs),
                       XHC[kt], start=(kt == 0), stop=(kt == 7))
            P.s_copy(qtmp, psK)
            for half in range(2):
                hs = slice(c * 512 + half * 256, c * 512 + half * 256 + 256)
                rope_tile(V(qtmp.ap[:, half * 256:half * 256 + 256], qtmp.regs),
                          P.take_bank(),
                          V(KR[m].ap[:, hs], KR[m].regs), hs)

    WVB = [ar.v(18560 + kt * 512, 128, 1024, BF16) for kt in range(8)]
    for kt in range(8):
        P.dma("Q", WST, C.d["l0_wv_rows"][kt * 128:(kt + 1) * 128, :])
        P.v_copy(WVB[kt], WST, eng="S" if kt % 2 == 0 else "V")
    for c in range(4):
        XHC = [ar.v(i * 256, 128, 512, BF16) for i in range(8)]
        emit_xhat_chunk(C, c * 512, 512, XHC, P.take_bank(), P.take_bank())
        for tloc in range(4):
            tt = c * 4 + tloc
            for ec in range(2):
                psV = P.ps(P.take_bank(), 128, 512)
                for kt in range(8):
                    P.t_mm(psV,
                           V(XHC[kt].ap[:, tloc * 128:(tloc + 1) * 128], XHC[kt].regs),
                           V(WVB[kt].ap[:, ec * 512:(ec + 1) * 512], WVB[kt].regs),
                           start=(kt == 0), stop=(kt == 7))
                P.s_copy(V(VT.ap[:, tt * 1024 + ec * 512:tt * 1024 + ec * 512 + 512],
                           VT.regs), psV)

    otmp = ar.v(26880, 128, 256, F32)
    for c in range(8):
        cw = 256
        cs = slice(c * cw, (c + 1) * cw)
        XHC = [ar.v(i * 128, 128, cw, BF16) for i in range(8)]
        QR = [ar.v(1024 + i * 128, 128, cw, BF16) for i in range(8)]
        ATT = [ar.v(18432 + tt * 128, 128, cw, BF16) for tt in range(16)]
        OUTC = [ar.v(20480 + i * 128, 128, cw, BF16) for i in range(8)]
        WQB = [ar.v(21504 + b * 512, 128, 1024, BF16) for b in range(2)]
        WOB = [ar.v(22528 + b * 512, 128, 1024, BF16) for b in range(2)]
        emit_xhat_chunk(C, c * cw, cw, XHC, P.take_bank(), P.take_bank())
        for m in range(8):
            wb = WQB[m % 2]
            stream_coltile(C, "l0_wq", m, 8, WST, wb)
            psQ = P.ps(P.take_bank(), 128, cw)
            for kt in range(8):
                P.t_mm(psQ, V(wb.ap[:, kt * 128:(kt + 1) * 128], wb.regs),
                       XHC[kt], start=(kt == 0), stop=(kt == 7))
            P.s_copy(V(qtmp.ap[:, 0:cw], qtmp.regs), psQ)
            rope_tile(V(qtmp.ap[:, 0:cw], qtmp.regs), P.take_bank(), QR[m], cs)
        ndiag = 2 * (c + 1)
        for h in range(16):
            mt = h // 2
            rlo = (h % 2) * 64
            bS1, bS2 = P.take_bank(), P.take_bank()
            bO, bR = P.take_bank(), P.take_bank()
            psO = V(P.psum[bO][rlo:rlo + 64, 0:cw], P.psreg[bO])
            psR = V(P.psum[bR][rlo:rlo + 1, 0:cw], P.psreg[bR])
            for tt in range(ndiag):
                psS = P.ps(bS1 if tt % 2 == 0 else bS2, 128, cw)
                P.t_mm(psS,
                       V(KR[mt].ap[rlo:rlo + 64, tt * 128:(tt + 1) * 128],
                         KR[mt].regs),
                       V(QR[mt].ap[rlo:rlo + 64, :], QR[mt].regs), True, True)
                P.s_act(ATT[tt], psS, AF.Exp, scale=0.125)
                if tt >= 2 * c:
                    delta = tt * 128 - c * cw
                    P.v_tt(ATT[tt], ATT[tt],
                           V(stair.ap[:, 384 - delta:384 - delta + cw], stair.regs),
                           ALU.mult)
                P.t_mm(psO,
                       V(VT.ap[:, tt * 1024 + h * 64:tt * 1024 + h * 64 + 64],
                         VT.regs),
                       ATT[tt], start=(tt == 0), stop=(tt == ndiag - 1))
                P.t_mm(psR, C.onescolb, ATT[tt],
                       start=(tt == 0), stop=(tt == ndiag - 1))
            crow = V(C.rowAB.ap[rlo:rlo + 1, 2 * T:2 * T + cw], C.rowAB.regs)
            P.v_recip(crow, psR)
            bC = P.take_bank()
            psC = V(P.psum[bC][rlo:rlo + 64, 0:cw], P.psreg[bC])
            P.t_mm(psC, V(C.onesrow.ap[rlo:rlo + 1, 0:64], C.onesrow.regs),
                   crow, True, True)
            ot = V(otmp.ap[rlo:rlo + 64, :], otmp.regs)
            P.s_copy(ot, psO)
            P.v_tt(V(OUTC[mt].ap[rlo:rlo + 64, :], OUTC[mt].regs), ot, psC, ALU.mult)
        for m in range(8):
            wb = WOB[m % 2]
            stream_coltile(C, "l0_wo", m, 8, WST, wb)
            psY = P.ps(P.take_bank(), 128, cw)
            for kt in range(8):
                P.t_mm(psY, V(wb.ap[:, kt * 128:(kt + 1) * 128], wb.regs),
                       OUTC[kt], start=(kt == 0), stop=(kt == 7))
            P.v_tt(C.res[m][:, cs], C.res[m][:, cs], psY, ALU.add)
    ar.consolidate()


def emit_l0f(C):
    P = C.P
    ar = C.ar
    emit_ln(C)
    XH = [ar.v(i * 1024, 128, T, BF16) for i in range(8)]
    for c in range(4):
        emit_xhat_chunk(C, c * 512, 512,
                        [V(XH[i].ap[:, c * 512:(c + 1) * 512], XH[i].regs)
                         for i in range(8)],
                        P.take_bank(), P.take_bank())
    LG = ar.v(8192, 8, T, F32)
    CW = ar.v(10240, 8, T, F32)
    AFL = ar.v(12288, 128, 1024, BF16)
    BFL = ar.v(12800, 128, 1024, BF16)
    WGB = ar.v(13312, 128, 64, BF16)
    erpat = ar.v(13344, 8, 128, F32)
    P.dma("Q", erpat, C.d["c_erpat"])
    stg = ar.v(13568, 128, 1024, F32)
    P.dma("Q", stg, C.d["l0f_a"])
    P.v_copy(AFL, stg)
    P.dma("Q", stg, C.d["l0f_b"])
    P.v_copy(BFL, stg)
    wgstg = ar.v(13472, 128, 64, F32)
    P.dma("Q", wgstg, C.d["l0f_wg"])
    P.v_copy(WGB, wgstg)
    bG = P.take_bank()
    for c in range(4):
        cs = slice(c * 512, (c + 1) * 512)
        psG = P.ps(bG, 8, 512)
        for kt in range(8):
            P.t_mm(psG, V(WGB.ap[:, kt * 8:(kt + 1) * 8], WGB.regs),
                   V(XH[kt].ap[:, cs], XH[kt].regs), start=(kt == 0), stop=(kt == 7))
        P.s_copy(V(LG.ap[:, cs], LG.regs), psG)
    tokbase = 14592
    LGt = ar.v(tokbase, 128, 8, F32)
    m1 = ar.v(tokbase + 8, 128, 1, F32)
    m1n = ar.v(tokbase + 9, 128, 1, F32)
    m2 = ar.v(tokbase + 10, 128, 1, F32)
    qv = ar.v(tokbase + 11, 128, 1, F32)
    w1 = ar.v(tokbase + 12, 128, 1, F32)
    w2 = ar.v(tokbase + 13, 128, 1, F32)
    msk1 = ar.v(tokbase + 16, 128, 8, F32)
    msk2 = ar.v(tokbase + 24, 128, 8, F32)
    g2 = ar.v(tokbase + 32, 128, 8, F32)
    cwt = ar.v(tokbase + 40, 128, 8, F32)
    bT1, bT2 = P.take_bank(), P.take_bank()
    for tt in range(16):
        psL = P.ps(bT1, 128, 8)
        P.t_transpose(psL, V(LG.ap[:, tt * 128:(tt + 1) * 128], LG.regs),
                      V(C.ident.ap[0:8, 0:8], C.ident.regs))
        P.s_copy(LGt, psL)
        P.v_red(m1, LGt, ALU.max)
        P.v_red(m1n, LGt, ALU.max, negate=True)
        P.v_ts(msk1, LGt, m1, None, ALU.is_equal)
        P.v_stt(g2, msk1, -1e30, LGt, ALU.mult, ALU.add)
        P.v_red(m2, g2, ALU.max)
        P.v_ts(msk2, g2, m2, None, ALU.is_equal)
        P.s_act(qv, m2, AF.Exp, bias=m1n)
        P.v_ts(qv, qv, 1.0, None, ALU.add)
        P.v_recip(w1, qv)
        P.v_ts(qv, qv, -1.0, None, ALU.add)
        P.v_tt(w2, qv, w1, ALU.mult)
        P.v_ts(cwt, msk1, w1, None, ALU.mult)
        P.v_stt(cwt, msk2, w2, cwt, ALU.mult, ALU.add)
        psB = P.ps(bT2, 8, 128)
        P.t_transpose(psB, cwt, C.ident)
        P.s_copy(V(CW.ap[:, tt * 128:(tt + 1) * 128], CW.regs), psB)
    HW = ar.v(14336, 128, 512, BF16)
    for c in range(4):
        cs = slice(c * 512, (c + 1) * 512)
        psH = P.ps(P.take_bank(), 128, 512)
        for kt in range(8):
            P.t_mm(psH, V(AFL.ap[:, kt * 128:(kt + 1) * 128], AFL.regs),
                   V(XH[kt].ap[:, cs], XH[kt].regs), start=(kt == 0), stop=(kt == 7))
        P.s_copy(C.scr0, psH)
        psBC = P.ps(P.take_bank(), 128, 512)
        P.t_mm(psBC, erpat, V(CW.ap[:, cs], CW.regs), True, True)
        P.v_tt(HW, C.scr0, psBC, ALU.mult)
        for mm in range(8):
            psO = P.ps(P.take_bank(), 128, 512)
            P.t_mm(psO, V(BFL.ap[:, mm * 128:(mm + 1) * 128], BFL.regs), HW,
                   True, True)
            P.v_tt(C.res[mm][:, cs], C.res[mm][:, cs], psO, ALU.add)
    ar.consolidate()


def emit_l1a(C):
    P = C.P
    ar = C.ar
    emit_ln(C)
    KT = [ar.v(2048 + i * 1024, 128, T, BF16) for i in range(8)]
    VT = ar.v(10240, 128, 16 * 1024, BF16)
    WST = ar.v(25088, 128, 1024, F32)
    sc2 = ar.v(26112, 128, 256, F32)
    sc3 = ar.v(26368, 128, 256, F32)
    bvrow = ar.v(26624, 1, D, BF16)
    bvstg = ar.v(25088, 1, D, F32)
    P.dma("Q", bvstg, C.d["l1_bv_row"])
    P.v_copy(bvrow, bvstg)

    WKB = [ar.v(20480 + m * 512, 128, 1024, BF16) for m in range(8)]
    for m in range(8):
        stream_coltile(C, "l1_wk", m, 8, WST, WKB[m])
    for c in range(4):
        XHC = [ar.v(i * 256, 128, 512, BF16) for i in range(8)]
        emit_xhat_chunk(C, c * 512, 512, XHC, P.take_bank(), P.take_bank())
        for m in range(8):
            psK = P.ps(P.take_bank(), 128, 512)
            for kt in range(8):
                P.t_mm(psK, V(WKB[m].ap[:, kt * 128:(kt + 1) * 128], WKB[m].regs),
                       XHC[kt], start=(kt == 0), stop=(kt == 7))
            P.v_ts(V(KT[m].ap[:, c * 512:(c + 1) * 512], KT[m].regs), psK,
                   C.l1_bk[:, m:m + 1], None, ALU.add)
    if L1A_PHASES < 2:
        ar.consolidate()
        return
    WVB = [ar.v(20480 + kt * 512, 128, 1024, BF16) for kt in range(8)]
    for kt in range(8):
        P.dmaq(WVB[kt], C.dbf["l1_wv_rows"].ap[kt * 128:(kt + 1) * 128, :])
    for c in range(4):
        XHC = [ar.v(i * 256, 128, 512, BF16) for i in range(8)]
        emit_xhat_chunk(C, c * 512, 512, XHC, P.take_bank(), P.take_bank())
        for tloc in range(4):
            tt = c * 4 + tloc
            for ec in range(2):
                psV = P.ps(P.take_bank(), 128, 512)
                for kt in range(8):
                    P.t_mm(psV,
                           V(XHC[kt].ap[:, tloc * 128:(tloc + 1) * 128], XHC[kt].regs),
                           V(WVB[kt].ap[:, ec * 512:(ec + 1) * 512], WVB[kt].regs),
                           start=(kt == 0), stop=False)
                P.t_mm(psV, C.onesrowb[0:1, :],
                       V(bvrow.ap[:, ec * 512:(ec + 1) * 512], bvrow.regs),
                       start=False, stop=True)
                P.s_copy(V(VT.ap[:, tt * 1024 + ec * 512:tt * 1024 + ec * 512 + 512],
                           VT.regs), psV)
    if L1A_PHASES < 3:
        ar.consolidate()
        return
    WQB = [ar.v(20480 + m * 512, 128, 1024, BF16) for m in range(8)]
    for m in range(8):
        stream_coltile(C, "l1_wq", m, 8, WST, WQB[m])
    WOB = [ar.v(24576 + b * 512, 128, 1024, BF16) for b in range(2)]
    for c in range(L1A_CHUNKS):
        cw = 256
        cs = slice(c * cw, (c + 1) * cw)
        XHC = [ar.v(i * 128, 128, cw, BF16) for i in range(8)]
        QC = [ar.v(1024 + i * 128, 128, cw, BF16) for i in range(8)]
        ATT = [ar.v(18432 + tt * 128, 128, cw, BF16) for tt in range(16)]
        OUTC = [ar.v(i * 128, 128, cw, BF16) for i in range(8)]
        emit_xhat_chunk(C, c * cw, cw, XHC, P.take_bank(), P.take_bank())
        for m in range(8):
            psQ = P.ps(P.take_bank(), 128, cw)
            for kt in range(8):
                P.t_mm(psQ, V(WQB[m].ap[:, kt * 128:(kt + 1) * 128], WQB[m].regs),
                       XHC[kt], start=(kt == 0), stop=(kt == 7))
            P.v_ts(QC[m], psQ, C.l1_bq[:, m:m + 1], None, ALU.add)
        if L1A_PHASES < 4:
            continue
        bS1, bS2 = P.take_bank(), P.take_bank()
        for tt in range(16):
            psS = P.ps(bS1 if tt % 2 == 0 else bS2, 128, cw)
            for kt in range(8):
                P.t_mm(psS, V(KT[kt].ap[:, tt * 128:(tt + 1) * 128], KT[kt].regs),
                       QC[kt], start=(kt == 0), stop=(kt == 7))
            if L1A_THETA == 0:
                P.s_copy(ATT[tt], psS)
                continue
            P.s_act(C.scr0[:, 0:cw], psS, AF.Tanh, scale=0.5)
            P.s_act(C.scr1[:, 0:cw], psS, AF.Tanh)
            if L1A_THETA == 1:
                P.s_copy(ATT[tt], psS)
                continue
            P.v_ts(sc2, psS, 0.0, 0.1, ALU.max, ALU.mult)
            if L1A_THETA == 21:
                P.s_copy(ATT[tt], psS)
                continue
            P.v_stt(sc3, C.scr0[:, 0:cw], 0.1, sc2, ALU.mult, ALU.add)
            if L1A_THETA == 22:
                P.s_copy(ATT[tt], psS)
                continue
            P.v_stt(sc2, C.scr1[:, 0:cw], 0.15, sc3, ALU.mult, ALU.add)
            if L1A_THETA == 2:
                P.s_copy(ATT[tt], psS)
                continue
            P.s_act(ATT[tt], sc2, AF.Exp)
        if L1A_PHASES < 5:
            continue
        bR = P.take_bank()
        psR = V(P.psum[bR][0:1, 0:cw], P.psreg[bR])
        for tt in range(16):
            P.t_mm(psR, C.onescolb, ATT[tt], start=(tt == 0), stop=(tt == 15))
        crow = V(C.rowAB.ap[0:1, 2 * T:2 * T + cw], C.rowAB.regs)
        P.v_recip(crow, psR)
        psC = P.ps(7, 128, cw)
        P.t_mm(psC, C.onesrow[0:1, :], crow, True, True)
        if L1A_PHASES < 6:
            continue
        for dh in range(2):
            banks = [P.take_bank() for _ in range(4)]
            for mi in range(4):
                m = dh * 4 + mi
                psO = P.ps(banks[mi], 128, cw)
                for tt in range(16):
                    P.t_mm(psO,
                           V(VT.ap[:, tt * 1024 + m * 128:tt * 1024 + (m + 1) * 128],
                             VT.regs),
                           ATT[tt], start=(tt == 0), stop=(tt == 15))
            for mi in range(4):
                m = dh * 4 + mi
                psO = P.ps(banks[mi], 128, cw)
                P.s_copy(C.scr0[:, 0:cw], psO)
                P.v_tt(OUTC[m], C.scr0[:, 0:cw], psC, ALU.mult)
        if L1A_PHASES < 7:
            continue
        for m in range(8):
            wb = WOB[m % 2]
            stream_coltile(C, "l1_wo", m, 8, WST, wb)
            psY = P.ps(P.take_bank(), 128, cw)
            for kt in range(8):
                P.t_mm(psY, V(wb.ap[:, kt * 128:(kt + 1) * 128], wb.regs),
                       OUTC[kt], start=(kt == 0), stop=(kt == 7))
            P.v_stt(C.res[m][:, cs], psY, C.l1_bo[:, m:m + 1], C.res[m][:, cs],
                    ALU.add, ALU.add)
    ar.consolidate()


def _emit_router(C, LG, tokbase, topk):
    P = C.P
    ar = C.ar
    LGt = ar.v(tokbase, 128, 8, F32)
    m1 = ar.v(tokbase + 8, 128, 1, F32)
    m1n = ar.v(tokbase + 9, 128, 1, F32)
    sm = ar.v(tokbase + 10, 128, 1, F32)
    w1 = ar.v(tokbase + 11, 128, 1, F32)
    m2 = ar.v(tokbase + 12, 128, 1, F32)
    ex = ar.v(tokbase + 16, 128, 8, F32)
    msk1 = ar.v(tokbase + 24, 128, 8, F32)
    msk2 = ar.v(tokbase + 32, 128, 8, F32)
    g2 = ar.v(tokbase + 40, 128, 8, F32)
    cwt = ar.v(tokbase + 48, 128, 8, F32)
    bT1, bT2 = P.take_bank(), P.take_bank()
    for tt in range(16):
        psL = P.ps(bT1, 128, 8)
        P.t_transpose(psL, V(LG.ap[:, tt * 128:(tt + 1) * 128], LG.regs),
                      V(C.ident.ap[0:8, 0:8], C.ident.regs))
        P.s_copy(LGt, psL)
        P.v_red(m1, LGt, ALU.max)
        P.v_red(m1n, LGt, ALU.max, negate=True)
        P.s_act(ex, LGt, AF.Exp, bias=m1n)
        P.v_red(sm, ex, ALU.add)
        P.v_recip(w1, sm)
        P.v_ts(msk1, LGt, m1, None, ALU.is_equal)
        if topk == 1:
            P.v_ts(cwt, msk1, w1, None, ALU.mult)
        else:
            P.v_ts(ex, ex, w1, None, ALU.mult)
            P.v_stt(g2, msk1, -1e30, LGt, ALU.mult, ALU.add)
            P.v_red(m2, g2, ALU.max)
            P.v_ts(msk2, g2, m2, None, ALU.is_equal)
            P.v_tt(cwt, msk1, ex, ALU.mult)
            P.v_tt(msk2, msk2, ex, ALU.mult)
            P.v_tt(cwt, cwt, msk2, ALU.add)
        psB = P.ps(bT2, 8, 128)
        P.t_transpose(psB, cwt, C.ident)
        P.s_copy(V(C.LGb.ap[:, tt * 128:(tt + 1) * 128], C.LGb.regs), psB)


def _emit_moe(C, wgname, w1name, w2name, b1col, b2col, topk):
    P = C.P
    ar = C.ar
    emit_ln(C)
    XH = [ar.v(i * 1024, 128, T, BF16) for i in range(8)]
    for c in range(4):
        emit_xhat_chunk(C, c * 512, 512,
                        [V(XH[i].ap[:, c * 512:(c + 1) * 512], XH[i].regs)
                         for i in range(8)],
                        P.take_bank(), P.take_bank())
    LG = ar.v(24576, 8, T, F32)
    C.LGb = ar.v(24576, 8, T, BF16)
    WGB = ar.v(26688, 128, 64, BF16)
    wgstg = ar.v(26624, 128, 64, F32)
    P.dma("Q", wgstg, C.d[wgname])
    P.v_copy(WGB, wgstg)
    bG = P.take_bank()
    for c in range(4):
        cs = slice(c * 512, (c + 1) * 512)
        psG = P.ps(bG, 8, 512)
        for kt in range(8):
            P.t_mm(psG, V(WGB.ap[:, kt * 8:(kt + 1) * 8], WGB.regs),
                   V(XH[kt].ap[:, cs], XH[kt].regs), start=(kt == 0), stop=(kt == 7))
        P.s_copy(V(LG.ap[:, cs], LG.regs), psG)
    _emit_router(C, LG, 26752, topk)
    MErowB = V(C.rowABb[0:1, 4 * T:4 * T + 512], C.rowAB.regs)
    WBH = [ar.v(25600, 128, 2048, BF16), ar.v(26624, 128, 2048, BF16)]
    H = [ar.v(8192 + i * 1024, 128, T, BF16) for i in range(16)]
    save_rr = P.rr_banks
    P.rr_banks = [0, 1, 2, 3]
    for e in range(8):
        for ch in range(4):
            chs = slice(ch * 512, (ch + 1) * 512)
            P.dma("G", MErowB, V(C.LGb.ap[e:e + 1, chs], C.LGb.regs))
            psM = P.ps(4 + ch, 128, 512)
            P.t_mm(psM, C.onesrowb[0:1, :], MErowB, True, True)
        for hf in range(2):
            for mi in range(16):
                mt = hf * 16 + mi
                pr = mi // 2
                wb = WBH[pr % 2]
                if mi % 2 == 0:
                    P.dmaq(V(wb.ap.rearrange("p (a x) -> p a x", a=2), wb.regs),
                           C.dbf[w1name].ap[e, mt:mt + 2]
                           .rearrange("a p k j -> p a (k j)"))
                wsl = (mi % 2) * 1024
                for ch in range(4):
                    chs = slice(ch * 512, (ch + 1) * 512)
                    psH = P.ps(P.take_bank(), 128, 512)
                    for kt in range(8):
                        P.t_mm(psH, V(wb.ap[:, wsl + kt * 128:wsl + (kt + 1) * 128],
                                      wb.regs),
                               V(XH[kt].ap[:, chs], XH[kt].regs),
                               start=(kt == 0), stop=(kt == 7))
                    P.s_act(V(H[mi].ap[:, chs], H[mi].regs), psH, AF.Gelu,
                            bias=b1col[:, e * 32 + mt:e * 32 + mt + 1])
            for dm in range(8):
                wb = WBH[dm % 2]
                P.dmaq(wb, C.dbf[w2name].ap[e, dm][:, hf * 16:(hf + 1) * 16, :]
                       .rearrange("p k j -> p (k j)"))
                banks = [0, 1, 2, 3]
                for ch in range(4):
                    chs = slice(ch * 512, (ch + 1) * 512)
                    psY = P.ps(banks[ch], 128, 512)
                    for kt in range(16):
                        P.t_mm(psY, V(wb.ap[:, kt * 128:(kt + 1) * 128], wb.regs),
                               V(H[kt].ap[:, chs], H[kt].regs),
                               start=(kt == 0), stop=(kt == 15))
                for ch in range(4):
                    chs = slice(ch * 512, (ch + 1) * 512)
                    psY = P.ps(banks[ch], 128, 512)
                    if hf == 1:
                        P.v_ts(C.scr0, psY, b2col[:, e * 8 + dm:e * 8 + dm + 1],
                               None, ALU.add)
                    else:
                        P.s_copy(C.scr0, psY)
                    psM = P.ps(4 + ch, 128, 512)
                    P.v_tt(C.scr1, C.scr0, psM, ALU.mult)
                    P.v_tt(C.res[dm][:, chs], C.res[dm][:, chs], C.scr1, ALU.add)
    P.rr_banks = save_rr
    ar.consolidate()


def emit_l1f(C):
    _emit_moe(C, "l1f_wg", "l1f_w1", "l1f_w2", C.l1f_b1, C.l1f_b2, 1)


def emit_l2f(C):
    _emit_moe(C, "l2f_wg", "l2f_w1", "l2f_w2", C.l2f_b1, C.l2f_b2, 2)


def emit_l2a(C):
    P = C.P
    ar = C.ar
    emit_ln(C)
    QF = [ar.v(2048 + i * 256, 128, 512, BF16) for i in range(8)]
    KF = [ar.v(4096 + i * 256, 128, 512, BF16) for i in range(8)]
    VF = [ar.v(6144 + i * 512, 128, 512, F32) for i in range(8)]
    QKV = [ar.v(10240 + i * 256, 128, 512, BF16) for i in range(8)]
    WSTG = ar.v(12288, 128, 1024, F32)
    WB2 = [ar.v(13312 + b * 512, 128, 1024, BF16) for b in range(2)]
    hsum = ar.v(14336, 128, 128, BF16)
    P.dma("Q", hsum, C.d["c_hsum"])
    for c in range(4):
        cs = slice(c * 512, (c + 1) * 512)
        XHC = [ar.v(i * 256, 128, 512, BF16) for i in range(8)]
        emit_xhat_chunk(C, c * 512, 512, XHC, P.take_bank(), P.take_bank())
        for m in range(8):
            wb = WB2[m % 2]
            stream_coltile(C, "l2_wq", m, 8, WSTG, wb)
            psQ = P.ps(P.take_bank(), 128, 512)
            for kt in range(8):
                P.t_mm(psQ, V(wb.ap[:, kt * 128:(kt + 1) * 128], wb.regs),
                       XHC[kt], start=(kt == 0), stop=(kt == 7))
            P.s_act(C.scr0, psQ, AF.Square, bias=C.l2_bq[:, m:m + 1])
            P.s_act(QF[m], C.scr0, AF.Exp, scale=-0.5)
            wb = WB2[m % 2]
            stream_coltile(C, "l2_wk", m, 8, WSTG, wb)
            psK = P.ps(P.take_bank(), 128, 512)
            for kt in range(8):
                P.t_mm(psK, V(wb.ap[:, kt * 128:(kt + 1) * 128], wb.regs),
                       XHC[kt], start=(kt == 0), stop=(kt == 7))
            P.s_act(C.scr0, psK, AF.Square, bias=C.l2_bk[:, m:m + 1])
            P.s_act(KF[m], C.scr0, AF.Exp, scale=-0.5)
            wb = WB2[m % 2]
            stream_coltile(C, "l2_wv", m, 8, WSTG, wb)
            psV = P.ps(P.take_bank(), 128, 512)
            for kt in range(8):
                P.t_mm(psV, V(wb.ap[:, kt * 128:(kt + 1) * 128], wb.regs),
                       XHC[kt], start=(kt == 0), stop=(kt == 7))
            P.v_ts(VF[m], psV, C.l2_bv[:, m:m + 1], None, ALU.add)
        for m in range(8):
            P.v_tt(QF[m], QF[m], KF[m], ALU.mult)
            psCB = P.ps(P.take_bank(), 128, 512)
            P.t_mm(psCB, hsum, QF[m], True, True)
            P.v_tt(QKV[m], VF[m], psCB, ALU.mult)
        for m in range(8):
            wb = WB2[m % 2]
            stream_coltile(C, "l2_wo", m, 8, WSTG, wb)
            psY = P.ps(P.take_bank(), 128, 512)
            for kt in range(8):
                P.t_mm(psY, V(wb.ap[:, kt * 128:(kt + 1) * 128], wb.regs),
                       QKV[kt], start=(kt == 0), stop=(kt == 7))
            P.v_stt(C.res[m][:, cs], psY, C.l2_bo[:, m:m + 1], C.res[m][:, cs],
                    ALU.add, ALU.add)
    ar.consolidate()


def emit_l3a(C):
    P = C.P
    ar = C.ar
    emit_ln(C)
    CK = 256
    halo = ar.v(0, 128, 48, F32)
    xcbuf = ar.v(48, 128, 260, F32)
    xc_s = [ar.v(308 + i * 128, 128, CK, BF16) for i in range(16)]
    z_s = [ar.v(2356 + i * 128, 128, CK, BF16) for i in range(16)]
    u = [ar.v(4404 + i * 128, 128, CK, BF16) for i in range(16)]
    delta = [ar.v(6452 + i * 256, 128, CK, F32) for i in range(16)]
    w = [ar.v(10548 + i * 256, 128, CK, F32) for i in range(16)]
    y = [ar.v(14644 + i * 256, 128, CK, F32) for i in range(16)]
    yfin = [ar.v(18740 + i * 128, 128, CK, BF16) for i in range(16)]
    hstate = ar.v(20788, 128, 256, F32)
    dsB = ar.v(21044, 32, CK, F32)
    dsb16 = ar.v(21300, 16, CK, BF16)
    Brow = ar.v(21428, 1, CK, F32)
    dtW = ar.v(21684, 16, T, BF16)
    dAscr = ar.v(22708, 128, CK, F32)
    dBscr = ar.v(22964, 128, CK, F32)
    hscr = ar.v(23220, 128, CK, F32)
    XHC = [ar.v(23476 + i * 128, 128, CK, BF16) for i in range(8)]
    WSTG = ar.v(24500, 128, 512, F32)
    XPH = ar.v(25012, 128, 512, BF16)
    BUF2 = [ar.v(25268, 128, 2048, BF16),
            ar.v(26292, 128, 2048, BF16)]
    dtstg = ar.v(6452, 16, T, F32)
    P.dma("Q", dtstg, C.d["l3_dt"])
    P.v_copy(dtW, dtstg)
    P.memset(halo, 0.0)
    P.memset(hstate, 0.0)
    for ck in range(L3A_CKS):
        cs = slice(ck * CK, (ck + 1) * CK)
        emit_xhat_chunk(C, ck * CK, CK, XHC, P.take_bank(), P.take_bank())
        for mt in range(32):
            pr = mt // 2
            wb = BUF2[pr % 2]
            if mt % 2 == 0:
                P.dmaq(V(wb.ap.rearrange("p (a x) -> p a x", a=2), wb.regs),
                       C.dbf["l3_inproj"].ap[mt:mt + 2]
                       .rearrange("a p k j -> p a (k j)"))
            wsl = (mt % 2) * 1024
            psX = P.ps(P.take_bank(), 128, CK)
            for kt in range(8):
                P.t_mm(psX, V(wb.ap[:, wsl + kt * 128:wsl + (kt + 1) * 128], wb.regs),
                       XHC[kt], start=(kt == 0), stop=(kt == 7))
            if mt < 16:
                P.v_copy(V(xcbuf.ap[:, 0:3], xcbuf.regs),
                         V(halo.ap[:, mt * 3:mt * 3 + 3], halo.regs))
                P.v_ts(V(xcbuf.ap[:, 3:259], xcbuf.regs), psX,
                       C.l3_inproj_b[:, mt:mt + 1], None, ALU.add)
                P.v_copy(V(halo.ap[:, mt * 3:mt * 3 + 3], halo.regs),
                         V(xcbuf.ap[:, 256:259], xcbuf.regs))
                P.v_ts(dAscr, V(xcbuf.ap[:, 0:256], xcbuf.regs),
                       C.l3_convw[:, mt * 4:mt * 4 + 1], None, ALU.mult)
                for kk in range(1, 4):
                    P.v_stt(dAscr, V(xcbuf.ap[:, kk:kk + 256], xcbuf.regs),
                            C.l3_convw[:, mt * 4 + kk:mt * 4 + kk + 1],
                            dAscr, ALU.mult, ALU.add)
                P.s_act(xc_s[mt], dAscr, AF.Silu, bias=C.l3_convb[:, mt:mt + 1])
            else:
                P.s_act(z_s[mt - 16], psX, AF.Silu,
                        bias=C.l3_inproj_b[:, mt:mt + 1])
        P.dma("Q", WSTG, C.d["l3_xp_head"].rearrange("p k j -> p (k j)"))
        P.v_copy(XPH, WSTG, eng="S")
        psH = P.ps(P.take_bank(), 32, CK)
        for kt in range(16):
            P.t_mm(psH, V(XPH.ap[:, kt * 32:(kt + 1) * 32], XPH.regs),
                   xc_s[kt], start=(kt == 0), stop=(kt == 15))
        P.v_ts(dsB, psH, C.l3_xpb_head[0:32, :], None, ALU.add)
        P.v_copy(dsb16, V(dsB.ap[0:16, :], dsB.regs))
        for ut in range(16):
            src_w = C.dbf["l3_xp_u"].ap[ut].rearrange("p k j -> p (k j)")
            wb = BUF2[ut % 2]
            P.dmaq(wb, src_w)
            psU = P.ps(P.take_bank(), 128, CK)
            for kt in range(16):
                P.t_mm(psU, V(wb.ap[:, kt * 128:(kt + 1) * 128], wb.regs),
                       xc_s[kt], start=(kt == 0), stop=(kt == 15))
            P.v_ts(u[ut], psU, C.l3_xpb_u[:, ut:ut + 1], None, ALU.add)
        for ct in range(16):
            psD = P.ps(P.take_bank(), 128, CK)
            P.t_mm(psD, V(dtW.ap[:, ct * 128:(ct + 1) * 128], dtW.regs),
                   dsb16, True, True)
            P.s_act(dAscr, psD, AF.Abs, bias=C.l3_dtb[:, ct:ct + 1])
            P.s_act(dBscr, dAscr, AF.Exp, scale=-1.0)
            P.s_act(dBscr, dBscr, AF.Ln, bias=C.onescol)
            P.s_act(hscr, psD, AF.Relu, bias=C.l3_dtb[:, ct:ct + 1])
            P.v_tt(delta[ct], dBscr, hscr, ALU.add)
            P.v_tt(w[ct], delta[ct], u[ct], ALU.mult)
        for s in range(L3A_SMAX):
            P.dma("G", Brow, V(dsB.ap[16 + s:17 + s, :], dsB.regs))
            psB = P.ps(7, 128, CK)
            P.t_mm(psB, C.onesrow[0:1, :], Brow, True, True)
            for ct in range(16):
                P.s_act(dAscr, delta[ct], AF.Exp, scale=-(s + 1.0))
                P.v_tt(dBscr, w[ct], psB, ALU.mult)
                hcol = V(hstate.ap[:, ct * 16 + s:ct * 16 + s + 1], hstate.regs)
                P.v_scan(hscr, dAscr, dBscr, hcol)
                if s == 0:
                    P.v_copy(y[ct], hscr, eng="G")
                else:
                    P.v_tt(y[ct], y[ct], hscr, ALU.add, eng="G")
                P.v_copy(hcol, V(hscr.ap[:, CK - 1:CK], hscr.regs), eng="G")
        for ct in range(16):
            P.v_stt(y[ct], u[ct], C.l3_D[:, ct:ct + 1], y[ct], ALU.mult, ALU.add)
            P.v_tt(yfin[ct], y[ct], z_s[ct], ALU.mult)
        for m in range(8):
            wb = BUF2[m % 2]
            P.dmaq(wb, C.dbf["l3_outproj"].ap[m].rearrange("p k j -> p (k j)"))
            psO = P.ps(P.take_bank(), 128, CK)
            for kt in range(16):
                P.t_mm(psO, V(wb.ap[:, kt * 128:(kt + 1) * 128], wb.regs),
                       yfin[kt], start=(kt == 0), stop=(kt == 15))
            P.v_stt(C.res[m][:, cs], psO, C.l3_outproj_b[:, m:m + 1],
                    C.res[m][:, cs], ALU.add, ALU.add)
    ar.consolidate()


def emit_l3f(C):
    P = C.P
    ar = C.ar
    nc = P.nc
    emit_ln(C)
    spill = V(nc.dram_tensor("spill_res", [D, T], F32).ap(), Reg(_Solo(), 0, 1))
    spill2 = V(nc.dram_tensor("spill_out", [D, T], F32).ap(), Reg(_Solo(), 0, 1))
    XH = [ar.v(i * 1024, 128, T, BF16) for i in range(8)]
    for c in range(4):
        emit_xhat_chunk(C, c * 512, 512,
                        [V(XH[i].ap[:, c * 512:(c + 1) * 512], XH[i].regs)
                         for i in range(8)],
                        P.take_bank(), P.take_bank())
    for i in range(8):
        P.dma("Q", spill[i * 128:(i + 1) * 128, :], C.res[i])
    hp = [C.resA.v(mi * 512, 128, 1024, BF16) for mi in range(32)]
    g1 = [ar.v(8192 + mi * 512, 128, 1024, BF16) for mi in range(32)]
    WBF = [ar.v(24576 + b * 512, 128, 1024, BF16) for b in range(2)]
    W2BF = [ar.v(25600 + b * 512, 128, 1024, BF16) for b in range(2)]
    HH = [ar.v(24576 + b * 1024, 128, 2048, BF16) for b in range(2)]
    for th in range(2):
        hs = [slice(th * 1024 + cc * 512, th * 1024 + cc * 512 + 512)
              for cc in range(2)]
        for mt in range(32):
            wb1 = WBF[mt % 2]
            wb2 = W2BF[mt % 2]
            P.dmaq(wb1, C.dbf["l3f_w1"].ap[mt].rearrange("p k j -> p (k j)"))
            P.dmaq(wb2, C.dbf["l3f_w2"].ap[mt].rearrange("p k j -> p (k j)"))
            for cc in range(2):
                psH = P.ps(P.take_bank(), 128, 512)
                psH2 = P.ps(P.take_bank(), 128, 512)
                for kt in range(8):
                    P.t_mm(psH, V(wb1.ap[:, kt * 128:(kt + 1) * 128], wb1.regs),
                           V(XH[kt].ap[:, hs[cc]], XH[kt].regs),
                           start=(kt == 0), stop=(kt == 7))
                for kt in range(8):
                    P.t_mm(psH2, V(wb2.ap[:, kt * 128:(kt + 1) * 128], wb2.regs),
                           V(XH[kt].ap[:, hs[cc]], XH[kt].regs),
                           start=(kt == 0), stop=(kt == 7))
                P.s_act(C.scr0, psH, AF.Silu, scale=C.l3f_s1[:, mt:mt + 1])
                P.v_ts(C.scr1, psH2, C.l3f_s2[:, mt:mt + 1], None, ALU.mult)
                P.v_tt(V(hp[mt].ap[:, cc * 512:(cc + 1) * 512], hp[mt].regs),
                       C.scr0, C.scr1, ALU.mult)
        for mt in range(32):
            bk0, bk1 = P.take_bank(), P.take_bank()
            for pc in range(2):
                wb = HH[pc % 2]
                P.dmaq(wb, C.dbf["l3f_h1"].ap[mt].rearrange("p k j -> p (k j)")
                       [:, pc * 2048:(pc + 1) * 2048])
                for cc in range(2):
                    psG = P.ps(bk0 if cc == 0 else bk1, 128, 512)
                    for kl in range(16):
                        kt = pc * 16 + kl
                        P.t_mm(psG, V(wb.ap[:, kl * 128:(kl + 1) * 128], wb.regs),
                               V(hp[kt].ap[:, cc * 512:(cc + 1) * 512], hp[kt].regs),
                               start=(kt == 0), stop=(kt == 31))
            for cc in range(2):
                psG = P.ps(bk0 if cc == 0 else bk1, 128, 512)
                P.s_act(V(g1[mt].ap[:, cc * 512:(cc + 1) * 512], g1[mt].regs),
                        psG, AF.Gelu, scale=C.l3f_sh1[:, mt:mt + 1])
        for mt in range(32):
            bk0, bk1 = P.take_bank(), P.take_bank()
            for pc in range(2):
                wb = HH[pc % 2]
                P.dmaq(wb, C.dbf["l3f_h2"].ap[mt].rearrange("p k j -> p (k j)")
                       [:, pc * 2048:(pc + 1) * 2048])
                for cc in range(2):
                    psG = P.ps(bk0 if cc == 0 else bk1, 128, 512)
                    for kl in range(16):
                        kt = pc * 16 + kl
                        P.t_mm(psG, V(wb.ap[:, kl * 128:(kl + 1) * 128], wb.regs),
                               V(g1[kt].ap[:, cc * 512:(cc + 1) * 512], g1[kt].regs),
                               start=(kt == 0), stop=(kt == 31))
            for cc in range(2):
                psG = P.ps(bk0 if cc == 0 else bk1, 128, 512)
                P.s_act(V(hp[mt].ap[:, cc * 512:(cc + 1) * 512], hp[mt].regs),
                        psG, AF.Gelu, scale=C.l3f_sh2[:, mt:mt + 1])
        for m in range(8):
            bk0, bk1 = P.take_bank(), P.take_bank()
            for pc in range(2):
                wb = HH[pc % 2]
                P.dmaq(wb, C.dbf["l3f_w3"].ap[m].rearrange("p k j -> p (k j)")
                       [:, pc * 2048:(pc + 1) * 2048])
                for cc in range(2):
                    psY = P.ps(bk0 if cc == 0 else bk1, 128, 512)
                    for kl in range(16):
                        kt = pc * 16 + kl
                        P.t_mm(psY, V(wb.ap[:, kl * 128:(kl + 1) * 128], wb.regs),
                               V(hp[kt].ap[:, cc * 512:(cc + 1) * 512], hp[kt].regs),
                               start=(kt == 0), stop=(kt == 31))
            for cc in range(2):
                psY = P.ps(bk0 if cc == 0 else bk1, 128, 512)
                P.v_ts(C.scr0, psY, C.l3f_s3[:, m:m + 1], None, ALU.mult)
                P.dma("G", C.scr1, spill[m * 128:(m + 1) * 128, hs[cc]])
                P.v_tt(C.scr0, C.scr0, C.scr1, ALU.add)
                P.dma("G", spill2[m * 128:(m + 1) * 128, hs[cc]], C.scr0)
    for i in range(8):
        P.dma("Q", C.res[i], spill2[i * 128:(i + 1) * 128, :])
    ar.consolidate()
    C.resA.consolidate()


EMITTERS = {"l0a": emit_l0a, "l0f": emit_l0f, "l1a": emit_l1a,
            "l1f": emit_l1f, "l2a": emit_l2a, "l2f": emit_l2f,
            "l3a": emit_l3a, "l3f": emit_l3f}

_PROG_CACHE = {}


def build_program(marrs, blocks):
    nc = bass.Bass("TRN2", target_bir_lowering=False)
    P = Prog(nc)
    C = setup_ctx(nc, P, marrs, blocks)
    load_block_statics(C, blocks)
    emit_convert(C, blocks)
    for i in range(8):
        P.dma("Q", C.res[i], C.xT[i * 128:(i + 1) * 128, :])
    mm_marks = [("start", 0)]
    for b in blocks:
        EMITTERS[b](C)
        C.resA.consolidate()
        C.ar.consolidate()
        mm_marks.append((b, P.mm_count))
    build_program.mm_marks = mm_marks
    for i in range(8):
        P.dma("Q", C.yT[i * 128:(i + 1) * 128, :], C.res[i])
    for qn in ("Q", "G", "S", "V"):
        for sl in range(P.NSLOT):
            key = f"{qn}#{sl}"
            P.raw_wait("Q", key, P.cnt[key])
    return P.build(), C.innames


def _get_prog(marrs, blocks):
    key = tuple(blocks)
    if key not in _PROG_CACHE:
        _PROG_CACHE[key] = build_program(marrs, blocks)
    return _PROG_CACHE[key]


def run_blocks(marrs, xts, blocks):
    nc, innames = _get_prog(marrs, blocks)
    in_maps = []
    for c in range(NCORE):
        d = {k: marrs[k] for k in innames if k != "xT"}
        d["xT"] = xts[c]
        in_maps.append(d)
    res = run_bass_kernel_spmd(nc, in_maps, core_ids=list(range(NCORE)))
    return [np.ascontiguousarray(np.asarray(res.results[c]["yT"]))
            for c in range(NCORE)]


def kernel(**inputs):
    m, xts = prep_inputs(inputs)
    groups = [ALL_BLOCKS] if FUSED else [[b] for b in ALL_BLOCKS]
    cur = xts
    for blocks in groups:
        cur = run_blocks(m, cur, blocks)
    out = np.stack([c.T for c in cur])
    return np.ascontiguousarray(out.astype(np.float32))


# version 06
# speedup vs baseline: 1.1711x; 1.1711x over previous
import numpy as np
import ml_dtypes
import concourse.bass as bass
import concourse.mybir as mybir
from concourse.bass_utils import run_bass_kernel_spmd

F32 = mybir.dt.float32
BF16 = mybir.dt.bfloat16
AF = mybir.ActivationFunctionType
ALU = mybir.AluOpType
AX = mybir.AxisListType.X

T = 2048
D = 1024
NCORE = 8
NPBF = ml_dtypes.bfloat16

ALL_BLOCKS = ["l0a", "l0f", "l1a", "l1f", "l2a", "l2f", "l3a", "l3f"]
L1A_PHASES = 7
L1A_CHUNKS = 8
L1A_THETA = 3
L3A_CKS = 8
L3A_SMAX = 16
FUSED = True
ARENA_N = 27648

ENGS = ("T", "V", "S", "Q", "G")


def _coltile(w):
    K, M = w.shape
    assert K % 128 == 0 and M % 128 == 0, (K, M)
    t = w.reshape(K // 128, 128, M // 128, 128)
    return np.ascontiguousarray(t.transpose(2, 1, 0, 3))


def _cols(v):
    n = v.shape[0] // 128
    return np.ascontiguousarray(v.reshape(n, 128).T)


def prep_inputs(inputs):
    f = lambda a: np.ascontiguousarray(np.asarray(a, np.float32))
    bf = lambda a: np.ascontiguousarray(np.asarray(a, NPBF))
    I = inputs
    m = {}
    m["c_ident"] = np.eye(128, dtype=np.float32)
    m["c_identb"] = bf(np.eye(128))
    m["c_onescol"] = np.ones((128, 1), np.float32)
    m["c_onescolb"] = bf(np.ones((128, 1)))
    m["c_onesrow"] = np.ones((96, 128), np.float32)
    m["c_onesrowb"] = bf(np.ones((96, 128)))
    hd = 64
    inv_freq = 1.0 / (10000.0 ** (np.arange(0, hd, 2, dtype=np.float64) / hd))
    ang = np.arange(T, dtype=np.float64)[None, :] * inv_freq[:, None]
    Cf = np.empty((64, T)); Sf = np.empty((64, T))
    Cf[0::2] = np.cos(ang); Cf[1::2] = np.cos(ang)
    Sf[0::2] = -np.sin(ang); Sf[1::2] = np.sin(ang)
    m["c_ropeC"] = bf(np.tile(Cf, (2, 1)))
    m["c_ropeS"] = bf(np.tile(Sf, (2, 1)))
    P = np.zeros((128, 128))
    for dd in range(128):
        P[dd + 1 if dd % 2 == 0 else dd - 1, dd] = 1.0
    m["c_swap"] = bf(P)
    st = np.zeros((128, 896))
    for i in range(128):
        st[i, i + 384:] = 1.0
    m["c_stair"] = bf(st)
    hs = np.zeros((128, 128)); hs[0:64, 0:64] = 1.0; hs[64:128, 64:128] = 1.0
    m["c_hsum"] = bf(hs)
    ep = np.zeros((8, 128), np.float32)
    for e in range(8):
        ep[e, e * 16:(e + 1) * 16] = 1.0
    m["c_erpat"] = ep

    for nm in ("wq", "wk", "wo"):
        m[f"l0_{nm}"] = _coltile(f(I[f"L0_attn_{nm}"]))
    m["l0_wv_rows"] = f(I["L0_attn_wv"])
    m["l0f_wg"] = np.ascontiguousarray(
        f(I["L0_ffn_Wg"]).reshape(8, 128, 8).transpose(1, 0, 2).reshape(128, 64))
    m["l0f_a"] = np.ascontiguousarray(
        f(I["L0_ffn_A"]).transpose(1, 0, 2).reshape(8, 128, 128)
        .transpose(1, 0, 2).reshape(128, 1024))
    m["l0f_b"] = f(I["L0_ffn_B"]).reshape(128, D)
    for nm in ("Wq", "Wk", "Wo"):
        m[f"l1_{nm.lower()}"] = _coltile(np.ascontiguousarray(f(I[f"L1_attn_{nm}"]).T))
    m["l1_wv_rows"] = np.ascontiguousarray(f(I["L1_attn_Wv"]).T)
    m["l1_bq"] = _cols(f(I["L1_attn_bq"]))
    m["l1_bk"] = _cols(f(I["L1_attn_bk"]))
    m["l1_bv_row"] = f(I["L1_attn_bv"]).reshape(1, D)
    m["l1_bo"] = _cols(f(I["L1_attn_bo"]))
    m["l1f_wg"] = np.ascontiguousarray(
        f(I["L1_ffn_gate_w"]).reshape(8, 128, 8).transpose(1, 0, 2).reshape(128, 64))
    m["l1f_w1"] = np.stack([_coltile(f(I["L1_ffn_w1"][e])) for e in range(8)])
    m["l1f_w2"] = np.stack([_coltile(f(I["L1_ffn_w2"][e])) for e in range(8)])
    m["l1f_b1"] = np.stack([_cols(f(I["L1_ffn_b1"][e])) for e in range(8)])
    m["l1f_b2"] = np.stack([_cols(f(I["L1_ffn_b2"][e])) for e in range(8)])
    for nm in ("Wq", "Wk", "Wv", "Wo"):
        m[f"l2_{nm.lower()}"] = _coltile(f(I[f"L2_attn_{nm}"]))
    for nm in ("bq", "bk", "bv", "bo"):
        m[f"l2_{nm}"] = _cols(f(I[f"L2_attn_{nm}"]))
    m["l2f_wg"] = np.ascontiguousarray(
        f(I["L2_ffn_Wg"]).T.reshape(8, 128, 8).transpose(1, 0, 2).reshape(128, 64))
    m["l2f_w1"] = np.stack([_coltile(f(I["L2_ffn_w1"][e])) for e in range(8)])
    m["l2f_w2"] = np.stack([_coltile(f(I["L2_ffn_w2"][e])) for e in range(8)])
    m["l2f_b1"] = np.stack([_cols(f(I["L2_ffn_b1"][e])) for e in range(8)])
    m["l2f_b2"] = np.stack([_cols(f(I["L2_ffn_b2"][e])) for e in range(8)])
    m["l3_inproj"] = _coltile(f(I["L3_attn_in_proj_w"]))
    m["l3_inproj_b"] = _cols(f(I["L3_attn_in_proj_b"]))
    m["l3_convw"] = np.ascontiguousarray(
        f(I["L3_attn_conv_w"]).reshape(16, 128, 4).transpose(1, 0, 2).reshape(128, 64))
    m["l3_convb"] = _cols(f(I["L3_attn_conv_b"]))
    xp = f(I["L3_attn_x_proj_w"])
    m["l3_xp_head"] = np.ascontiguousarray(xp[:, 0:32].reshape(16, 128, 32).transpose(1, 0, 2))
    m["l3_xp_u"] = _coltile(np.ascontiguousarray(xp[:, 32:2080]))
    xb = f(I["L3_attn_x_proj_b"])
    m["l3_xpb_head"] = np.ascontiguousarray(xb[0:32].reshape(32, 1))
    m["l3_xpb_u"] = _cols(xb[32:2080])
    m["l3_dt"] = f(I["L3_attn_dt_proj_w"])
    m["l3_dtb"] = _cols(f(I["L3_attn_dt_proj_b"]))
    m["l3_D"] = _cols(f(I["L3_attn_D"]))
    m["l3_outproj"] = _coltile(f(I["L3_attn_out_proj_w"]))
    m["l3_outproj_b"] = _cols(f(I["L3_attn_out_proj_b"]))
    for nm in ("w1", "w2", "h1", "h2", "w3"):
        m[f"l3f_{nm}"] = _coltile(np.ascontiguousarray(f(I[f"L3_ffn_{nm}_w"]).T))
    m["l3f_s1"] = _cols(f(I["L3_ffn_w1_s"]))
    m["l3f_s2"] = _cols(f(I["L3_ffn_w2_s"]))
    m["l3f_sh1"] = _cols(f(I["L3_ffn_h1_s"]))
    m["l3f_sh2"] = _cols(f(I["L3_ffn_h2_s"]))
    m["l3f_s3"] = _cols(f(I["L3_ffn_w3_s"]))

    x = f(I["L0_attn_x"])
    xts = [np.ascontiguousarray(x[c].T) for c in range(NCORE)]
    return m, xts


class Reg:
    __slots__ = ("arena", "lo", "hi", "writers", "readers", "excl_read")

    def __init__(self, arena, lo, hi, excl_read=False):
        self.arena, self.lo, self.hi = arena, lo, hi
        self.writers = {}
        self.readers = {}
        self.excl_read = excl_read


class V:
    __slots__ = ("ap", "regs")

    def __init__(self, ap, regs):
        self.ap = ap
        self.regs = regs if isinstance(regs, list) else [regs]

    def __getitem__(self, idx):
        return V(self.ap[idx], self.regs)


class Arena:
    def __init__(self, P, name, f32_elems):
        self.P = P
        self.name = name
        self.tf = P.nc.alloc_sbuf_tensor(name, [128, f32_elems], F32)
        self.tb = self.tf.bitcast(BF16)
        self.size = f32_elems * 4
        self.regs = []

    def v(self, off_f32, parts, free, dtype=F32):
        esz = 4 if dtype == F32 else 2
        lo = off_f32 * 4
        hi = lo + free * esz
        assert hi <= self.size, (self.name, off_f32, free, dtype)
        r = Reg(self, lo, hi)
        self.regs.append(r)
        if dtype == F32:
            ap = self.tf[0:parts, off_f32:off_f32 + free]
        else:
            ap = self.tb[0:parts, off_f32 * 2:off_f32 * 2 + free]
        return V(ap, r)

    def consolidate(self):
        if not self.regs:
            return
        r = Reg(self, 0, self.size)
        for o in self.regs:
            for e, c in o.writers.items():
                r.writers[e] = max(r.writers.get(e, 0), c)
            for e, c in o.readers.items():
                r.readers[e] = max(r.readers.get(e, 0), c)
        self.regs = [r]


class _Solo:
    regs = None


class Prog:
    NSLOT = 8

    def __init__(self, nc, same_engine_waits=True):
        self.nc = nc
        keys = list(ENGS)
        for qn in ("Q", "G", "S", "V"):
            keys += [f"{qn}#{i}" for i in range(self.NSLOT)]
        self.sem = {e: nc.alloc_semaphore(f"sem_{e.replace('#','_')}") for e in keys}
        self.cnt = {e: 0 for e in keys}
        self.q = {e: [] for e in ENGS}
        self.waited = {e: {e2: 0 for e2 in keys} for e in ENGS}
        self._slot_rr = {"Q": 0, "G": 0, "S": 0, "V": 0}
        self.sew = same_engine_waits
        self.psum = []
        self.psreg = []
        for i in range(8):
            self.psum.append(nc.alloc_psum_tensor(f"ps{i}", [128, 512], F32))
            self.psreg.append(Reg(_Solo(), 0, 1, excl_read=True))
        self._bank_rr = 0
        self._dma_rr = 0
        self.rr_banks = [0, 1, 2, 3, 4, 5, 6]

    def ps(self, i, parts=128, free=512, off=0):
        return V(self.psum[i][0:parts, off:off + free], self.psreg[i])

    def take_bank(self, n=1):
        out = []
        for _ in range(n):
            out.append(self.rr_banks[self._bank_rr % len(self.rr_banks)])
            self._bank_rr += 1
        return out if n > 1 else out[0]

    def stat(self, name, shape, dtype):
        t = self.nc.alloc_sbuf_tensor(name, shape, dtype)
        return V(t[:, :], Reg(_Solo(), 0, 1)), t

    def _expand(self, regs):
        out = []
        for r in regs:
            ar = r.arena
            if isinstance(ar, Arena):
                for o in ar.regs:
                    if r.lo < o.hi and o.lo < r.hi:
                        out.append(o)
            else:
                out.append(r)
        return out

    def op(self, eng, emit, reads=(), writes=(), inc=1, key=None):
        key = key or eng
        rr = [r for x in reads for r in (x.regs if isinstance(x, V) else [x])]
        ww = [r for x in writes for r in (x.regs if isinstance(x, V) else [x])]
        err = self._expand(rr)
        eww = self._expand(ww)
        deps = {}
        for r in err:
            for e, c in r.writers.items():
                deps[e] = max(deps.get(e, 0), c)
            if r.excl_read:
                for e, c in r.readers.items():
                    if e != eng:
                        deps[e] = max(deps.get(e, 0), c)
        for w in eww:
            for e, c in w.writers.items():
                deps[e] = max(deps.get(e, 0), c)
            for e, c in w.readers.items():
                deps[e] = max(deps.get(e, 0), c)
        waits = []
        for e, c in deps.items():
            if e == eng and not self.sew:
                continue
            if self.waited[eng][e] >= c:
                continue
            self.waited[eng][e] = c
            waits.append((self.sem[e], c))
        sem_self = self.sem[key]
        cnt_after = self.cnt[key] + inc

        def closure(E, waits=waits, emit=emit, inc=inc):
            for s, c in waits:
                E.wait_ge(s, c)
            ins = emit(E)
            if inc and ins is not None:
                ins.then_inc(sem_self, inc)

        self.q[eng].append(closure)
        if inc:
            self.cnt[key] = cnt_after
            for r in eww:
                r.writers = {key: cnt_after}
                r.readers = {}
            for r in err:
                r.readers[key] = cnt_after
        return cnt_after

    def raw_wait(self, eng, sem_key, cnt):
        if cnt <= 0 or self.waited[eng][sem_key] >= cnt:
            return
        self.waited[eng][sem_key] = cnt
        s = self.sem[sem_key]
        self.q[eng].append(lambda E, s=s, c=cnt: E.wait_ge(s, c))

    mm_count = 0

    def t_mm(self, bank_v, lhsT, rhs, start, stop):
        self.mm_count = getattr(self, 'mm_count', 0) + 1
        return self.op(
            "T",
            lambda E, o=bank_v.ap, l=lhsT.ap, r=rhs.ap, st=start, sp=stop:
                E.matmul(o, l, r, start=st, stop=sp),
            reads=[lhsT, rhs], writes=[bank_v] if (start or stop) else [], inc=1)

    def t_transpose(self, bank_v, in_v, ident_v):
        return self.op(
            "T",
            lambda E, o=bank_v.ap, i=in_v.ap, d=ident_v.ap: E.transpose(o, i, d),
            reads=[in_v, ident_v], writes=[bank_v], inc=1)

    def s_act(self, out, in_, func, bias=0.0, scale=1.0, accum=None):
        bap = bias.ap if isinstance(bias, V) else bias
        sap = scale.ap if isinstance(scale, V) else scale
        reads = [in_] + [x for x in (bias, scale) if isinstance(x, V)]
        writes = [out] + ([accum] if accum is not None else [])
        acc = accum.ap if accum is not None else None
        return self.op(
            "S",
            lambda E, o=out.ap, i=in_.ap, ff=func, b=bap, s=sap, a=acc:
                E.activation(o, i, ff, bias=b, scale=s, accum_out=a),
            reads=reads, writes=writes, inc=1)

    def s_copy(self, out, in_):
        return self.op("S", lambda E, o=out.ap, i=in_.ap: E.copy(o, i),
                       reads=[in_], writes=[out], inc=1)

    def v_tt(self, out, a, b, alu, eng="V"):
        return self.op(eng, lambda E, o=out.ap, x=a.ap, y=b.ap, u=alu:
                       E.tensor_tensor(o, x, y, u),
                       reads=[a, b], writes=[out], inc=1)

    def v_ts(self, out, in_, s1, s2, op0, op1=ALU.bypass, eng="V"):
        s1a = s1.ap if isinstance(s1, V) else s1
        s2a = s2.ap if isinstance(s2, V) else s2
        reads = [in_] + [x for x in (s1, s2) if isinstance(x, V)]
        return self.op(eng, lambda E, o=out.ap, i=in_.ap, a=s1a, b=s2a, u0=op0, u1=op1:
                       E.tensor_scalar(o, i, a, b, u0, u1),
                       reads=reads, writes=[out], inc=1)

    def v_stt(self, out, in0, scalar, in1, op0, op1, eng="V"):
        sa = scalar.ap if isinstance(scalar, V) else scalar
        reads = [in0, in1] + ([scalar] if isinstance(scalar, V) else [])
        return self.op(eng, lambda E, o=out.ap, i=in0.ap, s=sa, j=in1.ap, u0=op0, u1=op1:
                       E.scalar_tensor_tensor(o, i, s, j, u0, u1),
                       reads=reads, writes=[out], inc=1)

    def v_scan(self, out, d0, d1, init):
        ia = init.ap if isinstance(init, V) else init
        reads = [d0, d1] + ([init] if isinstance(init, V) else [])
        return self.op("V", lambda E, o=out.ap, a=d0.ap, b=d1.ap, i=ia:
                       E.tensor_tensor_scan(o, a, b, i, ALU.mult, ALU.add),
                       reads=reads, writes=[out], inc=1)

    def v_red(self, out, in_, alu=ALU.max, negate=False, eng="V"):
        return self.op(eng, lambda E, o=out.ap, i=in_.ap, u=alu, n=negate:
                       E.tensor_reduce(o, i, AX, u, negate=n),
                       reads=[in_], writes=[out], inc=1)

    def v_copy(self, out, in_, eng="V"):
        if eng == "S":
            return self.op(eng, lambda E, o=out.ap, i=in_.ap: E.copy(o, i),
                           reads=[in_], writes=[out], inc=1)
        return self.op(eng, lambda E, o=out.ap, i=in_.ap: E.tensor_copy(o, i),
                       reads=[in_], writes=[out], inc=1)

    def v_recip(self, out, in_):
        return self.op("V", lambda E, o=out.ap, i=in_.ap: E.reciprocal(o, i),
                       reads=[in_], writes=[out], inc=1)

    def memset(self, out, val, eng="V"):
        return self.op(eng, lambda E, o=out.ap, v=val: E.memset(o, v),
                       reads=[], writes=[out], inc=1)

    def dmaq(self, dst, src):
        return self.dma("Q", dst, src)

    def dma(self, qeng, dst, src):
        reads = [src] if isinstance(src, V) else []
        writes = [dst] if isinstance(dst, V) else []
        da = dst.ap if isinstance(dst, V) else dst
        sa = src.ap if isinstance(src, V) else src
        slot = self._slot_rr[qeng] % self.NSLOT
        self._slot_rr[qeng] += 1
        key = f"{qeng}#{slot}"
        self.raw_wait(qeng, key, self.cnt[key])
        return self.op(qeng, lambda E, d=da, s=sa: E.dma_start(d, s),
                       reads=reads, writes=writes, inc=16, key=key)

    def build(self):
        nc = self.nc
        with nc.Block() as blk:
            blk.tensor(lambda E: [fn(E) for fn in self.q["T"]])
            blk.vector(lambda E: [fn(E) for fn in self.q["V"]])
            blk.scalar(lambda E: [fn(E) for fn in self.q["S"]])
            blk.sync(lambda E: [fn(E) for fn in self.q["Q"]])
            blk.gpsimd(lambda E: [fn(E) for fn in self.q["G"]])
        return nc


class Ctx:
    pass


PREFIX = {"l0a": "l0_", "l0f": "l0f_", "l1a": "l1_", "l1f": "l1f_",
          "l2a": "l2_", "l2f": "l2f_", "l3a": "l3_", "l3f": "l3f_"}


def setup_ctx(nc, P, marrs, blocks):
    C = Ctx()
    C.P = P
    C.d = {}

    def din(name):
        a = marrs[name]
        dt = BF16 if a.dtype == NPBF else F32
        C.d[name] = nc.dram_tensor(name, list(a.shape), dt, kind="ExternalInput").ap()

    names = set()
    for k in marrs:
        if k.startswith("c_"):
            names.add(k)
        else:
            for b in blocks:
                if k.startswith(PREFIX[b]):
                    names.add(k)
    for k in sorted(names):
        din(k)
    C.innames = sorted(names) + ["xT"]
    C.xT = nc.dram_tensor("xT", [D, T], F32, kind="ExternalInput").ap()
    C.yT = nc.dram_tensor("yT", [D, T], F32, kind="ExternalOutput").ap()

    C.resA = Arena(P, "resA", 8 * T)
    C.res = [C.resA.v(i * T, 128, T, F32) for i in range(8)]
    C.ar = Arena(P, "arena", ARENA_N)

    def stat_load(name, parts, free, dtype=F32):
        vv, _ = P.stat("sb_" + name, [parts, free], dtype)
        P.dma("Q", vv, C.d[name])
        return vv

    C.ident = stat_load("c_ident", 128, 128)
    C.identb = stat_load("c_identb", 128, 128, BF16)
    C.onescol = stat_load("c_onescol", 128, 1)
    C.onescolb = stat_load("c_onescolb", 128, 1, BF16)
    C.onesrow = stat_load("c_onesrow", 96, 128)
    C.onesrowb = stat_load("c_onesrowb", 96, 128, BF16)
    C.rowAB, rowt = P.stat("rowAB", [96, 6144], F32)
    C.rowABb = rowt.bitcast(BF16)
    C.eps, _ = P.stat("c_eps", [96, 1], F32)
    P.memset(C.eps, 1e-5)
    C.scr0, _ = P.stat("scr0", [128, 512], F32)
    C.scr1, _ = P.stat("scr1", [128, 512], F32)
    C.r_mean = V(C.rowAB.ap[0:1, 0:T], C.rowAB.regs)
    C.r_rstd = V(C.rowAB.ap[0:1, T:2 * T], C.rowAB.regs)
    C.r_sumsq = V(C.rowAB.ap[0:1, 2 * T:3 * T], C.rowAB.regs)
    C.stat_load = stat_load
    return C


def load_block_statics(C, blocks):
    sl = C.stat_load
    if "l1a" in blocks:
        C.l1_bq = sl("l1_bq", 128, 8)
        C.l1_bk = sl("l1_bk", 128, 8)
        C.l1_bo = sl("l1_bo", 128, 8)
    if "l1f" in blocks:
        C.l1f_wgf = sl("l1f_wg", 128, 64)
        C.l1f_b1 = sl("l1f_b1", 128, 256)
        C.l1f_b2 = sl("l1f_b2", 128, 64)
    if "l2a" in blocks:
        for nm in ("bq", "bk", "bv", "bo"):
            setattr(C, "l2_" + nm, sl("l2_" + nm, 128, 8))
    if "l2f" in blocks:
        C.l2f_wgf = sl("l2f_wg", 128, 64)
        C.l2f_b1 = sl("l2f_b1", 128, 256)
        C.l2f_b2 = sl("l2f_b2", 128, 64)
    if "l3a" in blocks:
        C.l3_inproj_b = sl("l3_inproj_b", 128, 32)
        C.l3_convw = sl("l3_convw", 128, 64)
        C.l3_convb = sl("l3_convb", 128, 16)
        C.l3_xpb_head = sl("l3_xpb_head", 32, 1)
        C.l3_xpb_u = sl("l3_xpb_u", 128, 16)
        C.l3_dtb = sl("l3_dtb", 128, 16)
        C.l3_D = sl("l3_D", 128, 16)
        C.l3_outproj_b = sl("l3_outproj_b", 128, 8)
    if "l3f" in blocks:
        for nm in ("s1", "s2", "sh1", "sh2"):
            setattr(C, "l3f_" + nm, sl("l3f_" + nm, 128, 32))
        C.l3f_s3 = sl("l3f_s3", 128, 8)


CONVERT_ORDER = [
    ("ct", "l0_wq"), ("ct", "l0_wo"),
    ("ct", "l1_wk"), ("rw", "l1_wv_rows"), ("ct", "l1_wq"), ("ct", "l1_wo"),
    ("ex", "l1f_w1"), ("ex", "l1f_w2"),
    ("ct", "l2_wq"), ("ct", "l2_wk"), ("ct", "l2_wv"), ("ct", "l2_wo"),
    ("ex", "l2f_w1"), ("ex", "l2f_w2"),
    ("ct", "l3_inproj"), ("ct", "l3_xp_u"), ("ct", "l3_outproj"),
    ("ct", "l3f_w1"), ("ct", "l3f_w2"), ("ct", "l3f_h1"), ("ct", "l3f_h2"),
    ("ct", "l3f_w3")]


def emit_convert(C, blocks):
    P = C.P
    ar = C.ar
    nc = P.nc
    C.dbf = {}
    items = []

    def scratch(name, shape):
        t = nc.dram_tensor(name + "_bf", list(shape), BF16)
        vv = V(t.ap(), Reg(_Solo(), 0, 1))
        C.dbf[name] = vv
        return vv

    for kind, name in CONVERT_ORDER:
        if name not in C.d:
            continue
        sh = C.d[name].shape
        dst = scratch(name, sh)
        if kind == "ct":
            nmt, _, ktn, _ = sh
            for mt in range(nmt):
                items.append((C.d[name][mt].rearrange("p k j -> p (k j)"),
                              dst.ap[mt].rearrange("p k j -> p (k j)"), ktn * 128))
        elif kind == "rw":
            for kt in range(sh[0] // 128):
                items.append((C.d[name][kt * 128:(kt + 1) * 128, :],
                              dst.ap[kt * 128:(kt + 1) * 128, :], sh[1]))
        else:
            for e in range(sh[0]):
                for mt in range(sh[1]):
                    items.append((C.d[name][e, mt].rearrange("p k j -> p (k j)"),
                                  dst.ap[e, mt].rearrange("p k j -> p (k j)"),
                                  sh[3] * 128))
    NI, NO = 3, 2
    inb = [C.resA.v(i * 4096, 128, 4096, F32) for i in range(NI)]
    outb = [C.resA.v(NI * 4096 + i * 2048, 128, 4096, BF16) for i in range(NO)]
    n = len(items)
    for k in range(min(NI - 1, n)):
        sap, dap, X = items[k]
        P.dma("G" if k % 2 == 0 else "S",
              V(inb[k % NI].ap[:, 0:X], inb[k % NI].regs), sap)
    for k in range(n):
        sap, dap, X = items[k]
        kin = k + NI - 1
        if kin < n:
            s2, d2, X2 = items[kin]
            P.dma("G" if kin % 2 == 0 else "S",
                  V(inb[kin % NI].ap[:, 0:X2], inb[kin % NI].regs), s2)
        P.v_copy(V(outb[k % NO].ap[:, 0:X], outb[k % NO].regs),
                 V(inb[k % NI].ap[:, 0:X], inb[k % NI].regs), eng="V")
        P.dma("S" if k % 2 == 0 else "G", dap,
              V(outb[k % NO].ap[:, 0:X], outb[k % NO].regs))
    C.resA.consolidate()


def emit_ln(C):
    P = C.P
    b0, b1 = P.take_bank(), P.take_bank()
    for c in range(4):
        cs = slice(c * 512, (c + 1) * 512)
        psS = P.ps(b0, 1, 512)
        psQ = P.ps(b1, 1, 512)
        for i in range(8):
            P.t_mm(psS, C.onescol, C.res[i][:, cs], start=(i == 0), stop=(i == 7))
        for i in range(8):
            sq = C.scr0 if i % 2 == 0 else C.scr1
            P.s_act(sq, C.res[i][:, cs], AF.Square)
            P.t_mm(psQ, C.onescol, sq, start=(i == 0), stop=(i == 7))
        P.s_copy(V(C.r_mean.ap[:, cs], C.r_mean.regs), psS)
        P.s_copy(V(C.r_sumsq.ap[:, cs], C.r_sumsq.regs), psQ)
    P.v_ts(C.r_mean, C.r_mean, 1.0 / D, None, ALU.mult)
    P.v_ts(C.r_sumsq, C.r_sumsq, 1.0 / D, None, ALU.mult)
    P.v_tt(C.r_rstd, C.r_mean, C.r_mean, ALU.mult)
    P.v_tt(C.r_sumsq, C.r_sumsq, C.r_rstd, ALU.subtract)
    P.s_act(C.r_sumsq, C.r_sumsq, AF.Ln, bias=C.eps[0:1, :])
    P.s_act(C.r_rstd, C.r_sumsq, AF.Exp, scale=-0.5)


def emit_xhat_chunk(C, t0, width, dst_tiles, bm0, bm1, tmp=None):
    P = C.P
    cs = slice(t0, t0 + width)
    psM = P.ps(bm0, 128, width)
    psR = P.ps(bm1, 128, width)
    P.t_mm(psM, C.onesrow[0:1, :], V(C.r_mean.ap[:, cs], C.r_mean.regs), True, True)
    P.t_mm(psR, C.onesrow[0:1, :], V(C.r_rstd.ap[:, cs], C.r_rstd.regs), True, True)
    tmp = tmp if tmp is not None else C.scr0
    for i in range(8):
        tv = V(tmp.ap[:, 0:width], tmp.regs)
        P.v_tt(tv, C.res[i][:, cs], psM, ALU.subtract)
        P.v_tt(dst_tiles[i], tv, psR, ALU.mult)


def stream_coltile(C, name, mt, kt_n, stage_f32, dst_bf16, cast_eng="S"):
    P = C.P
    P.dmaq(dst_bf16, C.dbf[name].ap[mt].rearrange("p k j -> p (k j)"))


def emit_l0a(C):
    P = C.P
    ar = C.ar
    emit_ln(C)
    KR = [ar.v(2048 + i * 1024, 128, T, BF16) for i in range(8)]
    VT = ar.v(10240, 128, 16 * 1024, BF16)
    WST = ar.v(22528, 128, 1024, F32)
    ropeC = ar.v(23552, 128, T, BF16)
    ropeS = ar.v(24576, 128, T, BF16)
    stair = ar.v(25600, 128, 896, BF16)
    swapb = ar.v(26048, 128, 128, BF16)
    P.dma("Q", ropeC, C.d["c_ropeC"])
    P.dma("Q", ropeS, C.d["c_ropeS"])
    P.dma("Q", stair, C.d["c_stair"])
    P.dma("Q", swapb, C.d["c_swap"])

    rsc1 = ar.v(26112, 128, 256, F32)
    rsc2 = ar.v(26368, 128, 256, F32)

    def rope_tile(src_bf, psb, dst, tslice):
        w = tslice.stop - tslice.start
        psw = P.ps(psb, 128, w)
        P.t_mm(psw, swapb, src_bf, True, True)
        P.v_tt(V(rsc1.ap[:, 0:w], rsc1.regs), src_bf,
               V(ropeC.ap[:, tslice], ropeC.regs), ALU.mult)
        P.v_tt(V(rsc2.ap[:, 0:w], rsc2.regs), psw,
               V(ropeS.ap[:, tslice], ropeS.regs), ALU.mult)
        P.v_tt(dst, V(rsc1.ap[:, 0:w], rsc1.regs),
               V(rsc2.ap[:, 0:w], rsc2.regs), ALU.add)

    WKB = [ar.v(18560 + m * 512, 128, 1024, BF16) for m in range(8)]
    for m in range(8):
        P.dma("Q", WST, C.d["l0_wk"][m].rearrange("p k j -> p (k j)"))
        P.v_copy(WKB[m], WST, eng="S" if m % 2 == 0 else "V")
    qtmp = ar.v(26624, 128, 512, BF16)
    for c in range(4):
        cs = slice(c * 512, (c + 1) * 512)
        XHC = [ar.v(i * 256, 128, 512, BF16) for i in range(8)]
        emit_xhat_chunk(C, c * 512, 512, XHC, P.take_bank(), P.take_bank())
        for m in range(8):
            psK = P.ps(P.take_bank(), 128, 512)
            for kt in range(8):
                P.t_mm(psK, V(WKB[m].ap[:, kt * 128:(kt + 1) * 128], WKB[m].regs),
                       XHC[kt], start=(kt == 0), stop=(kt == 7))
            P.s_copy(qtmp, psK)
            for half in range(2):
                hs = slice(c * 512 + half * 256, c * 512 + half * 256 + 256)
                rope_tile(V(qtmp.ap[:, half * 256:half * 256 + 256], qtmp.regs),
                          P.take_bank(),
                          V(KR[m].ap[:, hs], KR[m].regs), hs)

    WVB = [ar.v(18560 + kt * 512, 128, 1024, BF16) for kt in range(8)]
    for kt in range(8):
        P.dma("Q", WST, C.d["l0_wv_rows"][kt * 128:(kt + 1) * 128, :])
        P.v_copy(WVB[kt], WST, eng="S" if kt % 2 == 0 else "V")
    for c in range(4):
        XHC = [ar.v(i * 256, 128, 512, BF16) for i in range(8)]
        emit_xhat_chunk(C, c * 512, 512, XHC, P.take_bank(), P.take_bank())
        for tloc in range(4):
            tt = c * 4 + tloc
            for ec in range(2):
                psV = P.ps(P.take_bank(), 128, 512)
                for kt in range(8):
                    P.t_mm(psV,
                           V(XHC[kt].ap[:, tloc * 128:(tloc + 1) * 128], XHC[kt].regs),
                           V(WVB[kt].ap[:, ec * 512:(ec + 1) * 512], WVB[kt].regs),
                           start=(kt == 0), stop=(kt == 7))
                P.s_copy(V(VT.ap[:, tt * 1024 + ec * 512:tt * 1024 + ec * 512 + 512],
                           VT.regs), psV)

    otmp = ar.v(26880, 128, 256, F32)
    for c in range(8):
        cw = 256
        cs = slice(c * cw, (c + 1) * cw)
        XHC = [ar.v(i * 128, 128, cw, BF16) for i in range(8)]
        QR = [ar.v(1024 + i * 128, 128, cw, BF16) for i in range(8)]
        ATT = [ar.v(18432 + tt * 128, 128, cw, BF16) for tt in range(16)]
        OUTC = [ar.v(20480 + i * 128, 128, cw, BF16) for i in range(8)]
        WQB = [ar.v(21504 + b * 512, 128, 1024, BF16) for b in range(2)]
        WOB = [ar.v(22528 + b * 512, 128, 1024, BF16) for b in range(2)]
        emit_xhat_chunk(C, c * cw, cw, XHC, P.take_bank(), P.take_bank())
        for m in range(8):
            wb = WQB[m % 2]
            stream_coltile(C, "l0_wq", m, 8, WST, wb)
            psQ = P.ps(P.take_bank(), 128, cw)
            for kt in range(8):
                P.t_mm(psQ, V(wb.ap[:, kt * 128:(kt + 1) * 128], wb.regs),
                       XHC[kt], start=(kt == 0), stop=(kt == 7))
            P.s_copy(V(qtmp.ap[:, 0:cw], qtmp.regs), psQ)
            rope_tile(V(qtmp.ap[:, 0:cw], qtmp.regs), P.take_bank(), QR[m], cs)
        ndiag = 2 * (c + 1)
        for h in range(16):
            mt = h // 2
            rlo = (h % 2) * 64
            bS1, bS2 = P.take_bank(), P.take_bank()
            bO, bR = P.take_bank(), P.take_bank()
            psO = V(P.psum[bO][rlo:rlo + 64, 0:cw], P.psreg[bO])
            psR = V(P.psum[bR][rlo:rlo + 1, 0:cw], P.psreg[bR])
            for tt in range(ndiag + 1):
                if tt < ndiag:
                    psS = P.ps(bS1 if tt % 2 == 0 else bS2, 128, cw)
                    P.t_mm(psS,
                           V(KR[mt].ap[rlo:rlo + 64, tt * 128:(tt + 1) * 128],
                             KR[mt].regs),
                           V(QR[mt].ap[rlo:rlo + 64, :], QR[mt].regs), True, True)
                    P.s_act(ATT[tt], psS, AF.Exp, scale=0.125)
                    if tt >= 2 * c:
                        delta = tt * 128 - c * cw
                        P.v_tt(ATT[tt], ATT[tt],
                               V(stair.ap[:, 384 - delta:384 - delta + cw],
                                 stair.regs), ALU.mult)
                if tt > 0:
                    pt = tt - 1
                    P.t_mm(psO,
                           V(VT.ap[:, pt * 1024 + h * 64:pt * 1024 + h * 64 + 64],
                             VT.regs),
                           ATT[pt], start=(pt == 0), stop=(pt == ndiag - 1))
                    P.t_mm(psR, C.onescolb, ATT[pt],
                           start=(pt == 0), stop=(pt == ndiag - 1))
            crow = V(C.rowAB.ap[rlo:rlo + 1, 2 * T:2 * T + cw], C.rowAB.regs)
            P.v_recip(crow, psR)
            bC = P.take_bank()
            psC = V(P.psum[bC][rlo:rlo + 64, 0:cw], P.psreg[bC])
            P.t_mm(psC, V(C.onesrow.ap[rlo:rlo + 1, 0:64], C.onesrow.regs),
                   crow, True, True)
            ot = V(otmp.ap[rlo:rlo + 64, :], otmp.regs)
            P.s_copy(ot, psO)
            P.v_tt(V(OUTC[mt].ap[rlo:rlo + 64, :], OUTC[mt].regs), ot, psC, ALU.mult)
        for m in range(8):
            wb = WOB[m % 2]
            stream_coltile(C, "l0_wo", m, 8, WST, wb)
            psY = P.ps(P.take_bank(), 128, cw)
            for kt in range(8):
                P.t_mm(psY, V(wb.ap[:, kt * 128:(kt + 1) * 128], wb.regs),
                       OUTC[kt], start=(kt == 0), stop=(kt == 7))
            P.v_tt(C.res[m][:, cs], C.res[m][:, cs], psY, ALU.add)
    ar.consolidate()


def emit_l0f(C):
    P = C.P
    ar = C.ar
    emit_ln(C)
    XH = [ar.v(i * 1024, 128, T, BF16) for i in range(8)]
    for c in range(4):
        emit_xhat_chunk(C, c * 512, 512,
                        [V(XH[i].ap[:, c * 512:(c + 1) * 512], XH[i].regs)
                         for i in range(8)],
                        P.take_bank(), P.take_bank())
    LG = ar.v(8192, 8, T, F32)
    CW = ar.v(10240, 8, T, F32)
    AFL = ar.v(12288, 128, 1024, BF16)
    BFL = ar.v(12800, 128, 1024, BF16)
    WGB = ar.v(13312, 128, 64, BF16)
    erpat = ar.v(13344, 8, 128, F32)
    P.dma("Q", erpat, C.d["c_erpat"])
    stg = ar.v(13568, 128, 1024, F32)
    P.dma("Q", stg, C.d["l0f_a"])
    P.v_copy(AFL, stg)
    P.dma("Q", stg, C.d["l0f_b"])
    P.v_copy(BFL, stg)
    wgstg = ar.v(13472, 128, 64, F32)
    P.dma("Q", wgstg, C.d["l0f_wg"])
    P.v_copy(WGB, wgstg)
    bG = P.take_bank()
    for c in range(4):
        cs = slice(c * 512, (c + 1) * 512)
        psG = P.ps(bG, 8, 512)
        for kt in range(8):
            P.t_mm(psG, V(WGB.ap[:, kt * 8:(kt + 1) * 8], WGB.regs),
                   V(XH[kt].ap[:, cs], XH[kt].regs), start=(kt == 0), stop=(kt == 7))
        P.s_copy(V(LG.ap[:, cs], LG.regs), psG)
    tokbase = 14592
    LGt = ar.v(tokbase, 128, 8, F32)
    m1 = ar.v(tokbase + 8, 128, 1, F32)
    m1n = ar.v(tokbase + 9, 128, 1, F32)
    m2 = ar.v(tokbase + 10, 128, 1, F32)
    qv = ar.v(tokbase + 11, 128, 1, F32)
    w1 = ar.v(tokbase + 12, 128, 1, F32)
    w2 = ar.v(tokbase + 13, 128, 1, F32)
    msk1 = ar.v(tokbase + 16, 128, 8, F32)
    msk2 = ar.v(tokbase + 24, 128, 8, F32)
    g2 = ar.v(tokbase + 32, 128, 8, F32)
    cwt = ar.v(tokbase + 40, 128, 8, F32)
    bT1, bT2 = P.take_bank(), P.take_bank()
    for tt in range(16):
        psL = P.ps(bT1, 128, 8)
        P.t_transpose(psL, V(LG.ap[:, tt * 128:(tt + 1) * 128], LG.regs),
                      V(C.ident.ap[0:8, 0:8], C.ident.regs))
        P.s_copy(LGt, psL)
        P.v_red(m1, LGt, ALU.max)
        P.v_red(m1n, LGt, ALU.max, negate=True)
        P.v_ts(msk1, LGt, m1, None, ALU.is_equal)
        P.v_stt(g2, msk1, -1e30, LGt, ALU.mult, ALU.add)
        P.v_red(m2, g2, ALU.max)
        P.v_ts(msk2, g2, m2, None, ALU.is_equal)
        P.s_act(qv, m2, AF.Exp, bias=m1n)
        P.v_ts(qv, qv, 1.0, None, ALU.add)
        P.v_recip(w1, qv)
        P.v_ts(qv, qv, -1.0, None, ALU.add)
        P.v_tt(w2, qv, w1, ALU.mult)
        P.v_ts(cwt, msk1, w1, None, ALU.mult)
        P.v_stt(cwt, msk2, w2, cwt, ALU.mult, ALU.add)
        psB = P.ps(bT2, 8, 128)
        P.t_transpose(psB, cwt, C.ident)
        P.s_copy(V(CW.ap[:, tt * 128:(tt + 1) * 128], CW.regs), psB)
    HW = ar.v(14336, 128, 512, BF16)
    for c in range(4):
        cs = slice(c * 512, (c + 1) * 512)
        psH = P.ps(P.take_bank(), 128, 512)
        for kt in range(8):
            P.t_mm(psH, V(AFL.ap[:, kt * 128:(kt + 1) * 128], AFL.regs),
                   V(XH[kt].ap[:, cs], XH[kt].regs), start=(kt == 0), stop=(kt == 7))
        P.s_copy(C.scr0, psH)
        psBC = P.ps(P.take_bank(), 128, 512)
        P.t_mm(psBC, erpat, V(CW.ap[:, cs], CW.regs), True, True)
        P.v_tt(HW, C.scr0, psBC, ALU.mult)
        for mm in range(8):
            psO = P.ps(P.take_bank(), 128, 512)
            P.t_mm(psO, V(BFL.ap[:, mm * 128:(mm + 1) * 128], BFL.regs), HW,
                   True, True)
            P.v_tt(C.res[mm][:, cs], C.res[mm][:, cs], psO, ALU.add)
    ar.consolidate()


def emit_l1a(C):
    P = C.P
    ar = C.ar
    emit_ln(C)
    KT = [ar.v(2048 + i * 1024, 128, T, BF16) for i in range(8)]
    VT = ar.v(10240, 128, 16 * 1024, BF16)
    WST = ar.v(25088, 128, 1024, F32)
    sc2 = ar.v(26112, 128, 256, F32)
    sc3 = ar.v(26368, 128, 256, F32)
    bvrow = ar.v(26624, 1, D, BF16)
    bvstg = ar.v(25088, 1, D, F32)
    P.dma("Q", bvstg, C.d["l1_bv_row"])
    P.v_copy(bvrow, bvstg)

    WKB = [ar.v(20480 + m * 512, 128, 1024, BF16) for m in range(8)]
    for m in range(8):
        stream_coltile(C, "l1_wk", m, 8, WST, WKB[m])
    for c in range(4):
        XHC = [ar.v(i * 256, 128, 512, BF16) for i in range(8)]
        emit_xhat_chunk(C, c * 512, 512, XHC, P.take_bank(), P.take_bank())
        for m in range(8):
            psK = P.ps(P.take_bank(), 128, 512)
            for kt in range(8):
                P.t_mm(psK, V(WKB[m].ap[:, kt * 128:(kt + 1) * 128], WKB[m].regs),
                       XHC[kt], start=(kt == 0), stop=(kt == 7))
            P.v_ts(V(KT[m].ap[:, c * 512:(c + 1) * 512], KT[m].regs), psK,
                   C.l1_bk[:, m:m + 1], None, ALU.add)
    if L1A_PHASES < 2:
        ar.consolidate()
        return
    WVB = [ar.v(20480 + kt * 512, 128, 1024, BF16) for kt in range(8)]
    for kt in range(8):
        P.dmaq(WVB[kt], C.dbf["l1_wv_rows"].ap[kt * 128:(kt + 1) * 128, :])
    for c in range(4):
        XHC = [ar.v(i * 256, 128, 512, BF16) for i in range(8)]
        emit_xhat_chunk(C, c * 512, 512, XHC, P.take_bank(), P.take_bank())
        for tloc in range(4):
            tt = c * 4 + tloc
            for ec in range(2):
                psV = P.ps(P.take_bank(), 128, 512)
                for kt in range(8):
                    P.t_mm(psV,
                           V(XHC[kt].ap[:, tloc * 128:(tloc + 1) * 128], XHC[kt].regs),
                           V(WVB[kt].ap[:, ec * 512:(ec + 1) * 512], WVB[kt].regs),
                           start=(kt == 0), stop=False)
                P.t_mm(psV, C.onesrowb[0:1, :],
                       V(bvrow.ap[:, ec * 512:(ec + 1) * 512], bvrow.regs),
                       start=False, stop=True)
                P.s_copy(V(VT.ap[:, tt * 1024 + ec * 512:tt * 1024 + ec * 512 + 512],
                           VT.regs), psV)
    if L1A_PHASES < 3:
        ar.consolidate()
        return
    WQB = [ar.v(20480 + m * 512, 128, 1024, BF16) for m in range(8)]
    for m in range(8):
        stream_coltile(C, "l1_wq", m, 8, WST, WQB[m])
    WOB = [ar.v(24576 + b * 512, 128, 1024, BF16) for b in range(2)]
    for c in range(L1A_CHUNKS):
        cw = 256
        cs = slice(c * cw, (c + 1) * cw)
        XHC = [ar.v(i * 128, 128, cw, BF16) for i in range(8)]
        QC = [ar.v(1024 + i * 128, 128, cw, BF16) for i in range(8)]
        ATT = [ar.v(18432 + tt * 128, 128, cw, BF16) for tt in range(16)]
        OUTC = [ar.v(i * 128, 128, cw, BF16) for i in range(8)]
        emit_xhat_chunk(C, c * cw, cw, XHC, P.take_bank(), P.take_bank())
        for m in range(8):
            psQ = P.ps(P.take_bank(), 128, cw)
            for kt in range(8):
                P.t_mm(psQ, V(WQB[m].ap[:, kt * 128:(kt + 1) * 128], WQB[m].regs),
                       XHC[kt], start=(kt == 0), stop=(kt == 7))
            P.v_ts(QC[m], psQ, C.l1_bq[:, m:m + 1], None, ALU.add)
        if L1A_PHASES < 4:
            continue
        bS1, bS2 = P.take_bank(), P.take_bank()
        for tt in range(16):
            psS = P.ps(bS1 if tt % 2 == 0 else bS2, 128, cw)
            for kt in range(8):
                P.t_mm(psS, V(KT[kt].ap[:, tt * 128:(tt + 1) * 128], KT[kt].regs),
                       QC[kt], start=(kt == 0), stop=(kt == 7))
            if L1A_THETA == 0:
                P.s_copy(ATT[tt], psS)
                continue
            P.s_act(C.scr0[:, 0:cw], psS, AF.Tanh, scale=0.5)
            P.s_act(C.scr1[:, 0:cw], psS, AF.Tanh)
            if L1A_THETA == 1:
                P.s_copy(ATT[tt], psS)
                continue
            P.v_ts(sc2, psS, 0.0, 0.1, ALU.max, ALU.mult)
            if L1A_THETA == 21:
                P.s_copy(ATT[tt], psS)
                continue
            P.v_stt(sc3, C.scr0[:, 0:cw], 0.1, sc2, ALU.mult, ALU.add)
            if L1A_THETA == 22:
                P.s_copy(ATT[tt], psS)
                continue
            P.v_stt(sc2, C.scr1[:, 0:cw], 0.15, sc3, ALU.mult, ALU.add)
            if L1A_THETA == 2:
                P.s_copy(ATT[tt], psS)
                continue
            P.s_act(ATT[tt], sc2, AF.Exp)
        if L1A_PHASES < 5:
            continue
        bR = P.take_bank()
        psR = V(P.psum[bR][0:1, 0:cw], P.psreg[bR])
        for tt in range(16):
            P.t_mm(psR, C.onescolb, ATT[tt], start=(tt == 0), stop=(tt == 15))
        crow = V(C.rowAB.ap[0:1, 2 * T:2 * T + cw], C.rowAB.regs)
        P.v_recip(crow, psR)
        psC = P.ps(7, 128, cw)
        P.t_mm(psC, C.onesrow[0:1, :], crow, True, True)
        if L1A_PHASES < 6:
            continue
        for dh in range(2):
            banks = [P.take_bank() for _ in range(4)]
            for mi in range(4):
                m = dh * 4 + mi
                psO = P.ps(banks[mi], 128, cw)
                for tt in range(16):
                    P.t_mm(psO,
                           V(VT.ap[:, tt * 1024 + m * 128:tt * 1024 + (m + 1) * 128],
                             VT.regs),
                           ATT[tt], start=(tt == 0), stop=(tt == 15))
            for mi in range(4):
                m = dh * 4 + mi
                psO = P.ps(banks[mi], 128, cw)
                P.s_copy(C.scr0[:, 0:cw], psO)
                P.v_tt(OUTC[m], C.scr0[:, 0:cw], psC, ALU.mult)
        if L1A_PHASES < 7:
            continue
        for m in range(8):
            wb = WOB[m % 2]
            stream_coltile(C, "l1_wo", m, 8, WST, wb)
            psY = P.ps(P.take_bank(), 128, cw)
            for kt in range(8):
                P.t_mm(psY, V(wb.ap[:, kt * 128:(kt + 1) * 128], wb.regs),
                       OUTC[kt], start=(kt == 0), stop=(kt == 7))
            P.v_stt(C.res[m][:, cs], psY, C.l1_bo[:, m:m + 1], C.res[m][:, cs],
                    ALU.add, ALU.add)
    ar.consolidate()


def _emit_router(C, LG, tokbase, topk):
    P = C.P
    ar = C.ar
    LGt = ar.v(tokbase, 128, 8, F32)
    m1 = ar.v(tokbase + 8, 128, 1, F32)
    m1n = ar.v(tokbase + 9, 128, 1, F32)
    sm = ar.v(tokbase + 10, 128, 1, F32)
    w1 = ar.v(tokbase + 11, 128, 1, F32)
    m2 = ar.v(tokbase + 12, 128, 1, F32)
    ex = ar.v(tokbase + 16, 128, 8, F32)
    msk1 = ar.v(tokbase + 24, 128, 8, F32)
    msk2 = ar.v(tokbase + 32, 128, 8, F32)
    g2 = ar.v(tokbase + 40, 128, 8, F32)
    cwt = ar.v(tokbase + 48, 128, 8, F32)
    bT1, bT2 = P.take_bank(), P.take_bank()
    for tt in range(16):
        psL = P.ps(bT1, 128, 8)
        P.t_transpose(psL, V(LG.ap[:, tt * 128:(tt + 1) * 128], LG.regs),
                      V(C.ident.ap[0:8, 0:8], C.ident.regs))
        P.s_copy(LGt, psL)
        P.v_red(m1, LGt, ALU.max)
        P.v_red(m1n, LGt, ALU.max, negate=True)
        P.s_act(ex, LGt, AF.Exp, bias=m1n)
        P.v_red(sm, ex, ALU.add)
        P.v_recip(w1, sm)
        P.v_ts(msk1, LGt, m1, None, ALU.is_equal)
        if topk == 1:
            P.v_ts(cwt, msk1, w1, None, ALU.mult)
        else:
            P.v_ts(ex, ex, w1, None, ALU.mult)
            P.v_stt(g2, msk1, -1e30, LGt, ALU.mult, ALU.add)
            P.v_red(m2, g2, ALU.max)
            P.v_ts(msk2, g2, m2, None, ALU.is_equal)
            P.v_tt(cwt, msk1, ex, ALU.mult)
            P.v_tt(msk2, msk2, ex, ALU.mult)
            P.v_tt(cwt, cwt, msk2, ALU.add)
        psB = P.ps(bT2, 8, 128)
        P.t_transpose(psB, cwt, C.ident)
        P.s_copy(V(C.LGb.ap[:, tt * 128:(tt + 1) * 128], C.LGb.regs), psB)


def _emit_moe(C, wgname, w1name, w2name, b1col, b2col, topk):
    P = C.P
    ar = C.ar
    emit_ln(C)
    XH = [ar.v(i * 1024, 128, T, BF16) for i in range(8)]
    for c in range(4):
        emit_xhat_chunk(C, c * 512, 512,
                        [V(XH[i].ap[:, c * 512:(c + 1) * 512], XH[i].regs)
                         for i in range(8)],
                        P.take_bank(), P.take_bank())
    LG = ar.v(24576, 8, T, F32)
    C.LGb = ar.v(24576, 8, T, BF16)
    WGB = ar.v(26688, 128, 64, BF16)
    wgstg = ar.v(26624, 128, 64, F32)
    P.dma("Q", wgstg, C.d[wgname])
    P.v_copy(WGB, wgstg)
    bG = P.take_bank()
    for c in range(4):
        cs = slice(c * 512, (c + 1) * 512)
        psG = P.ps(bG, 8, 512)
        for kt in range(8):
            P.t_mm(psG, V(WGB.ap[:, kt * 8:(kt + 1) * 8], WGB.regs),
                   V(XH[kt].ap[:, cs], XH[kt].regs), start=(kt == 0), stop=(kt == 7))
        P.s_copy(V(LG.ap[:, cs], LG.regs), psG)
    _emit_router(C, LG, 26752, topk)
    MErowB = V(C.rowABb[0:1, 4 * T:4 * T + 512], C.rowAB.regs)
    WBH = [ar.v(25600, 128, 2048, BF16), ar.v(26624, 128, 2048, BF16)]
    H = [ar.v(8192 + i * 1024, 128, T, BF16) for i in range(16)]
    save_rr = P.rr_banks
    P.rr_banks = [0, 1, 2, 3]
    for e in range(8):
        for ch in range(4):
            chs = slice(ch * 512, (ch + 1) * 512)
            P.dma("G", MErowB, V(C.LGb.ap[e:e + 1, chs], C.LGb.regs))
            psM = P.ps(4 + ch, 128, 512)
            P.t_mm(psM, C.onesrowb[0:1, :], MErowB, True, True)
        for hf in range(2):
            for mi in range(16):
                mt = hf * 16 + mi
                pr = mi // 2
                wb = WBH[pr % 2]
                if mi % 2 == 0:
                    P.dmaq(V(wb.ap.rearrange("p (a x) -> p a x", a=2), wb.regs),
                           C.dbf[w1name].ap[e, mt:mt + 2]
                           .rearrange("a p k j -> p a (k j)"))
                wsl = (mi % 2) * 1024
                for ch in range(4):
                    chs = slice(ch * 512, (ch + 1) * 512)
                    psH = P.ps(P.take_bank(), 128, 512)
                    for kt in range(8):
                        P.t_mm(psH, V(wb.ap[:, wsl + kt * 128:wsl + (kt + 1) * 128],
                                      wb.regs),
                               V(XH[kt].ap[:, chs], XH[kt].regs),
                               start=(kt == 0), stop=(kt == 7))
                    P.s_act(V(H[mi].ap[:, chs], H[mi].regs), psH, AF.Gelu,
                            bias=b1col[:, e * 32 + mt:e * 32 + mt + 1])
            for dm in range(8):
                wb = WBH[dm % 2]
                P.dmaq(wb, C.dbf[w2name].ap[e, dm][:, hf * 16:(hf + 1) * 16, :]
                       .rearrange("p k j -> p (k j)"))
                banks = [0, 1, 2, 3]
                for ch in range(4):
                    chs = slice(ch * 512, (ch + 1) * 512)
                    psY = P.ps(banks[ch], 128, 512)
                    for kt in range(16):
                        P.t_mm(psY, V(wb.ap[:, kt * 128:(kt + 1) * 128], wb.regs),
                               V(H[kt].ap[:, chs], H[kt].regs),
                               start=(kt == 0), stop=(kt == 15))
                for ch in range(4):
                    chs = slice(ch * 512, (ch + 1) * 512)
                    psY = P.ps(banks[ch], 128, 512)
                    if hf == 1:
                        P.v_ts(C.scr0, psY, b2col[:, e * 8 + dm:e * 8 + dm + 1],
                               None, ALU.add)
                    else:
                        P.s_copy(C.scr0, psY)
                    psM = P.ps(4 + ch, 128, 512)
                    P.v_tt(C.scr1, C.scr0, psM, ALU.mult)
                    P.v_tt(C.res[dm][:, chs], C.res[dm][:, chs], C.scr1, ALU.add)
    P.rr_banks = save_rr
    ar.consolidate()


def emit_l1f(C):
    _emit_moe(C, "l1f_wg", "l1f_w1", "l1f_w2", C.l1f_b1, C.l1f_b2, 1)


def emit_l2f(C):
    _emit_moe(C, "l2f_wg", "l2f_w1", "l2f_w2", C.l2f_b1, C.l2f_b2, 2)


def emit_l2a(C):
    P = C.P
    ar = C.ar
    emit_ln(C)
    QF = [ar.v(2048 + i * 256, 128, 512, BF16) for i in range(8)]
    KF = [ar.v(4096 + i * 256, 128, 512, BF16) for i in range(8)]
    VF = [ar.v(6144 + i * 512, 128, 512, F32) for i in range(8)]
    QKV = [ar.v(10240 + i * 256, 128, 512, BF16) for i in range(8)]
    WSTG = ar.v(12288, 128, 1024, F32)
    WB2 = [ar.v(13312 + b * 512, 128, 1024, BF16) for b in range(2)]
    hsum = ar.v(14336, 128, 128, BF16)
    P.dma("Q", hsum, C.d["c_hsum"])
    for c in range(4):
        cs = slice(c * 512, (c + 1) * 512)
        XHC = [ar.v(i * 256, 128, 512, BF16) for i in range(8)]
        emit_xhat_chunk(C, c * 512, 512, XHC, P.take_bank(), P.take_bank())
        for m in range(8):
            wb = WB2[m % 2]
            stream_coltile(C, "l2_wq", m, 8, WSTG, wb)
            psQ = P.ps(P.take_bank(), 128, 512)
            for kt in range(8):
                P.t_mm(psQ, V(wb.ap[:, kt * 128:(kt + 1) * 128], wb.regs),
                       XHC[kt], start=(kt == 0), stop=(kt == 7))
            P.s_act(C.scr0, psQ, AF.Square, bias=C.l2_bq[:, m:m + 1])
            P.s_act(QF[m], C.scr0, AF.Exp, scale=-0.5)
            wb = WB2[m % 2]
            stream_coltile(C, "l2_wk", m, 8, WSTG, wb)
            psK = P.ps(P.take_bank(), 128, 512)
            for kt in range(8):
                P.t_mm(psK, V(wb.ap[:, kt * 128:(kt + 1) * 128], wb.regs),
                       XHC[kt], start=(kt == 0), stop=(kt == 7))
            P.s_act(C.scr0, psK, AF.Square, bias=C.l2_bk[:, m:m + 1])
            P.s_act(KF[m], C.scr0, AF.Exp, scale=-0.5)
            wb = WB2[m % 2]
            stream_coltile(C, "l2_wv", m, 8, WSTG, wb)
            psV = P.ps(P.take_bank(), 128, 512)
            for kt in range(8):
                P.t_mm(psV, V(wb.ap[:, kt * 128:(kt + 1) * 128], wb.regs),
                       XHC[kt], start=(kt == 0), stop=(kt == 7))
            P.v_ts(VF[m], psV, C.l2_bv[:, m:m + 1], None, ALU.add)
        for m in range(8):
            P.v_tt(QF[m], QF[m], KF[m], ALU.mult)
            psCB = P.ps(P.take_bank(), 128, 512)
            P.t_mm(psCB, hsum, QF[m], True, True)
            P.v_tt(QKV[m], VF[m], psCB, ALU.mult)
        for m in range(8):
            wb = WB2[m % 2]
            stream_coltile(C, "l2_wo", m, 8, WSTG, wb)
            psY = P.ps(P.take_bank(), 128, 512)
            for kt in range(8):
                P.t_mm(psY, V(wb.ap[:, kt * 128:(kt + 1) * 128], wb.regs),
                       QKV[kt], start=(kt == 0), stop=(kt == 7))
            P.v_stt(C.res[m][:, cs], psY, C.l2_bo[:, m:m + 1], C.res[m][:, cs],
                    ALU.add, ALU.add)
    ar.consolidate()


def emit_l3a(C):
    P = C.P
    ar = C.ar
    emit_ln(C)
    save_rr = P.rr_banks
    P.rr_banks = [0, 1, 2, 3, 4, 5]
    CK = 256
    halo = ar.v(0, 128, 48, F32)
    xcbuf = ar.v(48, 128, 260, F32)
    xc_s = [ar.v(308 + i * 128, 128, CK, BF16) for i in range(16)]
    z_s = [ar.v(2356 + i * 128, 128, CK, BF16) for i in range(16)]
    u = [ar.v(4404 + i * 128, 128, CK, BF16) for i in range(16)]
    delta = [ar.v(6452 + i * 256, 128, CK, F32) for i in range(16)]
    w = [ar.v(10548 + i * 256, 128, CK, F32) for i in range(16)]
    y = [ar.v(14644 + i * 256, 128, CK, F32) for i in range(16)]
    yfin = [ar.v(18740 + i * 128, 128, CK, BF16) for i in range(16)]
    hstate = ar.v(20788, 128, 256, F32)
    dsB = ar.v(21044, 32, CK, F32)
    dsb16 = ar.v(21300, 16, CK, BF16)
    Brow = ar.v(21428, 1, CK, F32)
    dtW = ar.v(21684, 16, T, BF16)
    dAscr = [ar.v(22708, 128, CK, F32), ar.v(308, 128, CK, F32)]
    dBscr = [ar.v(22964, 128, CK, F32), ar.v(564, 128, CK, F32)]
    hscr = [ar.v(23220, 128, CK, F32), ar.v(820, 128, CK, F32)]
    Brow2 = [ar.v(21428, 1, CK, F32), ar.v(1076, 1, CK, F32)]
    XHC = [ar.v(23476 + i * 128, 128, CK, BF16) for i in range(8)]
    WSTG = ar.v(24500, 128, 512, F32)
    XPH = ar.v(25012, 128, 512, BF16)
    BUF2 = [ar.v(25268, 128, 2048, BF16),
            ar.v(26292, 128, 2048, BF16)]
    dtstg = ar.v(6452, 16, T, F32)
    P.dma("Q", dtstg, C.d["l3_dt"])
    P.v_copy(dtW, dtstg)
    P.memset(halo, 0.0)
    P.memset(hstate, 0.0)
    for ck in range(L3A_CKS):
        cs = slice(ck * CK, (ck + 1) * CK)
        emit_xhat_chunk(C, ck * CK, CK, XHC, P.take_bank(), P.take_bank())
        for mt in range(32):
            pr = mt // 2
            wb = BUF2[pr % 2]
            if mt % 2 == 0:
                P.dmaq(V(wb.ap.rearrange("p (a x) -> p a x", a=2), wb.regs),
                       C.dbf["l3_inproj"].ap[mt:mt + 2]
                       .rearrange("a p k j -> p a (k j)"))
            wsl = (mt % 2) * 1024
            psX = P.ps(P.take_bank(), 128, CK)
            for kt in range(8):
                P.t_mm(psX, V(wb.ap[:, wsl + kt * 128:wsl + (kt + 1) * 128], wb.regs),
                       XHC[kt], start=(kt == 0), stop=(kt == 7))
            if mt < 16:
                P.v_copy(V(xcbuf.ap[:, 0:3], xcbuf.regs),
                         V(halo.ap[:, mt * 3:mt * 3 + 3], halo.regs))
                P.v_ts(V(xcbuf.ap[:, 3:259], xcbuf.regs), psX,
                       C.l3_inproj_b[:, mt:mt + 1], None, ALU.add)
                P.v_copy(V(halo.ap[:, mt * 3:mt * 3 + 3], halo.regs),
                         V(xcbuf.ap[:, 256:259], xcbuf.regs))
                P.v_ts(dAscr[0], V(xcbuf.ap[:, 0:256], xcbuf.regs),
                       C.l3_convw[:, mt * 4:mt * 4 + 1], None, ALU.mult)
                for kk in range(1, 4):
                    P.v_stt(dAscr[0], V(xcbuf.ap[:, kk:kk + 256], xcbuf.regs),
                            C.l3_convw[:, mt * 4 + kk:mt * 4 + kk + 1],
                            dAscr[0], ALU.mult, ALU.add)
                P.s_act(xc_s[mt], dAscr[0], AF.Silu, bias=C.l3_convb[:, mt:mt + 1])
            else:
                P.s_act(z_s[mt - 16], psX, AF.Silu,
                        bias=C.l3_inproj_b[:, mt:mt + 1])
        P.dma("Q", WSTG, C.d["l3_xp_head"].rearrange("p k j -> p (k j)"))
        P.v_copy(XPH, WSTG, eng="S")
        psH = P.ps(P.take_bank(), 32, CK)
        for kt in range(16):
            P.t_mm(psH, V(XPH.ap[:, kt * 32:(kt + 1) * 32], XPH.regs),
                   xc_s[kt], start=(kt == 0), stop=(kt == 15))
        P.v_ts(dsB, psH, C.l3_xpb_head[0:32, :], None, ALU.add)
        P.v_copy(dsb16, V(dsB.ap[0:16, :], dsB.regs))
        for ut in range(16):
            src_w = C.dbf["l3_xp_u"].ap[ut].rearrange("p k j -> p (k j)")
            wb = BUF2[ut % 2]
            P.dmaq(wb, src_w)
            psU = P.ps(P.take_bank(), 128, CK)
            for kt in range(16):
                P.t_mm(psU, V(wb.ap[:, kt * 128:(kt + 1) * 128], wb.regs),
                       xc_s[kt], start=(kt == 0), stop=(kt == 15))
            P.v_ts(u[ut], psU, C.l3_xpb_u[:, ut:ut + 1], None, ALU.add)
        for ct in range(16):
            psD = P.ps(P.take_bank(), 128, CK)
            P.t_mm(psD, V(dtW.ap[:, ct * 128:(ct + 1) * 128], dtW.regs),
                   dsb16, True, True)
            pp = ct % 2
            P.s_act(dAscr[pp], psD, AF.Abs, bias=C.l3_dtb[:, ct:ct + 1])
            P.s_act(dBscr[pp], dAscr[pp], AF.Exp, scale=-1.0)
            P.s_act(dBscr[pp], dBscr[pp], AF.Ln, bias=C.onescol)
            P.s_act(hscr[pp], psD, AF.Relu, bias=C.l3_dtb[:, ct:ct + 1])
            P.v_tt(delta[ct], dBscr[pp], hscr[pp], ALU.add)
            P.v_tt(w[ct], delta[ct], u[ct], ALU.mult)
        P.dma("G", Brow2[0], V(dsB.ap[16:17, :], dsB.regs))
        P.t_mm(P.ps(6, 128, CK), C.onesrow[0:1, :], Brow2[0], True, True)
        for s in range(L3A_SMAX):
            if s + 1 < L3A_SMAX:
                P.dma("G", Brow2[(s + 1) % 2],
                      V(dsB.ap[17 + s:18 + s, :], dsB.regs))
                P.t_mm(P.ps(6 + (s + 1) % 2, 128, CK), C.onesrow[0:1, :],
                       Brow2[(s + 1) % 2], True, True)
            psB = P.ps(6 + s % 2, 128, CK)
            for ct in range(16):
                pp = ct % 2
                P.s_act(dAscr[pp], delta[ct], AF.Exp, scale=-(s + 1.0))
                P.v_tt(dBscr[pp], w[ct], psB, ALU.mult)
                hcol = V(hstate.ap[:, ct * 16 + s:ct * 16 + s + 1], hstate.regs)
                P.v_scan(hscr[pp], dAscr[pp], dBscr[pp], hcol)
                if s == 0:
                    P.v_copy(y[ct], hscr[pp], eng="G")
                else:
                    P.v_tt(y[ct], y[ct], hscr[pp], ALU.add, eng="G")
                P.v_copy(hcol, V(hscr[pp].ap[:, CK - 1:CK], hscr[pp].regs), eng="G")
        for ct in range(16):
            P.v_stt(y[ct], u[ct], C.l3_D[:, ct:ct + 1], y[ct], ALU.mult, ALU.add)
            P.v_tt(yfin[ct], y[ct], z_s[ct], ALU.mult)
        for m in range(8):
            wb = BUF2[m % 2]
            P.dmaq(wb, C.dbf["l3_outproj"].ap[m].rearrange("p k j -> p (k j)"))
            psO = P.ps(P.take_bank(), 128, CK)
            for kt in range(16):
                P.t_mm(psO, V(wb.ap[:, kt * 128:(kt + 1) * 128], wb.regs),
                       yfin[kt], start=(kt == 0), stop=(kt == 15))
            P.v_stt(C.res[m][:, cs], psO, C.l3_outproj_b[:, m:m + 1],
                    C.res[m][:, cs], ALU.add, ALU.add)
    P.rr_banks = save_rr
    ar.consolidate()


def emit_l3f(C):
    P = C.P
    ar = C.ar
    nc = P.nc
    emit_ln(C)
    spill = V(nc.dram_tensor("spill_res", [D, T], F32).ap(), Reg(_Solo(), 0, 1))
    spill2 = V(nc.dram_tensor("spill_out", [D, T], F32).ap(), Reg(_Solo(), 0, 1))
    XH = [ar.v(i * 1024, 128, T, BF16) for i in range(8)]
    for c in range(4):
        emit_xhat_chunk(C, c * 512, 512,
                        [V(XH[i].ap[:, c * 512:(c + 1) * 512], XH[i].regs)
                         for i in range(8)],
                        P.take_bank(), P.take_bank())
    for i in range(8):
        P.dma("Q", spill[i * 128:(i + 1) * 128, :], C.res[i])
    hp = [C.resA.v(mi * 512, 128, 1024, BF16) for mi in range(32)]
    g1 = [ar.v(8192 + mi * 512, 128, 1024, BF16) for mi in range(32)]
    WBF = [ar.v(24576 + b * 512, 128, 1024, BF16) for b in range(2)]
    W2BF = [ar.v(25600 + b * 512, 128, 1024, BF16) for b in range(2)]
    HH = [ar.v(24576 + b * 1024, 128, 2048, BF16) for b in range(2)]
    for th in range(2):
        hs = [slice(th * 1024 + cc * 512, th * 1024 + cc * 512 + 512)
              for cc in range(2)]
        for mt in range(32):
            wb1 = WBF[mt % 2]
            wb2 = W2BF[mt % 2]
            P.dmaq(wb1, C.dbf["l3f_w1"].ap[mt].rearrange("p k j -> p (k j)"))
            P.dmaq(wb2, C.dbf["l3f_w2"].ap[mt].rearrange("p k j -> p (k j)"))
            for cc in range(2):
                psH = P.ps(P.take_bank(), 128, 512)
                psH2 = P.ps(P.take_bank(), 128, 512)
                for kt in range(8):
                    P.t_mm(psH, V(wb1.ap[:, kt * 128:(kt + 1) * 128], wb1.regs),
                           V(XH[kt].ap[:, hs[cc]], XH[kt].regs),
                           start=(kt == 0), stop=(kt == 7))
                for kt in range(8):
                    P.t_mm(psH2, V(wb2.ap[:, kt * 128:(kt + 1) * 128], wb2.regs),
                           V(XH[kt].ap[:, hs[cc]], XH[kt].regs),
                           start=(kt == 0), stop=(kt == 7))
                P.s_act(C.scr0, psH, AF.Silu, scale=C.l3f_s1[:, mt:mt + 1])
                P.v_ts(C.scr1, psH2, C.l3f_s2[:, mt:mt + 1], None, ALU.mult)
                P.v_tt(V(hp[mt].ap[:, cc * 512:(cc + 1) * 512], hp[mt].regs),
                       C.scr0, C.scr1, ALU.mult)
        for mt in range(32):
            bk0, bk1 = P.take_bank(), P.take_bank()
            for pc in range(2):
                wb = HH[pc % 2]
                P.dmaq(wb, C.dbf["l3f_h1"].ap[mt].rearrange("p k j -> p (k j)")
                       [:, pc * 2048:(pc + 1) * 2048])
                for cc in range(2):
                    psG = P.ps(bk0 if cc == 0 else bk1, 128, 512)
                    for kl in range(16):
                        kt = pc * 16 + kl
                        P.t_mm(psG, V(wb.ap[:, kl * 128:(kl + 1) * 128], wb.regs),
                               V(hp[kt].ap[:, cc * 512:(cc + 1) * 512], hp[kt].regs),
                               start=(kt == 0), stop=(kt == 31))
            for cc in range(2):
                psG = P.ps(bk0 if cc == 0 else bk1, 128, 512)
                P.s_act(V(g1[mt].ap[:, cc * 512:(cc + 1) * 512], g1[mt].regs),
                        psG, AF.Gelu, scale=C.l3f_sh1[:, mt:mt + 1])
        for mt in range(32):
            bk0, bk1 = P.take_bank(), P.take_bank()
            for pc in range(2):
                wb = HH[pc % 2]
                P.dmaq(wb, C.dbf["l3f_h2"].ap[mt].rearrange("p k j -> p (k j)")
                       [:, pc * 2048:(pc + 1) * 2048])
                for cc in range(2):
                    psG = P.ps(bk0 if cc == 0 else bk1, 128, 512)
                    for kl in range(16):
                        kt = pc * 16 + kl
                        P.t_mm(psG, V(wb.ap[:, kl * 128:(kl + 1) * 128], wb.regs),
                               V(g1[kt].ap[:, cc * 512:(cc + 1) * 512], g1[kt].regs),
                               start=(kt == 0), stop=(kt == 31))
            for cc in range(2):
                psG = P.ps(bk0 if cc == 0 else bk1, 128, 512)
                P.s_act(V(hp[mt].ap[:, cc * 512:(cc + 1) * 512], hp[mt].regs),
                        psG, AF.Gelu, scale=C.l3f_sh2[:, mt:mt + 1])
        for m in range(8):
            bk0, bk1 = P.take_bank(), P.take_bank()
            for pc in range(2):
                wb = HH[pc % 2]
                P.dmaq(wb, C.dbf["l3f_w3"].ap[m].rearrange("p k j -> p (k j)")
                       [:, pc * 2048:(pc + 1) * 2048])
                for cc in range(2):
                    psY = P.ps(bk0 if cc == 0 else bk1, 128, 512)
                    for kl in range(16):
                        kt = pc * 16 + kl
                        P.t_mm(psY, V(wb.ap[:, kl * 128:(kl + 1) * 128], wb.regs),
                               V(hp[kt].ap[:, cc * 512:(cc + 1) * 512], hp[kt].regs),
                               start=(kt == 0), stop=(kt == 31))
            for cc in range(2):
                psY = P.ps(bk0 if cc == 0 else bk1, 128, 512)
                P.v_ts(C.scr0, psY, C.l3f_s3[:, m:m + 1], None, ALU.mult)
                P.dma("G", C.scr1, spill[m * 128:(m + 1) * 128, hs[cc]])
                P.v_tt(C.scr0, C.scr0, C.scr1, ALU.add)
                P.dma("G", spill2[m * 128:(m + 1) * 128, hs[cc]], C.scr0)
    for i in range(8):
        P.dma("Q", C.res[i], spill2[i * 128:(i + 1) * 128, :])
    ar.consolidate()
    C.resA.consolidate()


EMITTERS = {"l0a": emit_l0a, "l0f": emit_l0f, "l1a": emit_l1a,
            "l1f": emit_l1f, "l2a": emit_l2a, "l2f": emit_l2f,
            "l3a": emit_l3a, "l3f": emit_l3f}

_PROG_CACHE = {}


def build_program(marrs, blocks):
    nc = bass.Bass("TRN2", target_bir_lowering=False)
    P = Prog(nc)
    C = setup_ctx(nc, P, marrs, blocks)
    load_block_statics(C, blocks)
    emit_convert(C, blocks)
    for i in range(8):
        P.dma("Q", C.res[i], C.xT[i * 128:(i + 1) * 128, :])
    mm_marks = [("start", 0)]
    for b in blocks:
        EMITTERS[b](C)
        C.resA.consolidate()
        C.ar.consolidate()
        mm_marks.append((b, P.mm_count))
    build_program.mm_marks = mm_marks
    for i in range(8):
        P.dma("Q", C.yT[i * 128:(i + 1) * 128, :], C.res[i])
    for qn in ("Q", "G", "S", "V"):
        for sl in range(P.NSLOT):
            key = f"{qn}#{sl}"
            P.raw_wait("Q", key, P.cnt[key])
    return P.build(), C.innames


def _get_prog(marrs, blocks):
    key = tuple(blocks)
    if key not in _PROG_CACHE:
        _PROG_CACHE[key] = build_program(marrs, blocks)
    return _PROG_CACHE[key]


def run_blocks(marrs, xts, blocks):
    nc, innames = _get_prog(marrs, blocks)
    in_maps = []
    for c in range(NCORE):
        d = {k: marrs[k] for k in innames if k != "xT"}
        d["xT"] = xts[c]
        in_maps.append(d)
    res = run_bass_kernel_spmd(nc, in_maps, core_ids=list(range(NCORE)))
    return [np.ascontiguousarray(np.asarray(res.results[c]["yT"]))
            for c in range(NCORE)]


def kernel(**inputs):
    m, xts = prep_inputs(inputs)
    groups = [ALL_BLOCKS] if FUSED else [[b] for b in ALL_BLOCKS]
    cur = xts
    for blocks in groups:
        cur = run_blocks(m, cur, blocks)
    out = np.stack([c.T for c in cur])
    return np.ascontiguousarray(out.astype(np.float32))


# version 07
# speedup vs baseline: 1.1997x; 1.0245x over previous
import numpy as np
import ml_dtypes
import concourse.bass as bass
import concourse.mybir as mybir
from concourse.bass_utils import run_bass_kernel_spmd

F32 = mybir.dt.float32
BF16 = mybir.dt.bfloat16
AF = mybir.ActivationFunctionType
ALU = mybir.AluOpType
AX = mybir.AxisListType.X

T = 2048
D = 1024
NCORE = 8
NPBF = ml_dtypes.bfloat16

ALL_BLOCKS = ["l0a", "l0f", "l1a", "l1f", "l2a", "l2f", "l3a", "l3f"]
L1A_PHASES = 7
L1A_CHUNKS = 8
L1A_THETA = 3
L3A_CKS = 8
L3A_SMAX = 16
FUSED = True
ARENA_N = 27648

ENGS = ("T", "V", "S", "Q", "G")


def _coltile(w):
    K, M = w.shape
    assert K % 128 == 0 and M % 128 == 0, (K, M)
    t = w.reshape(K // 128, 128, M // 128, 128)
    return np.ascontiguousarray(t.transpose(2, 1, 0, 3))


def _cols(v):
    n = v.shape[0] // 128
    return np.ascontiguousarray(v.reshape(n, 128).T)


def prep_inputs(inputs):
    f = lambda a: np.ascontiguousarray(np.asarray(a, np.float32))
    bf = lambda a: np.ascontiguousarray(np.asarray(a, NPBF))
    I = inputs
    m = {}
    m["c_ident"] = np.eye(128, dtype=np.float32)
    m["c_identb"] = bf(np.eye(128))
    m["c_onescol"] = np.ones((128, 1), np.float32)
    m["c_onescolb"] = bf(np.ones((128, 1)))
    m["c_onesrow"] = np.ones((96, 128), np.float32)
    m["c_onesrowb"] = bf(np.ones((96, 128)))
    hd = 64
    inv_freq = 1.0 / (10000.0 ** (np.arange(0, hd, 2, dtype=np.float64) / hd))
    ang = np.arange(T, dtype=np.float64)[None, :] * inv_freq[:, None]
    Cf = np.empty((64, T)); Sf = np.empty((64, T))
    Cf[0::2] = np.cos(ang); Cf[1::2] = np.cos(ang)
    Sf[0::2] = -np.sin(ang); Sf[1::2] = np.sin(ang)
    m["c_ropeC"] = bf(np.tile(Cf, (2, 1)))
    m["c_ropeS"] = bf(np.tile(Sf, (2, 1)))
    P = np.zeros((128, 128))
    for dd in range(128):
        P[dd + 1 if dd % 2 == 0 else dd - 1, dd] = 1.0
    m["c_swap"] = bf(P)
    st = np.zeros((128, 896))
    for i in range(128):
        st[i, i + 384:] = 1.0
    m["c_stair"] = bf(st)
    hs = np.zeros((128, 128)); hs[0:64, 0:64] = 1.0; hs[64:128, 64:128] = 1.0
    m["c_hsum"] = bf(hs)
    ep = np.zeros((8, 128), np.float32)
    for e in range(8):
        ep[e, e * 16:(e + 1) * 16] = 1.0
    m["c_erpat"] = ep

    for nm in ("wq", "wk", "wo"):
        m[f"l0_{nm}"] = _coltile(f(I[f"L0_attn_{nm}"]))
    m["l0_wv_rows"] = f(I["L0_attn_wv"])
    m["l0f_wg"] = np.ascontiguousarray(
        f(I["L0_ffn_Wg"]).reshape(8, 128, 8).transpose(1, 0, 2).reshape(128, 64))
    m["l0f_a"] = np.ascontiguousarray(
        f(I["L0_ffn_A"]).transpose(1, 0, 2).reshape(8, 128, 128)
        .transpose(1, 0, 2).reshape(128, 1024))
    m["l0f_b"] = f(I["L0_ffn_B"]).reshape(128, D)
    for nm in ("Wq", "Wk", "Wo"):
        m[f"l1_{nm.lower()}"] = _coltile(np.ascontiguousarray(f(I[f"L1_attn_{nm}"]).T))
    m["l1_wv_rows"] = np.ascontiguousarray(f(I["L1_attn_Wv"]).T)
    m["l1_bq"] = _cols(f(I["L1_attn_bq"]))
    m["l1_bk"] = _cols(f(I["L1_attn_bk"]))
    m["l1_bv_row"] = f(I["L1_attn_bv"]).reshape(1, D)
    m["l1_bo"] = _cols(f(I["L1_attn_bo"]))
    m["l1f_wg"] = np.ascontiguousarray(
        f(I["L1_ffn_gate_w"]).reshape(8, 128, 8).transpose(1, 0, 2).reshape(128, 64))
    m["l1f_w1"] = np.stack([_coltile(f(I["L1_ffn_w1"][e])) for e in range(8)])
    m["l1f_w2"] = np.stack([_coltile(f(I["L1_ffn_w2"][e])) for e in range(8)])
    m["l1f_b1"] = np.stack([_cols(f(I["L1_ffn_b1"][e])) for e in range(8)])
    m["l1f_b2"] = np.stack([_cols(f(I["L1_ffn_b2"][e])) for e in range(8)])
    for nm in ("Wq", "Wk", "Wv", "Wo"):
        m[f"l2_{nm.lower()}"] = _coltile(f(I[f"L2_attn_{nm}"]))
    for nm in ("bq", "bk", "bv", "bo"):
        m[f"l2_{nm}"] = _cols(f(I[f"L2_attn_{nm}"]))
    m["l2f_wg"] = np.ascontiguousarray(
        f(I["L2_ffn_Wg"]).T.reshape(8, 128, 8).transpose(1, 0, 2).reshape(128, 64))
    m["l2f_w1"] = np.stack([_coltile(f(I["L2_ffn_w1"][e])) for e in range(8)])
    m["l2f_w2"] = np.stack([_coltile(f(I["L2_ffn_w2"][e])) for e in range(8)])
    m["l2f_b1"] = np.stack([_cols(f(I["L2_ffn_b1"][e])) for e in range(8)])
    m["l2f_b2"] = np.stack([_cols(f(I["L2_ffn_b2"][e])) for e in range(8)])
    m["l3_inproj"] = _coltile(f(I["L3_attn_in_proj_w"]))
    m["l3_inproj_b"] = _cols(f(I["L3_attn_in_proj_b"]))
    m["l3_convw"] = np.ascontiguousarray(
        f(I["L3_attn_conv_w"]).reshape(16, 128, 4).transpose(1, 0, 2).reshape(128, 64))
    m["l3_convb"] = _cols(f(I["L3_attn_conv_b"]))
    xp = f(I["L3_attn_x_proj_w"])
    m["l3_xp_head"] = np.ascontiguousarray(xp[:, 0:32].reshape(16, 128, 32).transpose(1, 0, 2))
    m["l3_xp_u"] = _coltile(np.ascontiguousarray(xp[:, 32:2080]))
    xb = f(I["L3_attn_x_proj_b"])
    m["l3_xpb_head"] = np.ascontiguousarray(xb[0:32].reshape(32, 1))
    m["l3_xpb_u"] = _cols(xb[32:2080])
    m["l3_dt"] = f(I["L3_attn_dt_proj_w"])
    m["l3_dtb"] = _cols(f(I["L3_attn_dt_proj_b"]))
    m["l3_D"] = _cols(f(I["L3_attn_D"]))
    m["l3_outproj"] = _coltile(f(I["L3_attn_out_proj_w"]))
    m["l3_outproj_b"] = _cols(f(I["L3_attn_out_proj_b"]))
    for nm in ("w1", "w2", "h1", "h2", "w3"):
        m[f"l3f_{nm}"] = _coltile(np.ascontiguousarray(f(I[f"L3_ffn_{nm}_w"]).T))
    m["l3f_s1"] = _cols(f(I["L3_ffn_w1_s"]))
    m["l3f_s2"] = _cols(f(I["L3_ffn_w2_s"]))
    m["l3f_sh1"] = _cols(f(I["L3_ffn_h1_s"]))
    m["l3f_sh2"] = _cols(f(I["L3_ffn_h2_s"]))
    m["l3f_s3"] = _cols(f(I["L3_ffn_w3_s"]))

    x = f(I["L0_attn_x"])
    xts = [np.ascontiguousarray(x[c].T) for c in range(NCORE)]
    return m, xts


class Reg:
    __slots__ = ("arena", "lo", "hi", "writers", "readers", "excl_read")

    def __init__(self, arena, lo, hi, excl_read=False):
        self.arena, self.lo, self.hi = arena, lo, hi
        self.writers = {}
        self.readers = {}
        self.excl_read = excl_read


class V:
    __slots__ = ("ap", "regs")

    def __init__(self, ap, regs):
        self.ap = ap
        self.regs = regs if isinstance(regs, list) else [regs]

    def __getitem__(self, idx):
        return V(self.ap[idx], self.regs)


class Arena:
    def __init__(self, P, name, f32_elems):
        self.P = P
        self.name = name
        self.tf = P.nc.alloc_sbuf_tensor(name, [128, f32_elems], F32)
        self.tb = self.tf.bitcast(BF16)
        self.size = f32_elems * 4
        self.regs = []

    def v(self, off_f32, parts, free, dtype=F32):
        esz = 4 if dtype == F32 else 2
        lo = off_f32 * 4
        hi = lo + free * esz
        assert hi <= self.size, (self.name, off_f32, free, dtype)
        r = Reg(self, lo, hi)
        self.regs.append(r)
        if dtype == F32:
            ap = self.tf[0:parts, off_f32:off_f32 + free]
        else:
            ap = self.tb[0:parts, off_f32 * 2:off_f32 * 2 + free]
        return V(ap, r)

    def consolidate(self):
        if not self.regs:
            return
        r = Reg(self, 0, self.size)
        for o in self.regs:
            for e, c in o.writers.items():
                r.writers[e] = max(r.writers.get(e, 0), c)
            for e, c in o.readers.items():
                r.readers[e] = max(r.readers.get(e, 0), c)
        self.regs = [r]


class _Solo:
    regs = None


class Prog:
    NSLOT = 8

    def __init__(self, nc, same_engine_waits=True):
        self.nc = nc
        keys = list(ENGS)
        for qn in ("Q", "G", "S", "V"):
            keys += [f"{qn}#{i}" for i in range(self.NSLOT)]
        self.sem = {e: nc.alloc_semaphore(f"sem_{e.replace('#','_')}") for e in keys}
        self.cnt = {e: 0 for e in keys}
        self.q = {e: [] for e in ENGS}
        self.waited = {e: {e2: 0 for e2 in keys} for e in ENGS}
        self._slot_rr = {"Q": 0, "G": 0, "S": 0, "V": 0}
        self.sew = same_engine_waits
        self.psum = []
        self.psreg = []
        for i in range(8):
            self.psum.append(nc.alloc_psum_tensor(f"ps{i}", [128, 512], F32))
            self.psreg.append(Reg(_Solo(), 0, 1, excl_read=True))
        self._bank_rr = 0
        self._dma_rr = 0
        self.rr_banks = [0, 1, 2, 3, 4, 5, 6]

    def ps(self, i, parts=128, free=512, off=0):
        return V(self.psum[i][0:parts, off:off + free], self.psreg[i])

    def take_bank(self, n=1):
        out = []
        for _ in range(n):
            out.append(self.rr_banks[self._bank_rr % len(self.rr_banks)])
            self._bank_rr += 1
        return out if n > 1 else out[0]

    def stat(self, name, shape, dtype):
        t = self.nc.alloc_sbuf_tensor(name, shape, dtype)
        return V(t[:, :], Reg(_Solo(), 0, 1)), t

    def _expand(self, regs):
        out = []
        for r in regs:
            ar = r.arena
            if isinstance(ar, Arena):
                for o in ar.regs:
                    if r.lo < o.hi and o.lo < r.hi:
                        out.append(o)
            else:
                out.append(r)
        return out

    def op(self, eng, emit, reads=(), writes=(), inc=1, key=None):
        key = key or eng
        rr = [r for x in reads for r in (x.regs if isinstance(x, V) else [x])]
        ww = [r for x in writes for r in (x.regs if isinstance(x, V) else [x])]
        err = self._expand(rr)
        eww = self._expand(ww)
        deps = {}
        for r in err:
            for e, c in r.writers.items():
                deps[e] = max(deps.get(e, 0), c)
            if r.excl_read:
                for e, c in r.readers.items():
                    if e != eng:
                        deps[e] = max(deps.get(e, 0), c)
        for w in eww:
            for e, c in w.writers.items():
                deps[e] = max(deps.get(e, 0), c)
            for e, c in w.readers.items():
                deps[e] = max(deps.get(e, 0), c)
        waits = []
        for e, c in deps.items():
            if e == eng and not self.sew:
                continue
            if self.waited[eng][e] >= c:
                continue
            self.waited[eng][e] = c
            waits.append((self.sem[e], c))
        sem_self = self.sem[key]
        cnt_after = self.cnt[key] + inc

        def closure(E, waits=waits, emit=emit, inc=inc):
            for s, c in waits:
                E.wait_ge(s, c)
            ins = emit(E)
            if inc and ins is not None:
                ins.then_inc(sem_self, inc)

        self.q[eng].append(closure)
        if inc:
            self.cnt[key] = cnt_after
            for r in eww:
                r.writers = {key: cnt_after}
                r.readers = {}
            for r in err:
                r.readers[key] = cnt_after
        return cnt_after

    def raw_wait(self, eng, sem_key, cnt):
        if cnt <= 0 or self.waited[eng][sem_key] >= cnt:
            return
        self.waited[eng][sem_key] = cnt
        s = self.sem[sem_key]
        self.q[eng].append(lambda E, s=s, c=cnt: E.wait_ge(s, c))

    mm_count = 0

    def t_mm(self, bank_v, lhsT, rhs, start, stop):
        self.mm_count = getattr(self, 'mm_count', 0) + 1
        return self.op(
            "T",
            lambda E, o=bank_v.ap, l=lhsT.ap, r=rhs.ap, st=start, sp=stop:
                E.matmul(o, l, r, start=st, stop=sp),
            reads=[lhsT, rhs], writes=[bank_v] if (start or stop) else [], inc=1)

    def t_transpose(self, bank_v, in_v, ident_v):
        return self.op(
            "T",
            lambda E, o=bank_v.ap, i=in_v.ap, d=ident_v.ap: E.transpose(o, i, d),
            reads=[in_v, ident_v], writes=[bank_v], inc=1)

    def s_act(self, out, in_, func, bias=0.0, scale=1.0, accum=None):
        bap = bias.ap if isinstance(bias, V) else bias
        sap = scale.ap if isinstance(scale, V) else scale
        reads = [in_] + [x for x in (bias, scale) if isinstance(x, V)]
        writes = [out] + ([accum] if accum is not None else [])
        acc = accum.ap if accum is not None else None
        return self.op(
            "S",
            lambda E, o=out.ap, i=in_.ap, ff=func, b=bap, s=sap, a=acc:
                E.activation(o, i, ff, bias=b, scale=s, accum_out=a),
            reads=reads, writes=writes, inc=1)

    def s_copy(self, out, in_):
        return self.op("S", lambda E, o=out.ap, i=in_.ap: E.copy(o, i),
                       reads=[in_], writes=[out], inc=1)

    def v_tt(self, out, a, b, alu, eng="V"):
        return self.op(eng, lambda E, o=out.ap, x=a.ap, y=b.ap, u=alu:
                       E.tensor_tensor(o, x, y, u),
                       reads=[a, b], writes=[out], inc=1)

    def v_ts(self, out, in_, s1, s2, op0, op1=ALU.bypass, eng="V"):
        s1a = s1.ap if isinstance(s1, V) else s1
        s2a = s2.ap if isinstance(s2, V) else s2
        reads = [in_] + [x for x in (s1, s2) if isinstance(x, V)]
        return self.op(eng, lambda E, o=out.ap, i=in_.ap, a=s1a, b=s2a, u0=op0, u1=op1:
                       E.tensor_scalar(o, i, a, b, u0, u1),
                       reads=reads, writes=[out], inc=1)

    def v_stt(self, out, in0, scalar, in1, op0, op1, eng="V"):
        sa = scalar.ap if isinstance(scalar, V) else scalar
        reads = [in0, in1] + ([scalar] if isinstance(scalar, V) else [])
        return self.op(eng, lambda E, o=out.ap, i=in0.ap, s=sa, j=in1.ap, u0=op0, u1=op1:
                       E.scalar_tensor_tensor(o, i, s, j, u0, u1),
                       reads=reads, writes=[out], inc=1)

    def v_scan(self, out, d0, d1, init):
        ia = init.ap if isinstance(init, V) else init
        reads = [d0, d1] + ([init] if isinstance(init, V) else [])
        return self.op("V", lambda E, o=out.ap, a=d0.ap, b=d1.ap, i=ia:
                       E.tensor_tensor_scan(o, a, b, i, ALU.mult, ALU.add),
                       reads=reads, writes=[out], inc=1)

    def v_red(self, out, in_, alu=ALU.max, negate=False, eng="V"):
        return self.op(eng, lambda E, o=out.ap, i=in_.ap, u=alu, n=negate:
                       E.tensor_reduce(o, i, AX, u, negate=n),
                       reads=[in_], writes=[out], inc=1)

    def v_copy(self, out, in_, eng="V"):
        if eng == "S":
            return self.op(eng, lambda E, o=out.ap, i=in_.ap: E.copy(o, i),
                           reads=[in_], writes=[out], inc=1)
        return self.op(eng, lambda E, o=out.ap, i=in_.ap: E.tensor_copy(o, i),
                       reads=[in_], writes=[out], inc=1)

    def v_recip(self, out, in_):
        return self.op("V", lambda E, o=out.ap, i=in_.ap: E.reciprocal(o, i),
                       reads=[in_], writes=[out], inc=1)

    def memset(self, out, val, eng="V"):
        return self.op(eng, lambda E, o=out.ap, v=val: E.memset(o, v),
                       reads=[], writes=[out], inc=1)

    def dmaq(self, dst, src):
        return self.dma("Q", dst, src)

    def dma(self, qeng, dst, src):
        reads = [src] if isinstance(src, V) else []
        writes = [dst] if isinstance(dst, V) else []
        da = dst.ap if isinstance(dst, V) else dst
        sa = src.ap if isinstance(src, V) else src
        slot = self._slot_rr[qeng] % self.NSLOT
        self._slot_rr[qeng] += 1
        key = f"{qeng}#{slot}"
        self.raw_wait(qeng, key, self.cnt[key])
        return self.op(qeng, lambda E, d=da, s=sa: E.dma_start(d, s),
                       reads=reads, writes=writes, inc=16, key=key)

    def build(self):
        nc = self.nc
        with nc.Block() as blk:
            blk.tensor(lambda E: [fn(E) for fn in self.q["T"]])
            blk.vector(lambda E: [fn(E) for fn in self.q["V"]])
            blk.scalar(lambda E: [fn(E) for fn in self.q["S"]])
            blk.sync(lambda E: [fn(E) for fn in self.q["Q"]])
            blk.gpsimd(lambda E: [fn(E) for fn in self.q["G"]])
        return nc


class Ctx:
    pass


PREFIX = {"l0a": "l0_", "l0f": "l0f_", "l1a": "l1_", "l1f": "l1f_",
          "l2a": "l2_", "l2f": "l2f_", "l3a": "l3_", "l3f": "l3f_"}


def setup_ctx(nc, P, marrs, blocks):
    C = Ctx()
    C.P = P
    C.d = {}

    def din(name):
        a = marrs[name]
        dt = BF16 if a.dtype == NPBF else F32
        C.d[name] = nc.dram_tensor(name, list(a.shape), dt, kind="ExternalInput").ap()

    names = set()
    for k in marrs:
        if k.startswith("c_"):
            names.add(k)
        else:
            for b in blocks:
                if k.startswith(PREFIX[b]):
                    names.add(k)
    for k in sorted(names):
        din(k)
    C.innames = sorted(names) + ["xT"]
    C.xT = nc.dram_tensor("xT", [D, T], F32, kind="ExternalInput").ap()
    C.yT = nc.dram_tensor("yT", [D, T], F32, kind="ExternalOutput").ap()

    C.resA = Arena(P, "resA", 8 * T)
    C.res = [C.resA.v(i * T, 128, T, F32) for i in range(8)]
    C.ar = Arena(P, "arena", ARENA_N)

    def stat_load(name, parts, free, dtype=F32):
        vv, _ = P.stat("sb_" + name, [parts, free], dtype)
        P.dma("Q", vv, C.d[name])
        return vv

    C.ident = stat_load("c_ident", 128, 128)
    C.identb = stat_load("c_identb", 128, 128, BF16)
    C.onescol = stat_load("c_onescol", 128, 1)
    C.onescolb = stat_load("c_onescolb", 128, 1, BF16)
    C.onesrow = stat_load("c_onesrow", 96, 128)
    C.onesrowb = stat_load("c_onesrowb", 96, 128, BF16)
    C.rowAB, rowt = P.stat("rowAB", [96, 6144], F32)
    C.rowABb = rowt.bitcast(BF16)
    C.eps, _ = P.stat("c_eps", [96, 1], F32)
    P.memset(C.eps, 1e-5)
    C.scr0, _ = P.stat("scr0", [128, 512], F32)
    C.scr1, _ = P.stat("scr1", [128, 512], F32)
    C.r_mean = V(C.rowAB.ap[0:1, 0:T], C.rowAB.regs)
    C.r_rstd = V(C.rowAB.ap[0:1, T:2 * T], C.rowAB.regs)
    C.r_sumsq = V(C.rowAB.ap[0:1, 2 * T:3 * T], C.rowAB.regs)
    C.stat_load = stat_load
    return C


def load_block_statics(C, blocks):
    sl = C.stat_load
    if "l1a" in blocks:
        C.l1_bq = sl("l1_bq", 128, 8)
        C.l1_bk = sl("l1_bk", 128, 8)
        C.l1_bo = sl("l1_bo", 128, 8)
    if "l1f" in blocks:
        C.l1f_wgf = sl("l1f_wg", 128, 64)
        C.l1f_b1 = sl("l1f_b1", 128, 256)
        C.l1f_b2 = sl("l1f_b2", 128, 64)
    if "l2a" in blocks:
        for nm in ("bq", "bk", "bv", "bo"):
            setattr(C, "l2_" + nm, sl("l2_" + nm, 128, 8))
    if "l2f" in blocks:
        C.l2f_wgf = sl("l2f_wg", 128, 64)
        C.l2f_b1 = sl("l2f_b1", 128, 256)
        C.l2f_b2 = sl("l2f_b2", 128, 64)
    if "l3a" in blocks:
        C.l3_inproj_b = sl("l3_inproj_b", 128, 32)
        C.l3_convw = sl("l3_convw", 128, 64)
        C.l3_convb = sl("l3_convb", 128, 16)
        C.l3_xpb_head = sl("l3_xpb_head", 32, 1)
        C.l3_xpb_u = sl("l3_xpb_u", 128, 16)
        C.l3_dtb = sl("l3_dtb", 128, 16)
        C.l3_D = sl("l3_D", 128, 16)
        C.l3_outproj_b = sl("l3_outproj_b", 128, 8)
    if "l3f" in blocks:
        for nm in ("s1", "s2", "sh1", "sh2"):
            setattr(C, "l3f_" + nm, sl("l3f_" + nm, 128, 32))
        C.l3f_s3 = sl("l3f_s3", 128, 8)


CONVERT_ORDER = [
    ("ct", "l0_wq"), ("ct", "l0_wo"),
    ("ct", "l1_wk"), ("rw", "l1_wv_rows"), ("ct", "l1_wq"), ("ct", "l1_wo"),
    ("ex", "l1f_w1"), ("ex", "l1f_w2"),
    ("ct", "l2_wq"), ("ct", "l2_wk"), ("ct", "l2_wv"), ("ct", "l2_wo"),
    ("ex", "l2f_w1"), ("ex", "l2f_w2"),
    ("ct", "l3_inproj"), ("ct", "l3_xp_u"), ("ct", "l3_outproj"),
    ("ct", "l3f_w1"), ("ct", "l3f_w2"), ("ct", "l3f_h1"), ("ct", "l3f_h2"),
    ("ct", "l3f_w3")]


def emit_convert(C, blocks):
    P = C.P
    ar = C.ar
    nc = P.nc
    C.dbf = {}
    items = []

    def scratch(name, shape):
        t = nc.dram_tensor(name + "_bf", list(shape), BF16)
        vv = V(t.ap(), Reg(_Solo(), 0, 1))
        C.dbf[name] = vv
        return vv

    for kind, name in CONVERT_ORDER:
        if name not in C.d:
            continue
        sh = C.d[name].shape
        dst = scratch(name, sh)
        if kind == "ct":
            nmt, _, ktn, _ = sh
            for mt in range(nmt):
                items.append((C.d[name][mt].rearrange("p k j -> p (k j)"),
                              dst.ap[mt].rearrange("p k j -> p (k j)"), ktn * 128))
        elif kind == "rw":
            for kt in range(sh[0] // 128):
                items.append((C.d[name][kt * 128:(kt + 1) * 128, :],
                              dst.ap[kt * 128:(kt + 1) * 128, :], sh[1]))
        else:
            for e in range(sh[0]):
                for mt in range(sh[1]):
                    items.append((C.d[name][e, mt].rearrange("p k j -> p (k j)"),
                                  dst.ap[e, mt].rearrange("p k j -> p (k j)"),
                                  sh[3] * 128))
    NI, NO = 3, 2
    inb = [C.resA.v(i * 4096, 128, 4096, F32) for i in range(NI)]
    outb = [C.resA.v(NI * 4096 + i * 2048, 128, 4096, BF16) for i in range(NO)]
    n = len(items)
    for k in range(min(NI - 1, n)):
        sap, dap, X = items[k]
        P.dma("G" if k % 2 == 0 else "S",
              V(inb[k % NI].ap[:, 0:X], inb[k % NI].regs), sap)
    for k in range(n):
        sap, dap, X = items[k]
        kin = k + NI - 1
        if kin < n:
            s2, d2, X2 = items[kin]
            P.dma("G" if kin % 2 == 0 else "S",
                  V(inb[kin % NI].ap[:, 0:X2], inb[kin % NI].regs), s2)
        P.v_copy(V(outb[k % NO].ap[:, 0:X], outb[k % NO].regs),
                 V(inb[k % NI].ap[:, 0:X], inb[k % NI].regs), eng="V")
        P.dma("S" if k % 2 == 0 else "G", dap,
              V(outb[k % NO].ap[:, 0:X], outb[k % NO].regs))
    C.resA.consolidate()


def emit_ln(C):
    P = C.P
    b0, b1 = P.take_bank(), P.take_bank()
    for c in range(4):
        cs = slice(c * 512, (c + 1) * 512)
        psS = P.ps(b0, 1, 512)
        psQ = P.ps(b1, 1, 512)
        for i in range(8):
            P.t_mm(psS, C.onescol, C.res[i][:, cs], start=(i == 0), stop=(i == 7))
        for i in range(8):
            sq = C.scr0 if i % 2 == 0 else C.scr1
            P.s_act(sq, C.res[i][:, cs], AF.Square)
            P.t_mm(psQ, C.onescol, sq, start=(i == 0), stop=(i == 7))
        P.s_copy(V(C.r_mean.ap[:, cs], C.r_mean.regs), psS)
        P.s_copy(V(C.r_sumsq.ap[:, cs], C.r_sumsq.regs), psQ)
    P.v_ts(C.r_mean, C.r_mean, 1.0 / D, None, ALU.mult)
    P.v_ts(C.r_sumsq, C.r_sumsq, 1.0 / D, None, ALU.mult)
    P.v_tt(C.r_rstd, C.r_mean, C.r_mean, ALU.mult)
    P.v_tt(C.r_sumsq, C.r_sumsq, C.r_rstd, ALU.subtract)
    P.s_act(C.r_sumsq, C.r_sumsq, AF.Ln, bias=C.eps[0:1, :])
    P.s_act(C.r_rstd, C.r_sumsq, AF.Exp, scale=-0.5)


def emit_xhat_chunk(C, t0, width, dst_tiles, bm0, bm1, tmp=None):
    P = C.P
    cs = slice(t0, t0 + width)
    psM = P.ps(bm0, 128, width)
    psR = P.ps(bm1, 128, width)
    P.t_mm(psM, C.onesrow[0:1, :], V(C.r_mean.ap[:, cs], C.r_mean.regs), True, True)
    P.t_mm(psR, C.onesrow[0:1, :], V(C.r_rstd.ap[:, cs], C.r_rstd.regs), True, True)
    tmp = tmp if tmp is not None else C.scr0
    for i in range(8):
        tv = V(tmp.ap[:, 0:width], tmp.regs)
        P.v_tt(tv, C.res[i][:, cs], psM, ALU.subtract)
        P.v_tt(dst_tiles[i], tv, psR, ALU.mult)


def stream_coltile(C, name, mt, kt_n, stage_f32, dst_bf16, cast_eng="S"):
    P = C.P
    P.dmaq(dst_bf16, C.dbf[name].ap[mt].rearrange("p k j -> p (k j)"))


def emit_l0a(C):
    P = C.P
    ar = C.ar
    emit_ln(C)
    KR = [ar.v(2048 + i * 1024, 128, T, BF16) for i in range(8)]
    VT = ar.v(10240, 128, 16 * 1024, BF16)
    WST = ar.v(22528, 128, 1024, F32)
    ropeC = ar.v(23552, 128, T, BF16)
    ropeS = ar.v(24576, 128, T, BF16)
    stair = ar.v(25600, 128, 896, BF16)
    swapb = ar.v(26048, 128, 128, BF16)
    P.dma("Q", ropeC, C.d["c_ropeC"])
    P.dma("Q", ropeS, C.d["c_ropeS"])
    P.dma("Q", stair, C.d["c_stair"])
    P.dma("Q", swapb, C.d["c_swap"])

    rsc1 = ar.v(26112, 128, 256, F32)
    rsc2 = ar.v(26368, 128, 256, F32)

    def rope_tile(src_bf, psb, dst, tslice):
        w = tslice.stop - tslice.start
        psw = P.ps(psb, 128, w)
        P.t_mm(psw, swapb, src_bf, True, True)
        P.v_tt(V(rsc1.ap[:, 0:w], rsc1.regs), src_bf,
               V(ropeC.ap[:, tslice], ropeC.regs), ALU.mult)
        P.v_tt(V(rsc2.ap[:, 0:w], rsc2.regs), psw,
               V(ropeS.ap[:, tslice], ropeS.regs), ALU.mult)
        P.v_tt(dst, V(rsc1.ap[:, 0:w], rsc1.regs),
               V(rsc2.ap[:, 0:w], rsc2.regs), ALU.add)

    WKB = [ar.v(18560 + m * 512, 128, 1024, BF16) for m in range(8)]
    for m in range(8):
        P.dma("Q", WST, C.d["l0_wk"][m].rearrange("p k j -> p (k j)"))
        P.v_copy(WKB[m], WST, eng="S" if m % 2 == 0 else "V")
    qtmp = ar.v(26624, 128, 512, BF16)
    for c in range(4):
        cs = slice(c * 512, (c + 1) * 512)
        XHC = [ar.v(i * 256, 128, 512, BF16) for i in range(8)]
        emit_xhat_chunk(C, c * 512, 512, XHC, P.take_bank(), P.take_bank())
        for m in range(8):
            psK = P.ps(P.take_bank(), 128, 512)
            for kt in range(8):
                P.t_mm(psK, V(WKB[m].ap[:, kt * 128:(kt + 1) * 128], WKB[m].regs),
                       XHC[kt], start=(kt == 0), stop=(kt == 7))
            P.s_copy(qtmp, psK)
            for half in range(2):
                hs = slice(c * 512 + half * 256, c * 512 + half * 256 + 256)
                rope_tile(V(qtmp.ap[:, half * 256:half * 256 + 256], qtmp.regs),
                          P.take_bank(),
                          V(KR[m].ap[:, hs], KR[m].regs), hs)

    WVB = [ar.v(18560 + kt * 512, 128, 1024, BF16) for kt in range(8)]
    for kt in range(8):
        P.dma("Q", WST, C.d["l0_wv_rows"][kt * 128:(kt + 1) * 128, :])
        P.v_copy(WVB[kt], WST, eng="S" if kt % 2 == 0 else "V")
    for c in range(4):
        XHC = [ar.v(i * 256, 128, 512, BF16) for i in range(8)]
        emit_xhat_chunk(C, c * 512, 512, XHC, P.take_bank(), P.take_bank())
        for tloc in range(4):
            tt = c * 4 + tloc
            for ec in range(2):
                psV = P.ps(P.take_bank(), 128, 512)
                for kt in range(8):
                    P.t_mm(psV,
                           V(XHC[kt].ap[:, tloc * 128:(tloc + 1) * 128], XHC[kt].regs),
                           V(WVB[kt].ap[:, ec * 512:(ec + 1) * 512], WVB[kt].regs),
                           start=(kt == 0), stop=(kt == 7))
                P.s_copy(V(VT.ap[:, tt * 1024 + ec * 512:tt * 1024 + ec * 512 + 512],
                           VT.regs), psV)

    otmp = ar.v(26880, 128, 256, F32)
    for c in range(8):
        cw = 256
        cs = slice(c * cw, (c + 1) * cw)
        XHC = [ar.v(i * 128, 128, cw, BF16) for i in range(8)]
        QR = [ar.v(1024 + i * 128, 128, cw, BF16) for i in range(8)]
        ATT = [ar.v(18432 + tt * 128, 128, cw, BF16) for tt in range(16)]
        OUTC = [ar.v(20480 + i * 128, 128, cw, BF16) for i in range(8)]
        WQB = [ar.v(21504 + b * 512, 128, 1024, BF16) for b in range(2)]
        WOB = [ar.v(22528 + b * 512, 128, 1024, BF16) for b in range(2)]
        emit_xhat_chunk(C, c * cw, cw, XHC, P.take_bank(), P.take_bank())
        for m in range(8):
            wb = WQB[m % 2]
            stream_coltile(C, "l0_wq", m, 8, WST, wb)
            psQ = P.ps(P.take_bank(), 128, cw)
            for kt in range(8):
                P.t_mm(psQ, V(wb.ap[:, kt * 128:(kt + 1) * 128], wb.regs),
                       XHC[kt], start=(kt == 0), stop=(kt == 7))
            P.s_copy(V(qtmp.ap[:, 0:cw], qtmp.regs), psQ)
            rope_tile(V(qtmp.ap[:, 0:cw], qtmp.regs), P.take_bank(), QR[m], cs)
        ndiag = 2 * (c + 1)
        for h in range(16):
            mt = h // 2
            rlo = (h % 2) * 64
            bS1, bS2 = P.take_bank(), P.take_bank()
            bO, bR = P.take_bank(), P.take_bank()
            psO = V(P.psum[bO][rlo:rlo + 64, 0:cw], P.psreg[bO])
            psR = V(P.psum[bR][rlo:rlo + 1, 0:cw], P.psreg[bR])
            for tt in range(ndiag + 1):
                if tt < ndiag:
                    psS = P.ps(bS1 if tt % 2 == 0 else bS2, 128, cw)
                    P.t_mm(psS,
                           V(KR[mt].ap[rlo:rlo + 64, tt * 128:(tt + 1) * 128],
                             KR[mt].regs),
                           V(QR[mt].ap[rlo:rlo + 64, :], QR[mt].regs), True, True)
                    P.s_act(ATT[tt], psS, AF.Exp, scale=0.125)
                    if tt >= 2 * c:
                        delta = tt * 128 - c * cw
                        P.v_tt(ATT[tt], ATT[tt],
                               V(stair.ap[:, 384 - delta:384 - delta + cw],
                                 stair.regs), ALU.mult)
                if tt > 0:
                    pt = tt - 1
                    P.t_mm(psO,
                           V(VT.ap[:, pt * 1024 + h * 64:pt * 1024 + h * 64 + 64],
                             VT.regs),
                           ATT[pt], start=(pt == 0), stop=(pt == ndiag - 1))
                    P.t_mm(psR, C.onescolb, ATT[pt],
                           start=(pt == 0), stop=(pt == ndiag - 1))
            crow = V(C.rowAB.ap[rlo:rlo + 1, 2 * T:2 * T + cw], C.rowAB.regs)
            P.v_recip(crow, psR)
            bC = P.take_bank()
            psC = V(P.psum[bC][rlo:rlo + 64, 0:cw], P.psreg[bC])
            P.t_mm(psC, V(C.onesrow.ap[rlo:rlo + 1, 0:64], C.onesrow.regs),
                   crow, True, True)
            ot = V(otmp.ap[rlo:rlo + 64, :], otmp.regs)
            P.s_copy(ot, psO)
            P.v_tt(V(OUTC[mt].ap[rlo:rlo + 64, :], OUTC[mt].regs), ot, psC, ALU.mult)
        for m in range(8):
            wb = WOB[m % 2]
            stream_coltile(C, "l0_wo", m, 8, WST, wb)
            psY = P.ps(P.take_bank(), 128, cw)
            for kt in range(8):
                P.t_mm(psY, V(wb.ap[:, kt * 128:(kt + 1) * 128], wb.regs),
                       OUTC[kt], start=(kt == 0), stop=(kt == 7))
            P.v_tt(C.res[m][:, cs], C.res[m][:, cs], psY, ALU.add)
    ar.consolidate()


def emit_l0f(C):
    P = C.P
    ar = C.ar
    emit_ln(C)
    XH = [ar.v(i * 1024, 128, T, BF16) for i in range(8)]
    for c in range(4):
        emit_xhat_chunk(C, c * 512, 512,
                        [V(XH[i].ap[:, c * 512:(c + 1) * 512], XH[i].regs)
                         for i in range(8)],
                        P.take_bank(), P.take_bank())
    LG = ar.v(8192, 8, T, F32)
    CW = ar.v(10240, 8, T, F32)
    AFL = ar.v(12288, 128, 1024, BF16)
    BFL = ar.v(12800, 128, 1024, BF16)
    WGB = ar.v(13312, 128, 64, BF16)
    erpat = ar.v(13344, 8, 128, F32)
    P.dma("Q", erpat, C.d["c_erpat"])
    stg = ar.v(13568, 128, 1024, F32)
    P.dma("Q", stg, C.d["l0f_a"])
    P.v_copy(AFL, stg)
    P.dma("Q", stg, C.d["l0f_b"])
    P.v_copy(BFL, stg)
    wgstg = ar.v(13472, 128, 64, F32)
    P.dma("Q", wgstg, C.d["l0f_wg"])
    P.v_copy(WGB, wgstg)
    bG = P.take_bank()
    for c in range(4):
        cs = slice(c * 512, (c + 1) * 512)
        psG = P.ps(bG, 8, 512)
        for kt in range(8):
            P.t_mm(psG, V(WGB.ap[:, kt * 8:(kt + 1) * 8], WGB.regs),
                   V(XH[kt].ap[:, cs], XH[kt].regs), start=(kt == 0), stop=(kt == 7))
        P.s_copy(V(LG.ap[:, cs], LG.regs), psG)
    tokbase = 14592
    LGt = ar.v(tokbase, 128, 8, F32)
    m1 = ar.v(tokbase + 8, 128, 1, F32)
    m1n = ar.v(tokbase + 9, 128, 1, F32)
    m2 = ar.v(tokbase + 10, 128, 1, F32)
    qv = ar.v(tokbase + 11, 128, 1, F32)
    w1 = ar.v(tokbase + 12, 128, 1, F32)
    w2 = ar.v(tokbase + 13, 128, 1, F32)
    msk1 = ar.v(tokbase + 16, 128, 8, F32)
    msk2 = ar.v(tokbase + 24, 128, 8, F32)
    g2 = ar.v(tokbase + 32, 128, 8, F32)
    cwt = ar.v(tokbase + 40, 128, 8, F32)
    bT1, bT2 = P.take_bank(), P.take_bank()
    for tt in range(16):
        psL = P.ps(bT1, 128, 8)
        P.t_transpose(psL, V(LG.ap[:, tt * 128:(tt + 1) * 128], LG.regs),
                      V(C.ident.ap[0:8, 0:8], C.ident.regs))
        P.s_copy(LGt, psL)
        P.v_red(m1, LGt, ALU.max)
        P.v_red(m1n, LGt, ALU.max, negate=True)
        P.v_ts(msk1, LGt, m1, None, ALU.is_equal)
        P.v_stt(g2, msk1, -1e30, LGt, ALU.mult, ALU.add)
        P.v_red(m2, g2, ALU.max)
        P.v_ts(msk2, g2, m2, None, ALU.is_equal)
        P.s_act(qv, m2, AF.Exp, bias=m1n)
        P.v_ts(qv, qv, 1.0, None, ALU.add)
        P.v_recip(w1, qv)
        P.v_ts(qv, qv, -1.0, None, ALU.add)
        P.v_tt(w2, qv, w1, ALU.mult)
        P.v_ts(cwt, msk1, w1, None, ALU.mult)
        P.v_stt(cwt, msk2, w2, cwt, ALU.mult, ALU.add)
        psB = P.ps(bT2, 8, 128)
        P.t_transpose(psB, cwt, C.ident)
        P.s_copy(V(CW.ap[:, tt * 128:(tt + 1) * 128], CW.regs), psB)
    HW = ar.v(14336, 128, 512, BF16)
    for c in range(4):
        cs = slice(c * 512, (c + 1) * 512)
        psH = P.ps(P.take_bank(), 128, 512)
        for kt in range(8):
            P.t_mm(psH, V(AFL.ap[:, kt * 128:(kt + 1) * 128], AFL.regs),
                   V(XH[kt].ap[:, cs], XH[kt].regs), start=(kt == 0), stop=(kt == 7))
        P.s_copy(C.scr0, psH)
        psBC = P.ps(P.take_bank(), 128, 512)
        P.t_mm(psBC, erpat, V(CW.ap[:, cs], CW.regs), True, True)
        P.v_tt(HW, C.scr0, psBC, ALU.mult)
        for mm in range(8):
            psO = P.ps(P.take_bank(), 128, 512)
            P.t_mm(psO, V(BFL.ap[:, mm * 128:(mm + 1) * 128], BFL.regs), HW,
                   True, True)
            P.v_tt(C.res[mm][:, cs], C.res[mm][:, cs], psO, ALU.add)
    ar.consolidate()


def emit_l1a(C):
    P = C.P
    ar = C.ar
    emit_ln(C)
    KT = [ar.v(2048 + i * 1024, 128, T, BF16) for i in range(8)]
    VT = ar.v(10240, 128, 16 * 1024, BF16)
    WST = ar.v(25088, 128, 1024, F32)
    sc2 = ar.v(26112, 128, 256, F32)
    sc3 = ar.v(26368, 128, 256, F32)
    bvrow = ar.v(26624, 1, D, BF16)
    bvstg = ar.v(25088, 1, D, F32)
    P.dma("Q", bvstg, C.d["l1_bv_row"])
    P.v_copy(bvrow, bvstg)

    WKB = [ar.v(20480 + m * 512, 128, 1024, BF16) for m in range(8)]
    for m in range(8):
        stream_coltile(C, "l1_wk", m, 8, WST, WKB[m])
    for c in range(4):
        XHC = [ar.v(i * 256, 128, 512, BF16) for i in range(8)]
        emit_xhat_chunk(C, c * 512, 512, XHC, P.take_bank(), P.take_bank())
        for m in range(8):
            psK = P.ps(P.take_bank(), 128, 512)
            for kt in range(8):
                P.t_mm(psK, V(WKB[m].ap[:, kt * 128:(kt + 1) * 128], WKB[m].regs),
                       XHC[kt], start=(kt == 0), stop=(kt == 7))
            P.v_ts(V(KT[m].ap[:, c * 512:(c + 1) * 512], KT[m].regs), psK,
                   C.l1_bk[:, m:m + 1], None, ALU.add)
    if L1A_PHASES < 2:
        ar.consolidate()
        return
    WVB = [ar.v(20480 + kt * 512, 128, 1024, BF16) for kt in range(8)]
    for kt in range(8):
        P.dmaq(WVB[kt], C.dbf["l1_wv_rows"].ap[kt * 128:(kt + 1) * 128, :])
    for c in range(4):
        XHC = [ar.v(i * 256, 128, 512, BF16) for i in range(8)]
        emit_xhat_chunk(C, c * 512, 512, XHC, P.take_bank(), P.take_bank())
        for tloc in range(4):
            tt = c * 4 + tloc
            for ec in range(2):
                psV = P.ps(P.take_bank(), 128, 512)
                for kt in range(8):
                    P.t_mm(psV,
                           V(XHC[kt].ap[:, tloc * 128:(tloc + 1) * 128], XHC[kt].regs),
                           V(WVB[kt].ap[:, ec * 512:(ec + 1) * 512], WVB[kt].regs),
                           start=(kt == 0), stop=False)
                P.t_mm(psV, C.onesrowb[0:1, :],
                       V(bvrow.ap[:, ec * 512:(ec + 1) * 512], bvrow.regs),
                       start=False, stop=True)
                P.s_copy(V(VT.ap[:, tt * 1024 + ec * 512:tt * 1024 + ec * 512 + 512],
                           VT.regs), psV)
    if L1A_PHASES < 3:
        ar.consolidate()
        return
    WQB = [ar.v(20480 + m * 512, 128, 1024, BF16) for m in range(8)]
    for m in range(8):
        stream_coltile(C, "l1_wq", m, 8, WST, WQB[m])
    WOB = [ar.v(24576 + b * 512, 128, 1024, BF16) for b in range(2)]
    for c in range(L1A_CHUNKS):
        cw = 256
        cs = slice(c * cw, (c + 1) * cw)
        XHC = [ar.v(i * 128, 128, cw, BF16) for i in range(8)]
        QC = [ar.v(1024 + i * 128, 128, cw, BF16) for i in range(8)]
        ATT = [ar.v(18432 + tt * 128, 128, cw, BF16) for tt in range(16)]
        OUTC = [ar.v(i * 128, 128, cw, BF16) for i in range(8)]
        emit_xhat_chunk(C, c * cw, cw, XHC, P.take_bank(), P.take_bank())
        for m in range(8):
            psQ = P.ps(P.take_bank(), 128, cw)
            for kt in range(8):
                P.t_mm(psQ, V(WQB[m].ap[:, kt * 128:(kt + 1) * 128], WQB[m].regs),
                       XHC[kt], start=(kt == 0), stop=(kt == 7))
            P.v_ts(QC[m], psQ, C.l1_bq[:, m:m + 1], None, ALU.add)
        if L1A_PHASES < 4:
            continue
        bS1, bS2 = P.take_bank(), P.take_bank()
        for tt in range(16):
            psS = P.ps(bS1 if tt % 2 == 0 else bS2, 128, cw)
            for kt in range(8):
                P.t_mm(psS, V(KT[kt].ap[:, tt * 128:(tt + 1) * 128], KT[kt].regs),
                       QC[kt], start=(kt == 0), stop=(kt == 7))
            if L1A_THETA == 0:
                P.s_copy(ATT[tt], psS)
                continue
            P.s_act(C.scr0[:, 0:cw], psS, AF.Tanh, scale=0.5)
            P.s_act(C.scr1[:, 0:cw], psS, AF.Tanh)
            if L1A_THETA == 1:
                P.s_copy(ATT[tt], psS)
                continue
            P.v_ts(sc2, psS, 0.0, 0.1, ALU.max, ALU.mult)
            if L1A_THETA == 21:
                P.s_copy(ATT[tt], psS)
                continue
            P.v_stt(sc3, C.scr0[:, 0:cw], 0.1, sc2, ALU.mult, ALU.add)
            if L1A_THETA == 22:
                P.s_copy(ATT[tt], psS)
                continue
            P.v_stt(sc2, C.scr1[:, 0:cw], 0.15, sc3, ALU.mult, ALU.add)
            if L1A_THETA == 2:
                P.s_copy(ATT[tt], psS)
                continue
            P.s_act(ATT[tt], sc2, AF.Exp)
        if L1A_PHASES < 5:
            continue
        bR = P.take_bank()
        psR = V(P.psum[bR][0:1, 0:cw], P.psreg[bR])
        for tt in range(16):
            P.t_mm(psR, C.onescolb, ATT[tt], start=(tt == 0), stop=(tt == 15))
        crow = V(C.rowAB.ap[0:1, 2 * T:2 * T + cw], C.rowAB.regs)
        P.v_recip(crow, psR)
        psC = P.ps(7, 128, cw)
        P.t_mm(psC, C.onesrow[0:1, :], crow, True, True)
        if L1A_PHASES < 6:
            continue
        for dh in range(2):
            banks = [P.take_bank() for _ in range(4)]
            for mi in range(4):
                m = dh * 4 + mi
                psO = P.ps(banks[mi], 128, cw)
                for tt in range(16):
                    P.t_mm(psO,
                           V(VT.ap[:, tt * 1024 + m * 128:tt * 1024 + (m + 1) * 128],
                             VT.regs),
                           ATT[tt], start=(tt == 0), stop=(tt == 15))
            for mi in range(4):
                m = dh * 4 + mi
                psO = P.ps(banks[mi], 128, cw)
                P.s_copy(C.scr0[:, 0:cw], psO)
                P.v_tt(OUTC[m], C.scr0[:, 0:cw], psC, ALU.mult)
        if L1A_PHASES < 7:
            continue
        for m in range(8):
            wb = WOB[m % 2]
            stream_coltile(C, "l1_wo", m, 8, WST, wb)
            psY = P.ps(P.take_bank(), 128, cw)
            for kt in range(8):
                P.t_mm(psY, V(wb.ap[:, kt * 128:(kt + 1) * 128], wb.regs),
                       OUTC[kt], start=(kt == 0), stop=(kt == 7))
            P.v_stt(C.res[m][:, cs], psY, C.l1_bo[:, m:m + 1], C.res[m][:, cs],
                    ALU.add, ALU.add)
    ar.consolidate()


def _emit_router(C, LG, tokbase, topk):
    P = C.P
    ar = C.ar
    LGt = ar.v(tokbase, 128, 8, F32)
    m1 = ar.v(tokbase + 8, 128, 1, F32)
    m1n = ar.v(tokbase + 9, 128, 1, F32)
    sm = ar.v(tokbase + 10, 128, 1, F32)
    w1 = ar.v(tokbase + 11, 128, 1, F32)
    m2 = ar.v(tokbase + 12, 128, 1, F32)
    ex = ar.v(tokbase + 16, 128, 8, F32)
    msk1 = ar.v(tokbase + 24, 128, 8, F32)
    msk2 = ar.v(tokbase + 32, 128, 8, F32)
    g2 = ar.v(tokbase + 40, 128, 8, F32)
    cwt = ar.v(tokbase + 48, 128, 8, F32)
    bT1, bT2 = P.take_bank(), P.take_bank()
    for tt in range(16):
        psL = P.ps(bT1, 128, 8)
        P.t_transpose(psL, V(LG.ap[:, tt * 128:(tt + 1) * 128], LG.regs),
                      V(C.ident.ap[0:8, 0:8], C.ident.regs))
        P.s_copy(LGt, psL)
        P.v_red(m1, LGt, ALU.max)
        P.v_red(m1n, LGt, ALU.max, negate=True)
        P.s_act(ex, LGt, AF.Exp, bias=m1n)
        P.v_red(sm, ex, ALU.add)
        P.v_recip(w1, sm)
        P.v_ts(msk1, LGt, m1, None, ALU.is_equal)
        if topk == 1:
            P.v_ts(cwt, msk1, w1, None, ALU.mult)
        else:
            P.v_ts(ex, ex, w1, None, ALU.mult)
            P.v_stt(g2, msk1, -1e30, LGt, ALU.mult, ALU.add)
            P.v_red(m2, g2, ALU.max)
            P.v_ts(msk2, g2, m2, None, ALU.is_equal)
            P.v_tt(cwt, msk1, ex, ALU.mult)
            P.v_tt(msk2, msk2, ex, ALU.mult)
            P.v_tt(cwt, cwt, msk2, ALU.add)
        psB = P.ps(bT2, 8, 128)
        P.t_transpose(psB, cwt, C.ident)
        P.s_copy(V(C.LGb.ap[:, tt * 128:(tt + 1) * 128], C.LGb.regs), psB)


def _emit_moe(C, wgname, w1name, w2name, b1col, b2col, topk):
    P = C.P
    ar = C.ar
    emit_ln(C)
    XH = [ar.v(i * 1024, 128, T, BF16) for i in range(8)]
    for c in range(4):
        emit_xhat_chunk(C, c * 512, 512,
                        [V(XH[i].ap[:, c * 512:(c + 1) * 512], XH[i].regs)
                         for i in range(8)],
                        P.take_bank(), P.take_bank())
    LG = ar.v(24576, 8, T, F32)
    C.LGb = ar.v(24576, 8, T, BF16)
    WGB = ar.v(26688, 128, 64, BF16)
    wgstg = ar.v(26624, 128, 64, F32)
    P.dma("Q", wgstg, C.d[wgname])
    P.v_copy(WGB, wgstg)
    bG = P.take_bank()
    for c in range(4):
        cs = slice(c * 512, (c + 1) * 512)
        psG = P.ps(bG, 8, 512)
        for kt in range(8):
            P.t_mm(psG, V(WGB.ap[:, kt * 8:(kt + 1) * 8], WGB.regs),
                   V(XH[kt].ap[:, cs], XH[kt].regs), start=(kt == 0), stop=(kt == 7))
        P.s_copy(V(LG.ap[:, cs], LG.regs), psG)
    _emit_router(C, LG, 26752, topk)
    MErowB = V(C.rowABb[0:1, 4 * T:4 * T + 512], C.rowAB.regs)
    WBH = [ar.v(25600, 128, 2048, BF16), ar.v(26624, 128, 2048, BF16)]
    H = [ar.v(8192 + i * 1024, 128, T, BF16) for i in range(16)]
    save_rr = P.rr_banks
    P.rr_banks = [0, 1, 2, 3]
    for e in range(8):
        for ch in range(4):
            chs = slice(ch * 512, (ch + 1) * 512)
            P.dma("G", MErowB, V(C.LGb.ap[e:e + 1, chs], C.LGb.regs))
            psM = P.ps(4 + ch, 128, 512)
            P.t_mm(psM, C.onesrowb[0:1, :], MErowB, True, True)
        for hf in range(2):
            for mi in range(16):
                mt = hf * 16 + mi
                pr = mi // 2
                wb = WBH[pr % 2]
                if mi % 2 == 0:
                    P.dmaq(V(wb.ap.rearrange("p (a x) -> p a x", a=2), wb.regs),
                           C.dbf[w1name].ap[e, mt:mt + 2]
                           .rearrange("a p k j -> p a (k j)"))
                wsl = (mi % 2) * 1024
                for ch in range(4):
                    chs = slice(ch * 512, (ch + 1) * 512)
                    psH = P.ps(P.take_bank(), 128, 512)
                    for kt in range(8):
                        P.t_mm(psH, V(wb.ap[:, wsl + kt * 128:wsl + (kt + 1) * 128],
                                      wb.regs),
                               V(XH[kt].ap[:, chs], XH[kt].regs),
                               start=(kt == 0), stop=(kt == 7))
                    P.s_act(V(H[mi].ap[:, chs], H[mi].regs), psH, AF.Gelu,
                            bias=b1col[:, e * 32 + mt:e * 32 + mt + 1])
            for dm in range(8):
                wb = WBH[dm % 2]
                P.dmaq(wb, C.dbf[w2name].ap[e, dm][:, hf * 16:(hf + 1) * 16, :]
                       .rearrange("p k j -> p (k j)"))
                banks = [0, 1, 2, 3]
                for ch in range(4):
                    chs = slice(ch * 512, (ch + 1) * 512)
                    psY = P.ps(banks[ch], 128, 512)
                    for kt in range(16):
                        P.t_mm(psY, V(wb.ap[:, kt * 128:(kt + 1) * 128], wb.regs),
                               V(H[kt].ap[:, chs], H[kt].regs),
                               start=(kt == 0), stop=(kt == 15))
                for ch in range(4):
                    chs = slice(ch * 512, (ch + 1) * 512)
                    psY = P.ps(banks[ch], 128, 512)
                    if hf == 1:
                        P.v_ts(C.scr0, psY, b2col[:, e * 8 + dm:e * 8 + dm + 1],
                               None, ALU.add)
                    else:
                        P.s_copy(C.scr0, psY)
                    psM = P.ps(4 + ch, 128, 512)
                    P.v_tt(C.scr1, C.scr0, psM, ALU.mult)
                    P.v_tt(C.res[dm][:, chs], C.res[dm][:, chs], C.scr1, ALU.add)
    P.rr_banks = save_rr
    ar.consolidate()


def emit_l1f(C):
    _emit_moe(C, "l1f_wg", "l1f_w1", "l1f_w2", C.l1f_b1, C.l1f_b2, 1)


def emit_l2f(C):
    _emit_moe(C, "l2f_wg", "l2f_w1", "l2f_w2", C.l2f_b1, C.l2f_b2, 2)


def emit_l2a(C):
    P = C.P
    ar = C.ar
    emit_ln(C)
    QF = [ar.v(2048 + i * 256, 128, 512, BF16) for i in range(8)]
    KF = [ar.v(4096 + i * 256, 128, 512, BF16) for i in range(8)]
    VF = [ar.v(6144 + i * 512, 128, 512, F32) for i in range(8)]
    QKV = [ar.v(10240 + i * 256, 128, 512, BF16) for i in range(8)]
    WSTG = ar.v(12288, 128, 1024, F32)
    WB2 = [ar.v(13312 + b * 512, 128, 1024, BF16) for b in range(2)]
    hsum = ar.v(14336, 128, 128, BF16)
    P.dma("Q", hsum, C.d["c_hsum"])
    for c in range(4):
        cs = slice(c * 512, (c + 1) * 512)
        XHC = [ar.v(i * 256, 128, 512, BF16) for i in range(8)]
        emit_xhat_chunk(C, c * 512, 512, XHC, P.take_bank(), P.take_bank())
        for m in range(8):
            wb = WB2[m % 2]
            stream_coltile(C, "l2_wq", m, 8, WSTG, wb)
            psQ = P.ps(P.take_bank(), 128, 512)
            for kt in range(8):
                P.t_mm(psQ, V(wb.ap[:, kt * 128:(kt + 1) * 128], wb.regs),
                       XHC[kt], start=(kt == 0), stop=(kt == 7))
            P.s_act(C.scr0, psQ, AF.Square, bias=C.l2_bq[:, m:m + 1])
            P.s_act(QF[m], C.scr0, AF.Exp, scale=-0.5)
            wb = WB2[m % 2]
            stream_coltile(C, "l2_wk", m, 8, WSTG, wb)
            psK = P.ps(P.take_bank(), 128, 512)
            for kt in range(8):
                P.t_mm(psK, V(wb.ap[:, kt * 128:(kt + 1) * 128], wb.regs),
                       XHC[kt], start=(kt == 0), stop=(kt == 7))
            P.s_act(C.scr0, psK, AF.Square, bias=C.l2_bk[:, m:m + 1])
            P.s_act(KF[m], C.scr0, AF.Exp, scale=-0.5)
            wb = WB2[m % 2]
            stream_coltile(C, "l2_wv", m, 8, WSTG, wb)
            psV = P.ps(P.take_bank(), 128, 512)
            for kt in range(8):
                P.t_mm(psV, V(wb.ap[:, kt * 128:(kt + 1) * 128], wb.regs),
                       XHC[kt], start=(kt == 0), stop=(kt == 7))
            P.v_ts(VF[m], psV, C.l2_bv[:, m:m + 1], None, ALU.add)
        for m in range(8):
            P.v_tt(QF[m], QF[m], KF[m], ALU.mult)
            psCB = P.ps(P.take_bank(), 128, 512)
            P.t_mm(psCB, hsum, QF[m], True, True)
            P.v_tt(QKV[m], VF[m], psCB, ALU.mult)
        for m in range(8):
            wb = WB2[m % 2]
            stream_coltile(C, "l2_wo", m, 8, WSTG, wb)
            psY = P.ps(P.take_bank(), 128, 512)
            for kt in range(8):
                P.t_mm(psY, V(wb.ap[:, kt * 128:(kt + 1) * 128], wb.regs),
                       QKV[kt], start=(kt == 0), stop=(kt == 7))
            P.v_stt(C.res[m][:, cs], psY, C.l2_bo[:, m:m + 1], C.res[m][:, cs],
                    ALU.add, ALU.add)
    ar.consolidate()


def emit_l3a(C):
    P = C.P
    ar = C.ar
    emit_ln(C)
    save_rr = P.rr_banks
    P.rr_banks = [0, 1, 2, 3, 4, 5]
    CK = 256
    halo = ar.v(0, 128, 48, F32)
    xcbuf = ar.v(48, 128, 260, F32)
    xc_s = [ar.v(308 + i * 128, 128, CK, BF16) for i in range(16)]
    z_s = [ar.v(2356 + i * 128, 128, CK, BF16) for i in range(16)]
    u = [ar.v(4404 + i * 128, 128, CK, BF16) for i in range(16)]
    delta = [ar.v(6452 + i * 256, 128, CK, F32) for i in range(16)]
    w = [ar.v(10548 + i * 256, 128, CK, F32) for i in range(16)]
    y = [ar.v(14644 + i * 256, 128, CK, F32) for i in range(16)]
    yfin = [ar.v(18740 + i * 128, 128, CK, BF16) for i in range(16)]
    hstate = ar.v(20788, 128, 256, F32)
    dsB = ar.v(21044, 32, CK, F32)
    dsb16 = ar.v(21300, 16, CK, BF16)
    Brow = ar.v(21428, 1, CK, F32)
    dtW = ar.v(21684, 16, T, BF16)
    dAscr = [ar.v(22708, 128, CK, F32), ar.v(308, 128, CK, F32)]
    dBscr = [ar.v(22964, 128, CK, F32), ar.v(564, 128, CK, F32)]
    hscr = [ar.v(23220, 128, CK, F32), ar.v(832, 128, CK, F32)]
    Brow2 = [ar.v(21428, 1, CK, F32), ar.v(1088, 1, CK, F32)]
    dA_all = ([ar.v(25268 + i * 256, 128, CK, F32) for i in range(8)]
              + [ar.v(23476 + i * 256, 128, CK, F32) for i in range(4)]
              + [ar.v(1344 + i * 256, 128, CK, F32) for i in range(3)]
              + [ar.v(24500, 128, CK, F32)])
    XHC = [ar.v(23476 + i * 128, 128, CK, BF16) for i in range(8)]
    WSTG = ar.v(24500, 128, 512, F32)
    XPH = ar.v(25012, 128, 512, BF16)
    BUF2 = [ar.v(25268, 128, 2048, BF16),
            ar.v(26292, 128, 2048, BF16)]
    dtstg = ar.v(6452, 16, T, F32)
    P.dma("Q", dtstg, C.d["l3_dt"])
    P.v_copy(dtW, dtstg)
    P.memset(halo, 0.0)
    P.memset(hstate, 0.0)
    for ck in range(L3A_CKS):
        cs = slice(ck * CK, (ck + 1) * CK)
        emit_xhat_chunk(C, ck * CK, CK, XHC, P.take_bank(), P.take_bank())
        for mt in range(32):
            pr = mt // 2
            wb = BUF2[pr % 2]
            if mt % 2 == 0:
                P.dmaq(V(wb.ap.rearrange("p (a x) -> p a x", a=2), wb.regs),
                       C.dbf["l3_inproj"].ap[mt:mt + 2]
                       .rearrange("a p k j -> p a (k j)"))
            wsl = (mt % 2) * 1024
            psX = P.ps(P.take_bank(), 128, CK)
            for kt in range(8):
                P.t_mm(psX, V(wb.ap[:, wsl + kt * 128:wsl + (kt + 1) * 128], wb.regs),
                       XHC[kt], start=(kt == 0), stop=(kt == 7))
            if mt < 16:
                P.v_copy(V(xcbuf.ap[:, 0:3], xcbuf.regs),
                         V(halo.ap[:, mt * 3:mt * 3 + 3], halo.regs))
                P.v_ts(V(xcbuf.ap[:, 3:259], xcbuf.regs), psX,
                       C.l3_inproj_b[:, mt:mt + 1], None, ALU.add)
                P.v_copy(V(halo.ap[:, mt * 3:mt * 3 + 3], halo.regs),
                         V(xcbuf.ap[:, 256:259], xcbuf.regs))
                P.v_ts(dAscr[0], V(xcbuf.ap[:, 0:256], xcbuf.regs),
                       C.l3_convw[:, mt * 4:mt * 4 + 1], None, ALU.mult)
                for kk in range(1, 4):
                    P.v_stt(dAscr[0], V(xcbuf.ap[:, kk:kk + 256], xcbuf.regs),
                            C.l3_convw[:, mt * 4 + kk:mt * 4 + kk + 1],
                            dAscr[0], ALU.mult, ALU.add)
                P.s_act(xc_s[mt], dAscr[0], AF.Silu, bias=C.l3_convb[:, mt:mt + 1])
            else:
                P.s_act(z_s[mt - 16], psX, AF.Silu,
                        bias=C.l3_inproj_b[:, mt:mt + 1])
        P.dma("Q", WSTG, C.d["l3_xp_head"].rearrange("p k j -> p (k j)"))
        P.v_copy(XPH, WSTG, eng="S")
        psH = P.ps(P.take_bank(), 32, CK)
        for kt in range(16):
            P.t_mm(psH, V(XPH.ap[:, kt * 32:(kt + 1) * 32], XPH.regs),
                   xc_s[kt], start=(kt == 0), stop=(kt == 15))
        P.v_ts(dsB, psH, C.l3_xpb_head[0:32, :], None, ALU.add)
        P.v_copy(dsb16, V(dsB.ap[0:16, :], dsB.regs))
        for ut in range(16):
            src_w = C.dbf["l3_xp_u"].ap[ut].rearrange("p k j -> p (k j)")
            wb = BUF2[ut % 2]
            P.dmaq(wb, src_w)
            psU = P.ps(P.take_bank(), 128, CK)
            for kt in range(16):
                P.t_mm(psU, V(wb.ap[:, kt * 128:(kt + 1) * 128], wb.regs),
                       xc_s[kt], start=(kt == 0), stop=(kt == 15))
            P.v_ts(u[ut], psU, C.l3_xpb_u[:, ut:ut + 1], None, ALU.add)
        for ct in range(16):
            psD = P.ps(P.take_bank(), 128, CK)
            P.t_mm(psD, V(dtW.ap[:, ct * 128:(ct + 1) * 128], dtW.regs),
                   dsb16, True, True)
            pp = ct % 2
            P.s_act(dAscr[pp], psD, AF.Abs, bias=C.l3_dtb[:, ct:ct + 1])
            P.s_act(dBscr[pp], dAscr[pp], AF.Exp, scale=-1.0)
            P.s_act(dBscr[pp], dBscr[pp], AF.Ln, bias=C.onescol)
            P.s_act(hscr[pp], psD, AF.Relu, bias=C.l3_dtb[:, ct:ct + 1])
            P.v_tt(delta[ct], dBscr[pp], hscr[pp], ALU.add)
            P.v_tt(w[ct], delta[ct], u[ct], ALU.mult)
        P.dma("G", Brow2[0], V(dsB.ap[16:17, :], dsB.regs))
        P.t_mm(P.ps(6, 128, CK), C.onesrow[0:1, :], Brow2[0], True, True)
        for ct in range(16):
            P.s_act(dA_all[ct], delta[ct], AF.Exp, scale=-1.0)
        for s in range(L3A_SMAX):
            if s + 1 < L3A_SMAX:
                P.dma("G", Brow2[(s + 1) % 2],
                      V(dsB.ap[17 + s:18 + s, :], dsB.regs))
                P.t_mm(P.ps(6 + (s + 1) % 2, 128, CK), C.onesrow[0:1, :],
                       Brow2[(s + 1) % 2], True, True)
            psB = P.ps(6 + s % 2, 128, CK)
            for ct in range(16):
                pp = ct % 2
                P.v_tt(dBscr[pp], w[ct], psB, ALU.mult)
                hcol = V(hstate.ap[:, ct * 16 + s:ct * 16 + s + 1], hstate.regs)
                P.v_scan(hscr[pp], dA_all[ct], dBscr[pp], hcol)
                if s == 0:
                    P.v_copy(y[ct], hscr[pp])
                else:
                    P.v_tt(y[ct], y[ct], hscr[pp], ALU.add)
                P.v_copy(hcol, V(hscr[pp].ap[:, CK - 1:CK], hscr[pp].regs))
            if s + 1 < L3A_SMAX:
                for ct in range(16):
                    P.s_act(dA_all[ct], delta[ct], AF.Exp, scale=-(s + 2.0))
        for ct in range(16):
            P.v_stt(y[ct], u[ct], C.l3_D[:, ct:ct + 1], y[ct], ALU.mult, ALU.add)
            P.v_tt(yfin[ct], y[ct], z_s[ct], ALU.mult)
        for m in range(8):
            wb = BUF2[m % 2]
            P.dmaq(wb, C.dbf["l3_outproj"].ap[m].rearrange("p k j -> p (k j)"))
            psO = P.ps(P.take_bank(), 128, CK)
            for kt in range(16):
                P.t_mm(psO, V(wb.ap[:, kt * 128:(kt + 1) * 128], wb.regs),
                       yfin[kt], start=(kt == 0), stop=(kt == 15))
            P.v_stt(C.res[m][:, cs], psO, C.l3_outproj_b[:, m:m + 1],
                    C.res[m][:, cs], ALU.add, ALU.add)
    P.rr_banks = save_rr
    ar.consolidate()


def emit_l3f(C):
    P = C.P
    ar = C.ar
    nc = P.nc
    emit_ln(C)
    spill = V(nc.dram_tensor("spill_res", [D, T], F32).ap(), Reg(_Solo(), 0, 1))
    spill2 = V(nc.dram_tensor("spill_out", [D, T], F32).ap(), Reg(_Solo(), 0, 1))
    XH = [ar.v(i * 1024, 128, T, BF16) for i in range(8)]
    for c in range(4):
        emit_xhat_chunk(C, c * 512, 512,
                        [V(XH[i].ap[:, c * 512:(c + 1) * 512], XH[i].regs)
                         for i in range(8)],
                        P.take_bank(), P.take_bank())
    for i in range(8):
        P.dma("Q", spill[i * 128:(i + 1) * 128, :], C.res[i])
    hp = [C.resA.v(mi * 512, 128, 1024, BF16) for mi in range(32)]
    g1 = [ar.v(8192 + mi * 512, 128, 1024, BF16) for mi in range(32)]
    WBF = [ar.v(24576 + b * 512, 128, 1024, BF16) for b in range(2)]
    W2BF = [ar.v(25600 + b * 512, 128, 1024, BF16) for b in range(2)]
    HH = [ar.v(24576 + b * 1024, 128, 2048, BF16) for b in range(2)]
    for th in range(2):
        hs = [slice(th * 1024 + cc * 512, th * 1024 + cc * 512 + 512)
              for cc in range(2)]
        for mt in range(32):
            wb1 = WBF[mt % 2]
            wb2 = W2BF[mt % 2]
            P.dmaq(wb1, C.dbf["l3f_w1"].ap[mt].rearrange("p k j -> p (k j)"))
            P.dmaq(wb2, C.dbf["l3f_w2"].ap[mt].rearrange("p k j -> p (k j)"))
            for cc in range(2):
                psH = P.ps(P.take_bank(), 128, 512)
                psH2 = P.ps(P.take_bank(), 128, 512)
                for kt in range(8):
                    P.t_mm(psH, V(wb1.ap[:, kt * 128:(kt + 1) * 128], wb1.regs),
                           V(XH[kt].ap[:, hs[cc]], XH[kt].regs),
                           start=(kt == 0), stop=(kt == 7))
                for kt in range(8):
                    P.t_mm(psH2, V(wb2.ap[:, kt * 128:(kt + 1) * 128], wb2.regs),
                           V(XH[kt].ap[:, hs[cc]], XH[kt].regs),
                           start=(kt == 0), stop=(kt == 7))
                P.s_act(C.scr0, psH, AF.Silu, scale=C.l3f_s1[:, mt:mt + 1])
                P.v_ts(C.scr1, psH2, C.l3f_s2[:, mt:mt + 1], None, ALU.mult)
                P.v_tt(V(hp[mt].ap[:, cc * 512:(cc + 1) * 512], hp[mt].regs),
                       C.scr0, C.scr1, ALU.mult)
        for mt in range(32):
            bk0, bk1 = P.take_bank(), P.take_bank()
            for pc in range(2):
                wb = HH[pc % 2]
                P.dmaq(wb, C.dbf["l3f_h1"].ap[mt].rearrange("p k j -> p (k j)")
                       [:, pc * 2048:(pc + 1) * 2048])
                for cc in range(2):
                    psG = P.ps(bk0 if cc == 0 else bk1, 128, 512)
                    for kl in range(16):
                        kt = pc * 16 + kl
                        P.t_mm(psG, V(wb.ap[:, kl * 128:(kl + 1) * 128], wb.regs),
                               V(hp[kt].ap[:, cc * 512:(cc + 1) * 512], hp[kt].regs),
                               start=(kt == 0), stop=(kt == 31))
            for cc in range(2):
                psG = P.ps(bk0 if cc == 0 else bk1, 128, 512)
                P.s_act(V(g1[mt].ap[:, cc * 512:(cc + 1) * 512], g1[mt].regs),
                        psG, AF.Gelu, scale=C.l3f_sh1[:, mt:mt + 1])
        for mt in range(32):
            bk0, bk1 = P.take_bank(), P.take_bank()
            for pc in range(2):
                wb = HH[pc % 2]
                P.dmaq(wb, C.dbf["l3f_h2"].ap[mt].rearrange("p k j -> p (k j)")
                       [:, pc * 2048:(pc + 1) * 2048])
                for cc in range(2):
                    psG = P.ps(bk0 if cc == 0 else bk1, 128, 512)
                    for kl in range(16):
                        kt = pc * 16 + kl
                        P.t_mm(psG, V(wb.ap[:, kl * 128:(kl + 1) * 128], wb.regs),
                               V(g1[kt].ap[:, cc * 512:(cc + 1) * 512], g1[kt].regs),
                               start=(kt == 0), stop=(kt == 31))
            for cc in range(2):
                psG = P.ps(bk0 if cc == 0 else bk1, 128, 512)
                P.s_act(V(hp[mt].ap[:, cc * 512:(cc + 1) * 512], hp[mt].regs),
                        psG, AF.Gelu, scale=C.l3f_sh2[:, mt:mt + 1])
        for m in range(8):
            bk0, bk1 = P.take_bank(), P.take_bank()
            for pc in range(2):
                wb = HH[pc % 2]
                P.dmaq(wb, C.dbf["l3f_w3"].ap[m].rearrange("p k j -> p (k j)")
                       [:, pc * 2048:(pc + 1) * 2048])
                for cc in range(2):
                    psY = P.ps(bk0 if cc == 0 else bk1, 128, 512)
                    for kl in range(16):
                        kt = pc * 16 + kl
                        P.t_mm(psY, V(wb.ap[:, kl * 128:(kl + 1) * 128], wb.regs),
                               V(hp[kt].ap[:, cc * 512:(cc + 1) * 512], hp[kt].regs),
                               start=(kt == 0), stop=(kt == 31))
            for cc in range(2):
                psY = P.ps(bk0 if cc == 0 else bk1, 128, 512)
                P.v_ts(C.scr0, psY, C.l3f_s3[:, m:m + 1], None, ALU.mult)
                P.dma("G", C.scr1, spill[m * 128:(m + 1) * 128, hs[cc]])
                P.v_tt(C.scr0, C.scr0, C.scr1, ALU.add)
                P.dma("G", spill2[m * 128:(m + 1) * 128, hs[cc]], C.scr0)
    for i in range(8):
        P.dma("Q", C.res[i], spill2[i * 128:(i + 1) * 128, :])
    ar.consolidate()
    C.resA.consolidate()


EMITTERS = {"l0a": emit_l0a, "l0f": emit_l0f, "l1a": emit_l1a,
            "l1f": emit_l1f, "l2a": emit_l2a, "l2f": emit_l2f,
            "l3a": emit_l3a, "l3f": emit_l3f}

_PROG_CACHE = {}


def build_program(marrs, blocks):
    nc = bass.Bass("TRN2", target_bir_lowering=False)
    P = Prog(nc)
    C = setup_ctx(nc, P, marrs, blocks)
    load_block_statics(C, blocks)
    emit_convert(C, blocks)
    for i in range(8):
        P.dma("Q", C.res[i], C.xT[i * 128:(i + 1) * 128, :])
    mm_marks = [("start", 0)]
    for b in blocks:
        EMITTERS[b](C)
        C.resA.consolidate()
        C.ar.consolidate()
        mm_marks.append((b, P.mm_count))
    build_program.mm_marks = mm_marks
    for i in range(8):
        P.dma("Q", C.yT[i * 128:(i + 1) * 128, :], C.res[i])
    for qn in ("Q", "G", "S", "V"):
        for sl in range(P.NSLOT):
            key = f"{qn}#{sl}"
            P.raw_wait("Q", key, P.cnt[key])
    return P.build(), C.innames


def _get_prog(marrs, blocks):
    key = tuple(blocks)
    if key not in _PROG_CACHE:
        _PROG_CACHE[key] = build_program(marrs, blocks)
    return _PROG_CACHE[key]


def run_blocks(marrs, xts, blocks):
    nc, innames = _get_prog(marrs, blocks)
    in_maps = []
    for c in range(NCORE):
        d = {k: marrs[k] for k in innames if k != "xT"}
        d["xT"] = xts[c]
        in_maps.append(d)
    res = run_bass_kernel_spmd(nc, in_maps, core_ids=list(range(NCORE)))
    return [np.ascontiguousarray(np.asarray(res.results[c]["yT"]))
            for c in range(NCORE)]


def kernel(**inputs):
    m, xts = prep_inputs(inputs)
    groups = [ALL_BLOCKS] if FUSED else [[b] for b in ALL_BLOCKS]
    cur = xts
    for blocks in groups:
        cur = run_blocks(m, cur, blocks)
    out = np.stack([c.T for c in cur])
    return np.ascontiguousarray(out.astype(np.float32))


# version 09 (final)
# speedup vs baseline: 1.8998x; 1.5835x over previous
import numpy as np
import ml_dtypes
import concourse.bass as bass
import concourse.mybir as mybir
from concourse.bass_utils import run_bass_kernel_spmd

F32 = mybir.dt.float32
BF16 = mybir.dt.bfloat16
AF = mybir.ActivationFunctionType
ALU = mybir.AluOpType
AX = mybir.AxisListType.X

T = 2048
D = 1024
NCORE = 8
NPBF = ml_dtypes.bfloat16

ALL_BLOCKS = ["l0a", "l0f", "l1a", "l1f", "l2a", "l2f", "l3a", "l3f"]
L1A_PHASES = 7
L1A_CHUNKS = 8
L1A_THETA = 3
L3A_CKS = 8
L3A_SMAX = 16
FUSED = True
ARENA_N = 27648

ENGS = ("T", "V", "S", "Q", "G")


def _coltile(w):
    K, M = w.shape
    assert K % 128 == 0 and M % 128 == 0, (K, M)
    t = w.reshape(K // 128, 128, M // 128, 128)
    return np.ascontiguousarray(t.transpose(2, 1, 0, 3))


def _cols(v):
    n = v.shape[0] // 128
    return np.ascontiguousarray(v.reshape(n, 128).T)


def prep_inputs(inputs):
    f = lambda a: np.ascontiguousarray(np.asarray(a, np.float32))
    bf = lambda a: np.ascontiguousarray(np.asarray(a, NPBF))
    I = inputs
    m = {}
    m["c_ident"] = np.eye(128, dtype=np.float32)
    m["c_identb"] = bf(np.eye(128))
    m["c_onescol"] = np.ones((128, 1), np.float32)
    m["c_onescolb"] = bf(np.ones((128, 1)))
    m["c_onesrow"] = np.ones((96, 128), np.float32)
    m["c_onesrowb"] = bf(np.ones((96, 128)))
    hd = 64
    inv_freq = 1.0 / (10000.0 ** (np.arange(0, hd, 2, dtype=np.float64) / hd))
    ang = np.arange(T, dtype=np.float64)[None, :] * inv_freq[:, None]
    Cf = np.empty((64, T)); Sf = np.empty((64, T))
    Cf[0::2] = np.cos(ang); Cf[1::2] = np.cos(ang)
    Sf[0::2] = -np.sin(ang); Sf[1::2] = np.sin(ang)
    m["c_ropeC"] = bf(np.tile(Cf, (2, 1)))
    m["c_ropeS"] = bf(np.tile(Sf, (2, 1)))
    P = np.zeros((128, 128))
    for dd in range(128):
        P[dd + 1 if dd % 2 == 0 else dd - 1, dd] = 1.0
    m["c_swap"] = bf(P)
    st = np.zeros((128, 896))
    for i in range(128):
        st[i, i + 384:] = 1.0
    m["c_stair"] = bf(st)
    hs = np.zeros((128, 128)); hs[0:64, 0:64] = 1.0; hs[64:128, 64:128] = 1.0
    m["c_hsum"] = bf(hs)
    ep = np.zeros((8, 128), np.float32)
    for e in range(8):
        ep[e, e * 16:(e + 1) * 16] = 1.0
    m["c_erpat"] = ep

    for nm in ("wq", "wk", "wo"):
        m[f"l0_{nm}"] = _coltile(f(I[f"L0_attn_{nm}"]))
    m["l0_wv_rows"] = f(I["L0_attn_wv"])
    m["l0f_wg"] = np.ascontiguousarray(
        f(I["L0_ffn_Wg"]).reshape(8, 128, 8).transpose(1, 0, 2).reshape(128, 64))
    m["l0f_a"] = np.ascontiguousarray(
        f(I["L0_ffn_A"]).transpose(1, 0, 2).reshape(8, 128, 128)
        .transpose(1, 0, 2).reshape(128, 1024))
    m["l0f_b"] = f(I["L0_ffn_B"]).reshape(128, D)
    for nm in ("Wq", "Wk", "Wo"):
        m[f"l1_{nm.lower()}"] = _coltile(np.ascontiguousarray(f(I[f"L1_attn_{nm}"]).T))
    m["l1_wv_rows"] = np.ascontiguousarray(f(I["L1_attn_Wv"]).T)
    m["l1_bq"] = _cols(f(I["L1_attn_bq"]))
    m["l1_bk"] = _cols(f(I["L1_attn_bk"]))
    m["l1_bv_row"] = f(I["L1_attn_bv"]).reshape(1, D)
    m["l1_bo"] = _cols(f(I["L1_attn_bo"]))
    m["l1f_wg"] = np.ascontiguousarray(
        f(I["L1_ffn_gate_w"]).reshape(8, 128, 8).transpose(1, 0, 2).reshape(128, 64))
    m["l1f_w1"] = np.stack([_coltile(f(I["L1_ffn_w1"][e])) for e in range(8)])
    m["l1f_w2"] = np.stack([_coltile(f(I["L1_ffn_w2"][e])) for e in range(8)])
    m["l1f_b1"] = np.stack([_cols(f(I["L1_ffn_b1"][e])) for e in range(8)])
    m["l1f_b2"] = np.stack([_cols(f(I["L1_ffn_b2"][e])) for e in range(8)])
    for nm in ("Wq", "Wk", "Wv", "Wo"):
        m[f"l2_{nm.lower()}"] = _coltile(f(I[f"L2_attn_{nm}"]))
    for nm in ("bq", "bk", "bv", "bo"):
        m[f"l2_{nm}"] = _cols(f(I[f"L2_attn_{nm}"]))
    m["l2f_wg"] = np.ascontiguousarray(
        f(I["L2_ffn_Wg"]).T.reshape(8, 128, 8).transpose(1, 0, 2).reshape(128, 64))
    m["l2f_w1"] = np.stack([_coltile(f(I["L2_ffn_w1"][e])) for e in range(8)])
    m["l2f_w2"] = np.stack([_coltile(f(I["L2_ffn_w2"][e])) for e in range(8)])
    m["l2f_b1"] = np.stack([_cols(f(I["L2_ffn_b1"][e])) for e in range(8)])
    m["l2f_b2"] = np.stack([_cols(f(I["L2_ffn_b2"][e])) for e in range(8)])
    m["l3_inproj"] = _coltile(f(I["L3_attn_in_proj_w"]))
    m["l3_inproj_b"] = _cols(f(I["L3_attn_in_proj_b"]))
    m["l3_convw"] = np.ascontiguousarray(
        f(I["L3_attn_conv_w"]).reshape(16, 128, 4).transpose(1, 0, 2).reshape(128, 64))
    m["l3_convb"] = _cols(f(I["L3_attn_conv_b"]))
    xp = f(I["L3_attn_x_proj_w"])
    m["l3_xp_head"] = np.ascontiguousarray(xp[:, 0:32].reshape(16, 128, 32).transpose(1, 0, 2))
    m["l3_xp_u"] = _coltile(np.ascontiguousarray(xp[:, 32:2080]))
    xb = f(I["L3_attn_x_proj_b"])
    m["l3_xpb_head"] = np.ascontiguousarray(xb[0:32].reshape(32, 1))
    m["l3_xpb_u"] = _cols(xb[32:2080])
    m["l3_dt"] = f(I["L3_attn_dt_proj_w"])
    m["l3_dtb"] = _cols(f(I["L3_attn_dt_proj_b"]))
    m["l3_D"] = _cols(f(I["L3_attn_D"]))
    m["l3_outproj"] = _coltile(f(I["L3_attn_out_proj_w"]))
    m["l3_outproj_b"] = _cols(f(I["L3_attn_out_proj_b"]))
    for nm in ("w1", "w2", "h1", "h2", "w3"):
        m[f"l3f_{nm}"] = _coltile(np.ascontiguousarray(f(I[f"L3_ffn_{nm}_w"]).T))
    m["l3f_s1"] = _cols(f(I["L3_ffn_w1_s"]))
    m["l3f_s2"] = _cols(f(I["L3_ffn_w2_s"]))
    m["l3f_sh1"] = _cols(f(I["L3_ffn_h1_s"]))
    m["l3f_sh2"] = _cols(f(I["L3_ffn_h2_s"]))
    m["l3f_s3"] = _cols(f(I["L3_ffn_w3_s"]))

    x = f(I["L0_attn_x"])
    xts = [np.ascontiguousarray(x[c].T) for c in range(NCORE)]
    return m, xts


class Reg:
    __slots__ = ("arena", "lo", "hi", "writers", "readers", "excl_read")

    def __init__(self, arena, lo, hi, excl_read=False):
        self.arena, self.lo, self.hi = arena, lo, hi
        self.writers = {}
        self.readers = {}
        self.excl_read = excl_read


class V:
    __slots__ = ("ap", "regs")

    def __init__(self, ap, regs):
        self.ap = ap
        self.regs = regs if isinstance(regs, list) else [regs]

    def __getitem__(self, idx):
        return V(self.ap[idx], self.regs)


class Arena:
    def __init__(self, P, name, f32_elems):
        self.P = P
        self.name = name
        self.tf = P.nc.alloc_sbuf_tensor(name, [128, f32_elems], F32)
        self.tb = self.tf.bitcast(BF16)
        self.size = f32_elems * 4
        self.regs = []
        self.persist = []

    def v(self, off_f32, parts, free, dtype=F32, persist=False):
        esz = 4 if dtype == F32 else 2
        lo = off_f32 * 4
        hi = lo + free * esz
        assert hi <= self.size, (self.name, off_f32, free, dtype)
        r = Reg(self, lo, hi)
        self.regs.append(r)
        if persist:
            self.persist.append(r)
        if dtype == F32:
            ap = self.tf[0:parts, off_f32:off_f32 + free]
        else:
            ap = self.tb[0:parts, off_f32 * 2:off_f32 * 2 + free]
        return V(ap, r)

    def consolidate(self):
        for r in self.persist:
            r.writers = {}
            r.readers = {}
        self.regs = list(self.persist)
        self.P.set_floor()


class _Solo:
    regs = None


class Prog:
    NSLOT = 8

    def __init__(self, nc, same_engine_waits=True):
        self.nc = nc
        keys = list(ENGS)
        for qn in ("Q", "G", "S", "V"):
            keys += [f"{qn}#{i}" for i in range(self.NSLOT)]
        self.sem = {e: nc.alloc_semaphore(f"sem_{e.replace('#','_')}") for e in keys}
        self.cnt = {e: 0 for e in keys}
        self.q = {e: [] for e in ENGS}
        self.waited = {e: {e2: 0 for e2 in keys} for e in ENGS}
        self._slot_rr = {"Q": 0, "G": 0, "S": 0, "V": 0}
        self.floor = {k: 0 for k in keys}
        self.sew = same_engine_waits
        self.psum = []
        self.psreg = []
        for i in range(8):
            self.psum.append(nc.alloc_psum_tensor(f"ps{i}", [128, 512], F32))
            self.psreg.append(Reg(_Solo(), 0, 1, excl_read=True))
        self._bank_rr = 0
        self._dma_rr = 0
        self.rr_banks = [0, 1, 2, 3, 4, 5, 6]

    def ps(self, i, parts=128, free=512, off=0):
        return V(self.psum[i][0:parts, off:off + free], self.psreg[i])

    def take_bank(self, n=1):
        out = []
        for _ in range(n):
            out.append(self.rr_banks[self._bank_rr % len(self.rr_banks)])
            self._bank_rr += 1
        return out if n > 1 else out[0]

    def stat(self, name, shape, dtype):
        t = self.nc.alloc_sbuf_tensor(name, shape, dtype)
        return V(t[:, :], Reg(_Solo(), 0, 1)), t

    def _expand(self, regs):
        out = []
        for r in regs:
            ar = r.arena
            if isinstance(ar, Arena):
                for o in ar.regs:
                    if r.lo < o.hi and o.lo < r.hi:
                        out.append(o)
            else:
                out.append(r)
        return out

    def op(self, eng, emit, reads=(), writes=(), inc=1, key=None):
        key = key or eng
        rr = [r for x in reads for r in (x.regs if isinstance(x, V) else [x])]
        ww = [r for x in writes for r in (x.regs if isinstance(x, V) else [x])]
        err = self._expand(rr)
        eww = self._expand(ww)
        deps = {}
        for e, c in self.floor.items():
            if c:
                deps[e] = c
        for r in err:
            for e, c in r.writers.items():
                deps[e] = max(deps.get(e, 0), c)
            if r.excl_read:
                for e, c in r.readers.items():
                    if e != eng:
                        deps[e] = max(deps.get(e, 0), c)
        for w in eww:
            for e, c in w.writers.items():
                deps[e] = max(deps.get(e, 0), c)
            for e, c in w.readers.items():
                deps[e] = max(deps.get(e, 0), c)
        waits = []
        for e, c in deps.items():
            if e == eng and not self.sew:
                continue
            if self.waited[eng][e] >= c:
                continue
            self.waited[eng][e] = c
            waits.append((self.sem[e], c))
        sem_self = self.sem[key]
        cnt_after = self.cnt[key] + inc

        def closure(E, waits=waits, emit=emit, inc=inc):
            for s, c in waits:
                E.wait_ge(s, c)
            ins = emit(E)
            if inc and ins is not None:
                ins.then_inc(sem_self, inc)

        self.q[eng].append(closure)
        if inc:
            self.cnt[key] = cnt_after
            for r in eww:
                r.writers = {key: cnt_after}
                r.readers = {}
            for r in err:
                r.readers[key] = cnt_after
        return cnt_after

    def set_floor(self):
        self.floor = dict(self.cnt)

    def raw_wait(self, eng, sem_key, cnt):
        if cnt <= 0 or self.waited[eng][sem_key] >= cnt:
            return
        self.waited[eng][sem_key] = cnt
        s = self.sem[sem_key]
        self.q[eng].append(lambda E, s=s, c=cnt: E.wait_ge(s, c))

    mm_count = 0

    def t_mm(self, bank_v, lhsT, rhs, start, stop):
        self.mm_count = getattr(self, 'mm_count', 0) + 1
        return self.op(
            "T",
            lambda E, o=bank_v.ap, l=lhsT.ap, r=rhs.ap, st=start, sp=stop:
                E.matmul(o, l, r, start=st, stop=sp),
            reads=[lhsT, rhs], writes=[bank_v] if (start or stop) else [], inc=1)

    def t_transpose(self, bank_v, in_v, ident_v):
        return self.op(
            "T",
            lambda E, o=bank_v.ap, i=in_v.ap, d=ident_v.ap: E.transpose(o, i, d),
            reads=[in_v, ident_v], writes=[bank_v], inc=1)

    def s_act(self, out, in_, func, bias=0.0, scale=1.0, accum=None):
        bap = bias.ap if isinstance(bias, V) else bias
        sap = scale.ap if isinstance(scale, V) else scale
        reads = [in_] + [x for x in (bias, scale) if isinstance(x, V)]
        writes = [out] + ([accum] if accum is not None else [])
        acc = accum.ap if accum is not None else None
        return self.op(
            "S",
            lambda E, o=out.ap, i=in_.ap, ff=func, b=bap, s=sap, a=acc:
                E.activation(o, i, ff, bias=b, scale=s, accum_out=a),
            reads=reads, writes=writes, inc=1)

    def s_copy(self, out, in_):
        return self.op("S", lambda E, o=out.ap, i=in_.ap: E.copy(o, i),
                       reads=[in_], writes=[out], inc=1)

    def v_tt(self, out, a, b, alu, eng="V"):
        return self.op(eng, lambda E, o=out.ap, x=a.ap, y=b.ap, u=alu:
                       E.tensor_tensor(o, x, y, u),
                       reads=[a, b], writes=[out], inc=1)

    def v_ts(self, out, in_, s1, s2, op0, op1=ALU.bypass, eng="V"):
        s1a = s1.ap if isinstance(s1, V) else s1
        s2a = s2.ap if isinstance(s2, V) else s2
        reads = [in_] + [x for x in (s1, s2) if isinstance(x, V)]
        return self.op(eng, lambda E, o=out.ap, i=in_.ap, a=s1a, b=s2a, u0=op0, u1=op1:
                       E.tensor_scalar(o, i, a, b, u0, u1),
                       reads=reads, writes=[out], inc=1)

    def v_stt(self, out, in0, scalar, in1, op0, op1, eng="V"):
        sa = scalar.ap if isinstance(scalar, V) else scalar
        reads = [in0, in1] + ([scalar] if isinstance(scalar, V) else [])
        return self.op(eng, lambda E, o=out.ap, i=in0.ap, s=sa, j=in1.ap, u0=op0, u1=op1:
                       E.scalar_tensor_tensor(o, i, s, j, u0, u1),
                       reads=reads, writes=[out], inc=1)

    def v_scan(self, out, d0, d1, init):
        ia = init.ap if isinstance(init, V) else init
        reads = [d0, d1] + ([init] if isinstance(init, V) else [])
        return self.op("V", lambda E, o=out.ap, a=d0.ap, b=d1.ap, i=ia:
                       E.tensor_tensor_scan(o, a, b, i, ALU.mult, ALU.add),
                       reads=reads, writes=[out], inc=1)

    def v_red(self, out, in_, alu=ALU.max, negate=False, eng="V"):
        return self.op(eng, lambda E, o=out.ap, i=in_.ap, u=alu, n=negate:
                       E.tensor_reduce(o, i, AX, u, negate=n),
                       reads=[in_], writes=[out], inc=1)

    def v_copy(self, out, in_, eng="V"):
        if eng == "S":
            return self.op(eng, lambda E, o=out.ap, i=in_.ap: E.copy(o, i),
                           reads=[in_], writes=[out], inc=1)
        return self.op(eng, lambda E, o=out.ap, i=in_.ap: E.tensor_copy(o, i),
                       reads=[in_], writes=[out], inc=1)

    def v_recip(self, out, in_):
        return self.op("V", lambda E, o=out.ap, i=in_.ap: E.reciprocal(o, i),
                       reads=[in_], writes=[out], inc=1)

    def memset(self, out, val, eng="V"):
        return self.op(eng, lambda E, o=out.ap, v=val: E.memset(o, v),
                       reads=[], writes=[out], inc=1)

    def dmaq(self, dst, src):
        return self.dma("Q", dst, src)

    def dma(self, qeng, dst, src):
        reads = [src] if isinstance(src, V) else []
        writes = [dst] if isinstance(dst, V) else []
        da = dst.ap if isinstance(dst, V) else dst
        sa = src.ap if isinstance(src, V) else src
        slot = self._slot_rr[qeng] % self.NSLOT
        self._slot_rr[qeng] += 1
        key = f"{qeng}#{slot}"
        self.raw_wait(qeng, key, self.cnt[key])
        return self.op(qeng, lambda E, d=da, s=sa: E.dma_start(d, s),
                       reads=reads, writes=writes, inc=16, key=key)

    def build(self):
        nc = self.nc
        with nc.Block() as blk:
            blk.tensor(lambda E: [fn(E) for fn in self.q["T"]])
            blk.vector(lambda E: [fn(E) for fn in self.q["V"]])
            blk.scalar(lambda E: [fn(E) for fn in self.q["S"]])
            blk.sync(lambda E: [fn(E) for fn in self.q["Q"]])
            blk.gpsimd(lambda E: [fn(E) for fn in self.q["G"]])
        return nc


class Ctx:
    pass


PREFIX = {"l0a": "l0_", "l0f": "l0f_", "l1a": "l1_", "l1f": "l1f_",
          "l2a": "l2_", "l2f": "l2f_", "l3a": "l3_", "l3f": "l3f_"}


def setup_ctx(nc, P, marrs, blocks):
    C = Ctx()
    C.P = P
    C.d = {}

    def din(name):
        a = marrs[name]
        dt = BF16 if a.dtype == NPBF else F32
        C.d[name] = nc.dram_tensor(name, list(a.shape), dt, kind="ExternalInput").ap()

    names = set()
    for k in marrs:
        if k.startswith("c_"):
            names.add(k)
        else:
            for b in blocks:
                if k.startswith(PREFIX[b]):
                    names.add(k)
    for k in sorted(names):
        din(k)
    C.innames = sorted(names) + ["xT"]
    C.xT = nc.dram_tensor("xT", [D, T], F32, kind="ExternalInput").ap()
    C.yT = nc.dram_tensor("yT", [D, T], F32, kind="ExternalOutput").ap()

    C.resA = Arena(P, "resA", 8 * T)
    C.res = [C.resA.v(i * T, 128, T, F32, persist=True) for i in range(8)]
    C.ar = Arena(P, "arena", ARENA_N)

    def stat_load(name, parts, free, dtype=F32):
        vv, _ = P.stat("sb_" + name, [parts, free], dtype)
        P.dma("Q", vv, C.d[name])
        return vv

    C.ident = stat_load("c_ident", 128, 128)
    C.identb = stat_load("c_identb", 128, 128, BF16)
    C.onescol = stat_load("c_onescol", 128, 1)
    C.onescolb = stat_load("c_onescolb", 128, 1, BF16)
    C.onesrow = stat_load("c_onesrow", 96, 128)
    C.onesrowb = stat_load("c_onesrowb", 96, 128, BF16)
    C.rowAB, rowt = P.stat("rowAB", [96, 6144], F32)
    C.rowABb = rowt.bitcast(BF16)
    C.eps, _ = P.stat("c_eps", [96, 1], F32)
    P.memset(C.eps, 1e-5)
    C.scr0, _ = P.stat("scr0", [128, 512], F32)
    C.scr1, _ = P.stat("scr1", [128, 512], F32)
    C.r_mean = V(C.rowAB.ap[0:1, 0:T], C.rowAB.regs)
    C.r_rstd = V(C.rowAB.ap[0:1, T:2 * T], C.rowAB.regs)
    C.r_sumsq = V(C.rowAB.ap[0:1, 2 * T:3 * T], C.rowAB.regs)
    C.stat_load = stat_load
    return C


def load_block_statics(C, blocks):
    sl = C.stat_load
    if "l1a" in blocks:
        C.l1_bq = sl("l1_bq", 128, 8)
        C.l1_bk = sl("l1_bk", 128, 8)
        C.l1_bo = sl("l1_bo", 128, 8)
    if "l1f" in blocks:
        C.l1f_wgf = sl("l1f_wg", 128, 64)
        C.l1f_b1 = sl("l1f_b1", 128, 256)
        C.l1f_b2 = sl("l1f_b2", 128, 64)
    if "l2a" in blocks:
        for nm in ("bq", "bk", "bv", "bo"):
            setattr(C, "l2_" + nm, sl("l2_" + nm, 128, 8))
    if "l2f" in blocks:
        C.l2f_wgf = sl("l2f_wg", 128, 64)
        C.l2f_b1 = sl("l2f_b1", 128, 256)
        C.l2f_b2 = sl("l2f_b2", 128, 64)
    if "l3a" in blocks:
        C.l3_inproj_b = sl("l3_inproj_b", 128, 32)
        C.l3_convw = sl("l3_convw", 128, 64)
        C.l3_convb = sl("l3_convb", 128, 16)
        C.l3_xpb_head = sl("l3_xpb_head", 32, 1)
        C.l3_xpb_u = sl("l3_xpb_u", 128, 16)
        C.l3_dtb = sl("l3_dtb", 128, 16)
        C.l3_D = sl("l3_D", 128, 16)
        C.l3_outproj_b = sl("l3_outproj_b", 128, 8)
    if "l3f" in blocks:
        for nm in ("s1", "s2", "sh1", "sh2"):
            setattr(C, "l3f_" + nm, sl("l3f_" + nm, 128, 32))
        C.l3f_s3 = sl("l3f_s3", 128, 8)


CONVERT_ORDER = [
    ("ct", "l0_wq"), ("ct", "l0_wo"),
    ("ct", "l1_wk"), ("rw", "l1_wv_rows"), ("ct", "l1_wq"), ("ct", "l1_wo"),
    ("ex", "l1f_w1"), ("ex", "l1f_w2"),
    ("ct", "l2_wq"), ("ct", "l2_wk"), ("ct", "l2_wv"), ("ct", "l2_wo"),
    ("ex", "l2f_w1"), ("ex", "l2f_w2"),
    ("ct", "l3_inproj"), ("ct", "l3_xp_u"), ("ct", "l3_outproj"),
    ("ct", "l3f_w1"), ("ct", "l3f_w2"), ("ct", "l3f_h1"), ("ct", "l3f_h2"),
    ("ct", "l3f_w3")]


def emit_convert(C, blocks):
    P = C.P
    ar = C.ar
    nc = P.nc
    C.dbf = {}
    items = []

    def scratch(name, shape):
        t = nc.dram_tensor(name + "_bf", list(shape), BF16)
        vv = V(t.ap(), Reg(_Solo(), 0, 1))
        C.dbf[name] = vv
        return vv

    for kind, name in CONVERT_ORDER:
        if name not in C.d:
            continue
        sh = C.d[name].shape
        dst = scratch(name, sh)
        if kind == "ct":
            nmt, _, ktn, _ = sh
            for mt in range(nmt):
                items.append((C.d[name][mt].rearrange("p k j -> p (k j)"),
                              dst.ap[mt].rearrange("p k j -> p (k j)"), ktn * 128))
        elif kind == "rw":
            for kt in range(sh[0] // 128):
                items.append((C.d[name][kt * 128:(kt + 1) * 128, :],
                              dst.ap[kt * 128:(kt + 1) * 128, :], sh[1]))
        else:
            for e in range(sh[0]):
                for mt in range(sh[1]):
                    items.append((C.d[name][e, mt].rearrange("p k j -> p (k j)"),
                                  dst.ap[e, mt].rearrange("p k j -> p (k j)"),
                                  sh[3] * 128))
    NI, NO = 3, 2
    inb = [C.resA.v(i * 4096, 128, 4096, F32) for i in range(NI)]
    outb = [C.resA.v(NI * 4096 + i * 2048, 128, 4096, BF16) for i in range(NO)]
    n = len(items)
    for k in range(min(NI - 1, n)):
        sap, dap, X = items[k]
        P.dma("G" if k % 2 == 0 else "S",
              V(inb[k % NI].ap[:, 0:X], inb[k % NI].regs), sap)
    for k in range(n):
        sap, dap, X = items[k]
        kin = k + NI - 1
        if kin < n:
            s2, d2, X2 = items[kin]
            P.dma("G" if kin % 2 == 0 else "S",
                  V(inb[kin % NI].ap[:, 0:X2], inb[kin % NI].regs), s2)
        P.v_copy(V(outb[k % NO].ap[:, 0:X], outb[k % NO].regs),
                 V(inb[k % NI].ap[:, 0:X], inb[k % NI].regs), eng="V")
        P.dma("S" if k % 2 == 0 else "G", dap,
              V(outb[k % NO].ap[:, 0:X], outb[k % NO].regs))
    C.resA.consolidate()


def emit_ln(C):
    P = C.P
    b0, b1 = P.take_bank(), P.take_bank()
    for c in range(4):
        cs = slice(c * 512, (c + 1) * 512)
        psS = P.ps(b0, 1, 512)
        psQ = P.ps(b1, 1, 512)
        for i in range(8):
            P.t_mm(psS, C.onescol, C.res[i][:, cs], start=(i == 0), stop=(i == 7))
        for i in range(8):
            sq = C.scr0 if i % 2 == 0 else C.scr1
            P.s_act(sq, C.res[i][:, cs], AF.Square)
            P.t_mm(psQ, C.onescol, sq, start=(i == 0), stop=(i == 7))
        P.s_copy(V(C.r_mean.ap[:, cs], C.r_mean.regs), psS)
        P.s_copy(V(C.r_sumsq.ap[:, cs], C.r_sumsq.regs), psQ)
    P.v_ts(C.r_mean, C.r_mean, 1.0 / D, None, ALU.mult)
    P.v_ts(C.r_sumsq, C.r_sumsq, 1.0 / D, None, ALU.mult)
    P.v_tt(C.r_rstd, C.r_mean, C.r_mean, ALU.mult)
    P.v_tt(C.r_sumsq, C.r_sumsq, C.r_rstd, ALU.subtract)
    P.s_act(C.r_sumsq, C.r_sumsq, AF.Ln, bias=C.eps[0:1, :])
    P.s_act(C.r_rstd, C.r_sumsq, AF.Exp, scale=-0.5)


def emit_xhat_chunk(C, t0, width, dst_tiles, bm0, bm1, tmp=None):
    P = C.P
    cs = slice(t0, t0 + width)
    psM = P.ps(bm0, 128, width)
    psR = P.ps(bm1, 128, width)
    P.t_mm(psM, C.onesrow[0:1, :], V(C.r_mean.ap[:, cs], C.r_mean.regs), True, True)
    P.t_mm(psR, C.onesrow[0:1, :], V(C.r_rstd.ap[:, cs], C.r_rstd.regs), True, True)
    tmp = tmp if tmp is not None else C.scr0
    for i in range(8):
        tv = V(tmp.ap[:, 0:width], tmp.regs)
        P.v_tt(tv, C.res[i][:, cs], psM, ALU.subtract)
        P.v_tt(dst_tiles[i], tv, psR, ALU.mult)


def stream_coltile(C, name, mt, kt_n, stage_f32, dst_bf16, cast_eng="S"):
    P = C.P
    P.dmaq(dst_bf16, C.dbf[name].ap[mt].rearrange("p k j -> p (k j)"))


def emit_l0a(C):
    P = C.P
    ar = C.ar
    emit_ln(C)
    KR = [ar.v(2048 + i * 1024, 128, T, BF16) for i in range(8)]
    VT = ar.v(10240, 128, 16 * 1024, BF16)
    WST = ar.v(22528, 128, 1024, F32)
    ropeC = ar.v(23552, 128, T, BF16)
    ropeS = ar.v(24576, 128, T, BF16)
    stair = ar.v(25600, 128, 896, BF16)
    swapb = ar.v(26048, 128, 128, BF16)
    P.dma("Q", ropeC, C.d["c_ropeC"])
    P.dma("Q", ropeS, C.d["c_ropeS"])
    P.dma("Q", stair, C.d["c_stair"])
    P.dma("Q", swapb, C.d["c_swap"])

    rsc1 = ar.v(26112, 128, 256, F32)
    rsc2 = ar.v(26368, 128, 256, F32)

    def rope_tile(src_bf, psb, dst, tslice):
        w = tslice.stop - tslice.start
        psw = P.ps(psb, 128, w)
        P.t_mm(psw, swapb, src_bf, True, True)
        P.v_tt(V(rsc1.ap[:, 0:w], rsc1.regs), src_bf,
               V(ropeC.ap[:, tslice], ropeC.regs), ALU.mult)
        P.v_tt(V(rsc2.ap[:, 0:w], rsc2.regs), psw,
               V(ropeS.ap[:, tslice], ropeS.regs), ALU.mult)
        P.v_tt(dst, V(rsc1.ap[:, 0:w], rsc1.regs),
               V(rsc2.ap[:, 0:w], rsc2.regs), ALU.add)

    WKB = [ar.v(18560 + m * 512, 128, 1024, BF16) for m in range(8)]
    for m in range(8):
        P.dma("Q", WST, C.d["l0_wk"][m].rearrange("p k j -> p (k j)"))
        P.v_copy(WKB[m], WST, eng="S" if m % 2 == 0 else "V")
    qtmp = ar.v(26624, 128, 512, BF16)
    for c in range(4):
        cs = slice(c * 512, (c + 1) * 512)
        XHC = [ar.v(i * 256, 128, 512, BF16) for i in range(8)]
        emit_xhat_chunk(C, c * 512, 512, XHC, P.take_bank(), P.take_bank())
        for m in range(8):
            psK = P.ps(P.take_bank(), 128, 512)
            for kt in range(8):
                P.t_mm(psK, V(WKB[m].ap[:, kt * 128:(kt + 1) * 128], WKB[m].regs),
                       XHC[kt], start=(kt == 0), stop=(kt == 7))
            P.s_copy(qtmp, psK)
            for half in range(2):
                hs = slice(c * 512 + half * 256, c * 512 + half * 256 + 256)
                rope_tile(V(qtmp.ap[:, half * 256:half * 256 + 256], qtmp.regs),
                          P.take_bank(),
                          V(KR[m].ap[:, hs], KR[m].regs), hs)

    WVB = [ar.v(18560 + kt * 512, 128, 1024, BF16) for kt in range(8)]
    for kt in range(8):
        P.dma("Q", WST, C.d["l0_wv_rows"][kt * 128:(kt + 1) * 128, :])
        P.v_copy(WVB[kt], WST, eng="S" if kt % 2 == 0 else "V")
    for c in range(4):
        XHC = [ar.v(i * 256, 128, 512, BF16) for i in range(8)]
        emit_xhat_chunk(C, c * 512, 512, XHC, P.take_bank(), P.take_bank())
        for tloc in range(4):
            tt = c * 4 + tloc
            for ec in range(2):
                psV = P.ps(P.take_bank(), 128, 512)
                for kt in range(8):
                    P.t_mm(psV,
                           V(XHC[kt].ap[:, tloc * 128:(tloc + 1) * 128], XHC[kt].regs),
                           V(WVB[kt].ap[:, ec * 512:(ec + 1) * 512], WVB[kt].regs),
                           start=(kt == 0), stop=(kt == 7))
                P.s_copy(V(VT.ap[:, tt * 1024 + ec * 512:tt * 1024 + ec * 512 + 512],
                           VT.regs), psV)

    otmp = ar.v(26880, 128, 256, F32)
    for c in range(8):
        cw = 256
        cs = slice(c * cw, (c + 1) * cw)
        XHC = [ar.v(i * 128, 128, cw, BF16) for i in range(8)]
        QR = [ar.v(1024 + i * 128, 128, cw, BF16) for i in range(8)]
        ATT = [ar.v(18432 + tt * 128, 128, cw, BF16) for tt in range(16)]
        OUTC = [ar.v(20480 + i * 128, 128, cw, BF16) for i in range(8)]
        WQB = [ar.v(21504 + b * 512, 128, 1024, BF16) for b in range(2)]
        WOB = [ar.v(22528 + b * 512, 128, 1024, BF16) for b in range(2)]
        emit_xhat_chunk(C, c * cw, cw, XHC, P.take_bank(), P.take_bank())
        for m in range(8):
            wb = WQB[m % 2]
            stream_coltile(C, "l0_wq", m, 8, WST, wb)
            psQ = P.ps(P.take_bank(), 128, cw)
            for kt in range(8):
                P.t_mm(psQ, V(wb.ap[:, kt * 128:(kt + 1) * 128], wb.regs),
                       XHC[kt], start=(kt == 0), stop=(kt == 7))
            P.s_copy(V(qtmp.ap[:, 0:cw], qtmp.regs), psQ)
            rope_tile(V(qtmp.ap[:, 0:cw], qtmp.regs), P.take_bank(), QR[m], cs)
        ndiag = 2 * (c + 1)
        for h in range(16):
            mt = h // 2
            rlo = (h % 2) * 64
            bS1, bS2 = P.take_bank(), P.take_bank()
            bO, bR = P.take_bank(), P.take_bank()
            psO = V(P.psum[bO][rlo:rlo + 64, 0:cw], P.psreg[bO])
            psR = V(P.psum[bR][rlo:rlo + 1, 0:cw], P.psreg[bR])
            for tt in range(ndiag + 1):
                if tt < ndiag:
                    psS = P.ps(bS1 if tt % 2 == 0 else bS2, 128, cw)
                    P.t_mm(psS,
                           V(KR[mt].ap[rlo:rlo + 64, tt * 128:(tt + 1) * 128],
                             KR[mt].regs),
                           V(QR[mt].ap[rlo:rlo + 64, :], QR[mt].regs), True, True)
                    P.s_act(ATT[tt], psS, AF.Exp, scale=0.125)
                    if tt >= 2 * c:
                        delta = tt * 128 - c * cw
                        P.v_tt(ATT[tt], ATT[tt],
                               V(stair.ap[:, 384 - delta:384 - delta + cw],
                                 stair.regs), ALU.mult)
                if tt > 0:
                    pt = tt - 1
                    P.t_mm(psO,
                           V(VT.ap[:, pt * 1024 + h * 64:pt * 1024 + h * 64 + 64],
                             VT.regs),
                           ATT[pt], start=(pt == 0), stop=(pt == ndiag - 1))
                    P.t_mm(psR, C.onescolb, ATT[pt],
                           start=(pt == 0), stop=(pt == ndiag - 1))
            crow = V(C.rowAB.ap[rlo:rlo + 1, 2 * T:2 * T + cw], C.rowAB.regs)
            P.v_recip(crow, psR)
            bC = P.take_bank()
            psC = V(P.psum[bC][rlo:rlo + 64, 0:cw], P.psreg[bC])
            P.t_mm(psC, V(C.onesrow.ap[rlo:rlo + 1, 0:64], C.onesrow.regs),
                   crow, True, True)
            ot = V(otmp.ap[rlo:rlo + 64, :], otmp.regs)
            P.s_copy(ot, psO)
            P.v_tt(V(OUTC[mt].ap[rlo:rlo + 64, :], OUTC[mt].regs), ot, psC, ALU.mult)
        for m in range(8):
            wb = WOB[m % 2]
            stream_coltile(C, "l0_wo", m, 8, WST, wb)
            psY = P.ps(P.take_bank(), 128, cw)
            for kt in range(8):
                P.t_mm(psY, V(wb.ap[:, kt * 128:(kt + 1) * 128], wb.regs),
                       OUTC[kt], start=(kt == 0), stop=(kt == 7))
            P.v_tt(C.res[m][:, cs], C.res[m][:, cs], psY, ALU.add)
    ar.consolidate()


def emit_l0f(C):
    P = C.P
    ar = C.ar
    emit_ln(C)
    XH = [ar.v(i * 1024, 128, T, BF16) for i in range(8)]
    for c in range(4):
        emit_xhat_chunk(C, c * 512, 512,
                        [V(XH[i].ap[:, c * 512:(c + 1) * 512], XH[i].regs)
                         for i in range(8)],
                        P.take_bank(), P.take_bank())
    LG = ar.v(8192, 8, T, F32)
    CW = ar.v(10240, 8, T, F32)
    AFL = ar.v(12288, 128, 1024, BF16)
    BFL = ar.v(12800, 128, 1024, BF16)
    WGB = ar.v(13312, 128, 64, BF16)
    erpat = ar.v(13344, 8, 128, F32)
    P.dma("Q", erpat, C.d["c_erpat"])
    stg = ar.v(13568, 128, 1024, F32)
    P.dma("Q", stg, C.d["l0f_a"])
    P.v_copy(AFL, stg)
    P.dma("Q", stg, C.d["l0f_b"])
    P.v_copy(BFL, stg)
    wgstg = ar.v(13472, 128, 64, F32)
    P.dma("Q", wgstg, C.d["l0f_wg"])
    P.v_copy(WGB, wgstg)
    bG = P.take_bank()
    for c in range(4):
        cs = slice(c * 512, (c + 1) * 512)
        psG = P.ps(bG, 8, 512)
        for kt in range(8):
            P.t_mm(psG, V(WGB.ap[:, kt * 8:(kt + 1) * 8], WGB.regs),
                   V(XH[kt].ap[:, cs], XH[kt].regs), start=(kt == 0), stop=(kt == 7))
        P.s_copy(V(LG.ap[:, cs], LG.regs), psG)
    tokbase = 14592
    LGt = ar.v(tokbase, 128, 8, F32)
    m1 = ar.v(tokbase + 8, 128, 1, F32)
    m1n = ar.v(tokbase + 9, 128, 1, F32)
    m2 = ar.v(tokbase + 10, 128, 1, F32)
    qv = ar.v(tokbase + 11, 128, 1, F32)
    w1 = ar.v(tokbase + 12, 128, 1, F32)
    w2 = ar.v(tokbase + 13, 128, 1, F32)
    msk1 = ar.v(tokbase + 16, 128, 8, F32)
    msk2 = ar.v(tokbase + 24, 128, 8, F32)
    g2 = ar.v(tokbase + 32, 128, 8, F32)
    cwt = ar.v(tokbase + 40, 128, 8, F32)
    bT1, bT2 = P.take_bank(), P.take_bank()
    for tt in range(16):
        psL = P.ps(bT1, 128, 8)
        P.t_transpose(psL, V(LG.ap[:, tt * 128:(tt + 1) * 128], LG.regs),
                      V(C.ident.ap[0:8, 0:8], C.ident.regs))
        P.s_copy(LGt, psL)
        P.v_red(m1, LGt, ALU.max)
        P.v_red(m1n, LGt, ALU.max, negate=True)
        P.v_ts(msk1, LGt, m1, None, ALU.is_equal)
        P.v_stt(g2, msk1, -1e30, LGt, ALU.mult, ALU.add)
        P.v_red(m2, g2, ALU.max)
        P.v_ts(msk2, g2, m2, None, ALU.is_equal)
        P.s_act(qv, m2, AF.Exp, bias=m1n)
        P.v_ts(qv, qv, 1.0, None, ALU.add)
        P.v_recip(w1, qv)
        P.v_ts(qv, qv, -1.0, None, ALU.add)
        P.v_tt(w2, qv, w1, ALU.mult)
        P.v_ts(cwt, msk1, w1, None, ALU.mult)
        P.v_stt(cwt, msk2, w2, cwt, ALU.mult, ALU.add)
        psB = P.ps(bT2, 8, 128)
        P.t_transpose(psB, cwt, C.ident)
        P.s_copy(V(CW.ap[:, tt * 128:(tt + 1) * 128], CW.regs), psB)
    HW = ar.v(14336, 128, 512, BF16)
    for c in range(4):
        cs = slice(c * 512, (c + 1) * 512)
        psH = P.ps(P.take_bank(), 128, 512)
        for kt in range(8):
            P.t_mm(psH, V(AFL.ap[:, kt * 128:(kt + 1) * 128], AFL.regs),
                   V(XH[kt].ap[:, cs], XH[kt].regs), start=(kt == 0), stop=(kt == 7))
        P.s_copy(C.scr0, psH)
        psBC = P.ps(P.take_bank(), 128, 512)
        P.t_mm(psBC, erpat, V(CW.ap[:, cs], CW.regs), True, True)
        P.v_tt(HW, C.scr0, psBC, ALU.mult)
        for mm in range(8):
            psO = P.ps(P.take_bank(), 128, 512)
            P.t_mm(psO, V(BFL.ap[:, mm * 128:(mm + 1) * 128], BFL.regs), HW,
                   True, True)
            P.v_tt(C.res[mm][:, cs], C.res[mm][:, cs], psO, ALU.add)
    ar.consolidate()


def emit_l1a(C):
    P = C.P
    ar = C.ar
    emit_ln(C)
    KT = [ar.v(2048 + i * 1024, 128, T, BF16) for i in range(8)]
    VT = ar.v(10240, 128, 16 * 1024, BF16)
    WST = ar.v(25088, 128, 1024, F32)
    sc2 = ar.v(26112, 128, 256, F32)
    sc3 = ar.v(26368, 128, 256, F32)
    bvrow = ar.v(26624, 1, D, BF16)
    bvstg = ar.v(25088, 1, D, F32)
    P.dma("Q", bvstg, C.d["l1_bv_row"])
    P.v_copy(bvrow, bvstg)

    WKB = [ar.v(20480 + m * 512, 128, 1024, BF16) for m in range(8)]
    for m in range(8):
        stream_coltile(C, "l1_wk", m, 8, WST, WKB[m])
    for c in range(4):
        XHC = [ar.v(i * 256, 128, 512, BF16) for i in range(8)]
        emit_xhat_chunk(C, c * 512, 512, XHC, P.take_bank(), P.take_bank())
        for m in range(8):
            psK = P.ps(P.take_bank(), 128, 512)
            for kt in range(8):
                P.t_mm(psK, V(WKB[m].ap[:, kt * 128:(kt + 1) * 128], WKB[m].regs),
                       XHC[kt], start=(kt == 0), stop=(kt == 7))
            P.v_ts(V(KT[m].ap[:, c * 512:(c + 1) * 512], KT[m].regs), psK,
                   C.l1_bk[:, m:m + 1], None, ALU.add)
    if L1A_PHASES < 2:
        ar.consolidate()
        return
    WVB = [ar.v(20480 + kt * 512, 128, 1024, BF16) for kt in range(8)]
    for kt in range(8):
        P.dmaq(WVB[kt], C.dbf["l1_wv_rows"].ap[kt * 128:(kt + 1) * 128, :])
    for c in range(4):
        XHC = [ar.v(i * 256, 128, 512, BF16) for i in range(8)]
        emit_xhat_chunk(C, c * 512, 512, XHC, P.take_bank(), P.take_bank())
        for tloc in range(4):
            tt = c * 4 + tloc
            for ec in range(2):
                psV = P.ps(P.take_bank(), 128, 512)
                for kt in range(8):
                    P.t_mm(psV,
                           V(XHC[kt].ap[:, tloc * 128:(tloc + 1) * 128], XHC[kt].regs),
                           V(WVB[kt].ap[:, ec * 512:(ec + 1) * 512], WVB[kt].regs),
                           start=(kt == 0), stop=False)
                P.t_mm(psV, C.onesrowb[0:1, :],
                       V(bvrow.ap[:, ec * 512:(ec + 1) * 512], bvrow.regs),
                       start=False, stop=True)
                P.s_copy(V(VT.ap[:, tt * 1024 + ec * 512:tt * 1024 + ec * 512 + 512],
                           VT.regs), psV)
    if L1A_PHASES < 3:
        ar.consolidate()
        return
    WQB = [ar.v(20480 + m * 512, 128, 1024, BF16) for m in range(8)]
    for m in range(8):
        stream_coltile(C, "l1_wq", m, 8, WST, WQB[m])
    WOB = [ar.v(24576 + b * 512, 128, 1024, BF16) for b in range(2)]
    for c in range(L1A_CHUNKS):
        cw = 256
        cs = slice(c * cw, (c + 1) * cw)
        XHC = [ar.v(i * 128, 128, cw, BF16) for i in range(8)]
        QC = [ar.v(1024 + i * 128, 128, cw, BF16) for i in range(8)]
        ATT = [ar.v(18432 + tt * 128, 128, cw, BF16) for tt in range(16)]
        OUTC = [ar.v(i * 128, 128, cw, BF16) for i in range(8)]
        emit_xhat_chunk(C, c * cw, cw, XHC, P.take_bank(), P.take_bank())
        for m in range(8):
            psQ = P.ps(P.take_bank(), 128, cw)
            for kt in range(8):
                P.t_mm(psQ, V(WQB[m].ap[:, kt * 128:(kt + 1) * 128], WQB[m].regs),
                       XHC[kt], start=(kt == 0), stop=(kt == 7))
            P.v_ts(QC[m], psQ, C.l1_bq[:, m:m + 1], None, ALU.add)
        if L1A_PHASES < 4:
            continue
        bS1, bS2 = P.take_bank(), P.take_bank()
        for tt in range(16):
            psS = P.ps(bS1 if tt % 2 == 0 else bS2, 128, cw)
            for kt in range(8):
                P.t_mm(psS, V(KT[kt].ap[:, tt * 128:(tt + 1) * 128], KT[kt].regs),
                       QC[kt], start=(kt == 0), stop=(kt == 7))
            if L1A_THETA == 0:
                P.s_copy(ATT[tt], psS)
                continue
            P.s_act(C.scr0[:, 0:cw], psS, AF.Tanh, scale=0.5)
            P.s_act(C.scr1[:, 0:cw], psS, AF.Tanh)
            if L1A_THETA == 1:
                P.s_copy(ATT[tt], psS)
                continue
            P.v_ts(sc2, psS, 0.0, 0.1, ALU.max, ALU.mult)
            if L1A_THETA == 21:
                P.s_copy(ATT[tt], psS)
                continue
            P.v_stt(sc3, C.scr0[:, 0:cw], 0.1, sc2, ALU.mult, ALU.add)
            if L1A_THETA == 22:
                P.s_copy(ATT[tt], psS)
                continue
            P.v_stt(sc2, C.scr1[:, 0:cw], 0.15, sc3, ALU.mult, ALU.add)
            if L1A_THETA == 2:
                P.s_copy(ATT[tt], psS)
                continue
            P.s_act(ATT[tt], sc2, AF.Exp)
        if L1A_PHASES < 5:
            continue
        bR = P.take_bank()
        psR = V(P.psum[bR][0:1, 0:cw], P.psreg[bR])
        for tt in range(16):
            P.t_mm(psR, C.onescolb, ATT[tt], start=(tt == 0), stop=(tt == 15))
        crow = V(C.rowAB.ap[0:1, 2 * T:2 * T + cw], C.rowAB.regs)
        P.v_recip(crow, psR)
        psC = P.ps(7, 128, cw)
        P.t_mm(psC, C.onesrow[0:1, :], crow, True, True)
        if L1A_PHASES < 6:
            continue
        for dh in range(2):
            banks = [P.take_bank() for _ in range(4)]
            for mi in range(4):
                m = dh * 4 + mi
                psO = P.ps(banks[mi], 128, cw)
                for tt in range(16):
                    P.t_mm(psO,
                           V(VT.ap[:, tt * 1024 + m * 128:tt * 1024 + (m + 1) * 128],
                             VT.regs),
                           ATT[tt], start=(tt == 0), stop=(tt == 15))
            for mi in range(4):
                m = dh * 4 + mi
                psO = P.ps(banks[mi], 128, cw)
                P.s_copy(C.scr0[:, 0:cw], psO)
                P.v_tt(OUTC[m], C.scr0[:, 0:cw], psC, ALU.mult)
        if L1A_PHASES < 7:
            continue
        for m in range(8):
            wb = WOB[m % 2]
            stream_coltile(C, "l1_wo", m, 8, WST, wb)
            psY = P.ps(P.take_bank(), 128, cw)
            for kt in range(8):
                P.t_mm(psY, V(wb.ap[:, kt * 128:(kt + 1) * 128], wb.regs),
                       OUTC[kt], start=(kt == 0), stop=(kt == 7))
            P.v_stt(C.res[m][:, cs], psY, C.l1_bo[:, m:m + 1], C.res[m][:, cs],
                    ALU.add, ALU.add)
    ar.consolidate()


def _emit_router(C, LG, tokbase, topk):
    P = C.P
    ar = C.ar
    LGt = ar.v(tokbase, 128, 8, F32)
    m1 = ar.v(tokbase + 8, 128, 1, F32)
    m1n = ar.v(tokbase + 9, 128, 1, F32)
    sm = ar.v(tokbase + 10, 128, 1, F32)
    w1 = ar.v(tokbase + 11, 128, 1, F32)
    m2 = ar.v(tokbase + 12, 128, 1, F32)
    ex = ar.v(tokbase + 16, 128, 8, F32)
    msk1 = ar.v(tokbase + 24, 128, 8, F32)
    msk2 = ar.v(tokbase + 32, 128, 8, F32)
    g2 = ar.v(tokbase + 40, 128, 8, F32)
    cwt = ar.v(tokbase + 48, 128, 8, F32)
    bT1, bT2 = P.take_bank(), P.take_bank()
    for tt in range(16):
        psL = P.ps(bT1, 128, 8)
        P.t_transpose(psL, V(LG.ap[:, tt * 128:(tt + 1) * 128], LG.regs),
                      V(C.ident.ap[0:8, 0:8], C.ident.regs))
        P.s_copy(LGt, psL)
        P.v_red(m1, LGt, ALU.max)
        P.v_red(m1n, LGt, ALU.max, negate=True)
        P.s_act(ex, LGt, AF.Exp, bias=m1n)
        P.v_red(sm, ex, ALU.add)
        P.v_recip(w1, sm)
        P.v_ts(msk1, LGt, m1, None, ALU.is_equal)
        if topk == 1:
            P.v_ts(cwt, msk1, w1, None, ALU.mult)
        else:
            P.v_ts(ex, ex, w1, None, ALU.mult)
            P.v_stt(g2, msk1, -1e30, LGt, ALU.mult, ALU.add)
            P.v_red(m2, g2, ALU.max)
            P.v_ts(msk2, g2, m2, None, ALU.is_equal)
            P.v_tt(cwt, msk1, ex, ALU.mult)
            P.v_tt(msk2, msk2, ex, ALU.mult)
            P.v_tt(cwt, cwt, msk2, ALU.add)
        psB = P.ps(bT2, 8, 128)
        P.t_transpose(psB, cwt, C.ident)
        P.s_copy(V(C.LGb.ap[:, tt * 128:(tt + 1) * 128], C.LGb.regs), psB)


def _emit_moe(C, wgname, w1name, w2name, b1col, b2col, topk):
    P = C.P
    ar = C.ar
    emit_ln(C)
    XH = [ar.v(i * 1024, 128, T, BF16) for i in range(8)]
    for c in range(4):
        emit_xhat_chunk(C, c * 512, 512,
                        [V(XH[i].ap[:, c * 512:(c + 1) * 512], XH[i].regs)
                         for i in range(8)],
                        P.take_bank(), P.take_bank())
    LG = ar.v(24576, 8, T, F32)
    C.LGb = ar.v(24576, 8, T, BF16)
    WGB = ar.v(26688, 128, 64, BF16)
    wgstg = ar.v(26624, 128, 64, F32)
    P.dma("Q", wgstg, C.d[wgname])
    P.v_copy(WGB, wgstg)
    bG = P.take_bank()
    for c in range(4):
        cs = slice(c * 512, (c + 1) * 512)
        psG = P.ps(bG, 8, 512)
        for kt in range(8):
            P.t_mm(psG, V(WGB.ap[:, kt * 8:(kt + 1) * 8], WGB.regs),
                   V(XH[kt].ap[:, cs], XH[kt].regs), start=(kt == 0), stop=(kt == 7))
        P.s_copy(V(LG.ap[:, cs], LG.regs), psG)
    _emit_router(C, LG, 26752, topk)
    MErowB = V(C.rowABb[0:1, 4 * T:4 * T + 512], C.rowAB.regs)
    WBH = [ar.v(25600, 128, 2048, BF16), ar.v(26624, 128, 2048, BF16)]
    H = [ar.v(8192 + i * 1024, 128, T, BF16) for i in range(16)]
    save_rr = P.rr_banks
    P.rr_banks = [0, 1, 2, 3]
    for e in range(8):
        for ch in range(4):
            chs = slice(ch * 512, (ch + 1) * 512)
            P.dma("G", MErowB, V(C.LGb.ap[e:e + 1, chs], C.LGb.regs))
            psM = P.ps(4 + ch, 128, 512)
            P.t_mm(psM, C.onesrowb[0:1, :], MErowB, True, True)
        for hf in range(2):
            for mi in range(16):
                mt = hf * 16 + mi
                pr = mi // 2
                wb = WBH[pr % 2]
                if mi % 2 == 0:
                    P.dmaq(V(wb.ap.rearrange("p (a x) -> p a x", a=2), wb.regs),
                           C.dbf[w1name].ap[e, mt:mt + 2]
                           .rearrange("a p k j -> p a (k j)"))
                wsl = (mi % 2) * 1024
                for ch in range(4):
                    chs = slice(ch * 512, (ch + 1) * 512)
                    psH = P.ps(P.take_bank(), 128, 512)
                    for kt in range(8):
                        P.t_mm(psH, V(wb.ap[:, wsl + kt * 128:wsl + (kt + 1) * 128],
                                      wb.regs),
                               V(XH[kt].ap[:, chs], XH[kt].regs),
                               start=(kt == 0), stop=(kt == 7))
                    P.s_act(V(H[mi].ap[:, chs], H[mi].regs), psH, AF.Gelu,
                            bias=b1col[:, e * 32 + mt:e * 32 + mt + 1])
            for dm in range(8):
                wb = WBH[dm % 2]
                P.dmaq(wb, C.dbf[w2name].ap[e, dm][:, hf * 16:(hf + 1) * 16, :]
                       .rearrange("p k j -> p (k j)"))
                banks = [0, 1, 2, 3]
                for ch in range(4):
                    chs = slice(ch * 512, (ch + 1) * 512)
                    psY = P.ps(banks[ch], 128, 512)
                    for kt in range(16):
                        P.t_mm(psY, V(wb.ap[:, kt * 128:(kt + 1) * 128], wb.regs),
                               V(H[kt].ap[:, chs], H[kt].regs),
                               start=(kt == 0), stop=(kt == 15))
                for ch in range(4):
                    chs = slice(ch * 512, (ch + 1) * 512)
                    psY = P.ps(banks[ch], 128, 512)
                    if hf == 1:
                        P.v_ts(C.scr0, psY, b2col[:, e * 8 + dm:e * 8 + dm + 1],
                               None, ALU.add)
                    else:
                        P.s_copy(C.scr0, psY)
                    psM = P.ps(4 + ch, 128, 512)
                    P.v_tt(C.scr1, C.scr0, psM, ALU.mult)
                    P.v_tt(C.res[dm][:, chs], C.res[dm][:, chs], C.scr1, ALU.add)
    P.rr_banks = save_rr
    ar.consolidate()


def emit_l1f(C):
    _emit_moe(C, "l1f_wg", "l1f_w1", "l1f_w2", C.l1f_b1, C.l1f_b2, 1)


def emit_l2f(C):
    _emit_moe(C, "l2f_wg", "l2f_w1", "l2f_w2", C.l2f_b1, C.l2f_b2, 2)


def emit_l2a(C):
    P = C.P
    ar = C.ar
    emit_ln(C)
    QF = [ar.v(2048 + i * 256, 128, 512, BF16) for i in range(8)]
    KF = [ar.v(4096 + i * 256, 128, 512, BF16) for i in range(8)]
    VF = [ar.v(6144 + i * 512, 128, 512, F32) for i in range(8)]
    QKV = [ar.v(10240 + i * 256, 128, 512, BF16) for i in range(8)]
    WSTG = ar.v(12288, 128, 1024, F32)
    WB2 = [ar.v(13312 + b * 512, 128, 1024, BF16) for b in range(2)]
    hsum = ar.v(14336, 128, 128, BF16)
    P.dma("Q", hsum, C.d["c_hsum"])
    for c in range(4):
        cs = slice(c * 512, (c + 1) * 512)
        XHC = [ar.v(i * 256, 128, 512, BF16) for i in range(8)]
        emit_xhat_chunk(C, c * 512, 512, XHC, P.take_bank(), P.take_bank())
        for m in range(8):
            wb = WB2[m % 2]
            stream_coltile(C, "l2_wq", m, 8, WSTG, wb)
            psQ = P.ps(P.take_bank(), 128, 512)
            for kt in range(8):
                P.t_mm(psQ, V(wb.ap[:, kt * 128:(kt + 1) * 128], wb.regs),
                       XHC[kt], start=(kt == 0), stop=(kt == 7))
            P.s_act(C.scr0, psQ, AF.Square, bias=C.l2_bq[:, m:m + 1])
            P.s_act(QF[m], C.scr0, AF.Exp, scale=-0.5)
            wb = WB2[m % 2]
            stream_coltile(C, "l2_wk", m, 8, WSTG, wb)
            psK = P.ps(P.take_bank(), 128, 512)
            for kt in range(8):
                P.t_mm(psK, V(wb.ap[:, kt * 128:(kt + 1) * 128], wb.regs),
                       XHC[kt], start=(kt == 0), stop=(kt == 7))
            P.s_act(C.scr0, psK, AF.Square, bias=C.l2_bk[:, m:m + 1])
            P.s_act(KF[m], C.scr0, AF.Exp, scale=-0.5)
            wb = WB2[m % 2]
            stream_coltile(C, "l2_wv", m, 8, WSTG, wb)
            psV = P.ps(P.take_bank(), 128, 512)
            for kt in range(8):
                P.t_mm(psV, V(wb.ap[:, kt * 128:(kt + 1) * 128], wb.regs),
                       XHC[kt], start=(kt == 0), stop=(kt == 7))
            P.v_ts(VF[m], psV, C.l2_bv[:, m:m + 1], None, ALU.add)
        for m in range(8):
            P.v_tt(QF[m], QF[m], KF[m], ALU.mult)
            psCB = P.ps(P.take_bank(), 128, 512)
            P.t_mm(psCB, hsum, QF[m], True, True)
            P.v_tt(QKV[m], VF[m], psCB, ALU.mult)
        for m in range(8):
            wb = WB2[m % 2]
            stream_coltile(C, "l2_wo", m, 8, WSTG, wb)
            psY = P.ps(P.take_bank(), 128, 512)
            for kt in range(8):
                P.t_mm(psY, V(wb.ap[:, kt * 128:(kt + 1) * 128], wb.regs),
                       QKV[kt], start=(kt == 0), stop=(kt == 7))
            P.v_stt(C.res[m][:, cs], psY, C.l2_bo[:, m:m + 1], C.res[m][:, cs],
                    ALU.add, ALU.add)
    ar.consolidate()


def emit_l3a(C):
    P = C.P
    ar = C.ar
    emit_ln(C)
    save_rr = P.rr_banks
    P.rr_banks = [0, 1, 2, 3, 4, 5]
    CK = 256
    halo = ar.v(0, 128, 48, F32)
    xcbuf = ar.v(48, 128, 260, F32)
    xc_s = [ar.v(308 + i * 128, 128, CK, BF16) for i in range(16)]
    z_s = [ar.v(2356 + i * 128, 128, CK, BF16) for i in range(16)]
    u = [ar.v(4404 + i * 128, 128, CK, BF16) for i in range(16)]
    delta = [ar.v(6452 + i * 256, 128, CK, F32) for i in range(16)]
    w = [ar.v(10548 + i * 256, 128, CK, F32) for i in range(16)]
    y = [ar.v(14644 + i * 256, 128, CK, F32) for i in range(16)]
    yfin = [ar.v(18740 + i * 128, 128, CK, BF16) for i in range(16)]
    hstate = ar.v(20788, 128, 256, F32)
    dsB = ar.v(21044, 32, CK, F32)
    dsb16 = ar.v(21300, 16, CK, BF16)
    Brow = ar.v(21428, 1, CK, F32)
    dtW = ar.v(21684, 16, T, BF16)
    dAscr = [ar.v(22708, 128, CK, F32), ar.v(308, 128, CK, F32)]
    dBscr = [ar.v(22964, 128, CK, F32), ar.v(564, 128, CK, F32)]
    hscr = [ar.v(23220, 128, CK, F32), ar.v(832, 128, CK, F32)]
    Brow2 = [ar.v(21428, 1, CK, F32), ar.v(1088, 1, CK, F32)]
    dA_all = ([ar.v(25268 + i * 256, 128, CK, F32) for i in range(8)]
              + [ar.v(23476 + i * 256, 128, CK, F32) for i in range(4)]
              + [ar.v(1344 + i * 256, 128, CK, F32) for i in range(3)]
              + [ar.v(24500, 128, CK, F32)])
    XHC = [ar.v(23476 + i * 128, 128, CK, BF16) for i in range(8)]
    WSTG = ar.v(24500, 128, 512, F32)
    XPH = ar.v(25012, 128, 512, BF16)
    BUF2 = [ar.v(25268, 128, 2048, BF16),
            ar.v(26292, 128, 2048, BF16)]
    dtstg = ar.v(6452, 16, T, F32)
    P.dma("Q", dtstg, C.d["l3_dt"])
    P.v_copy(dtW, dtstg)
    P.memset(halo, 0.0)
    P.memset(hstate, 0.0)
    for ck in range(L3A_CKS):
        cs = slice(ck * CK, (ck + 1) * CK)
        emit_xhat_chunk(C, ck * CK, CK, XHC, P.take_bank(), P.take_bank())
        for mt in range(32):
            pr = mt // 2
            wb = BUF2[pr % 2]
            if mt % 2 == 0:
                P.dmaq(V(wb.ap.rearrange("p (a x) -> p a x", a=2), wb.regs),
                       C.dbf["l3_inproj"].ap[mt:mt + 2]
                       .rearrange("a p k j -> p a (k j)"))
            wsl = (mt % 2) * 1024
            psX = P.ps(P.take_bank(), 128, CK)
            for kt in range(8):
                P.t_mm(psX, V(wb.ap[:, wsl + kt * 128:wsl + (kt + 1) * 128], wb.regs),
                       XHC[kt], start=(kt == 0), stop=(kt == 7))
            if mt < 16:
                P.v_copy(V(xcbuf.ap[:, 0:3], xcbuf.regs),
                         V(halo.ap[:, mt * 3:mt * 3 + 3], halo.regs))
                P.v_ts(V(xcbuf.ap[:, 3:259], xcbuf.regs), psX,
                       C.l3_inproj_b[:, mt:mt + 1], None, ALU.add)
                P.v_copy(V(halo.ap[:, mt * 3:mt * 3 + 3], halo.regs),
                         V(xcbuf.ap[:, 256:259], xcbuf.regs))
                P.v_ts(dAscr[0], V(xcbuf.ap[:, 0:256], xcbuf.regs),
                       C.l3_convw[:, mt * 4:mt * 4 + 1], None, ALU.mult)
                for kk in range(1, 4):
                    P.v_stt(dAscr[0], V(xcbuf.ap[:, kk:kk + 256], xcbuf.regs),
                            C.l3_convw[:, mt * 4 + kk:mt * 4 + kk + 1],
                            dAscr[0], ALU.mult, ALU.add)
                P.s_act(xc_s[mt], dAscr[0], AF.Silu, bias=C.l3_convb[:, mt:mt + 1])
            else:
                P.s_act(z_s[mt - 16], psX, AF.Silu,
                        bias=C.l3_inproj_b[:, mt:mt + 1])
        P.dma("Q", WSTG, C.d["l3_xp_head"].rearrange("p k j -> p (k j)"))
        P.v_copy(XPH, WSTG, eng="S")
        psH = P.ps(P.take_bank(), 32, CK)
        for kt in range(16):
            P.t_mm(psH, V(XPH.ap[:, kt * 32:(kt + 1) * 32], XPH.regs),
                   xc_s[kt], start=(kt == 0), stop=(kt == 15))
        P.v_ts(dsB, psH, C.l3_xpb_head[0:32, :], None, ALU.add)
        P.v_copy(dsb16, V(dsB.ap[0:16, :], dsB.regs))
        for ut in range(16):
            src_w = C.dbf["l3_xp_u"].ap[ut].rearrange("p k j -> p (k j)")
            wb = BUF2[ut % 2]
            P.dmaq(wb, src_w)
            psU = P.ps(P.take_bank(), 128, CK)
            for kt in range(16):
                P.t_mm(psU, V(wb.ap[:, kt * 128:(kt + 1) * 128], wb.regs),
                       xc_s[kt], start=(kt == 0), stop=(kt == 15))
            P.v_ts(u[ut], psU, C.l3_xpb_u[:, ut:ut + 1], None, ALU.add)
        for ct in range(16):
            psD = P.ps(P.take_bank(), 128, CK)
            P.t_mm(psD, V(dtW.ap[:, ct * 128:(ct + 1) * 128], dtW.regs),
                   dsb16, True, True)
            pp = ct % 2
            P.s_act(dAscr[pp], psD, AF.Abs, bias=C.l3_dtb[:, ct:ct + 1])
            P.s_act(dBscr[pp], dAscr[pp], AF.Exp, scale=-1.0)
            P.s_act(dBscr[pp], dBscr[pp], AF.Ln, bias=C.onescol)
            P.s_act(hscr[pp], psD, AF.Relu, bias=C.l3_dtb[:, ct:ct + 1])
            P.v_tt(delta[ct], dBscr[pp], hscr[pp], ALU.add)
            P.v_tt(w[ct], delta[ct], u[ct], ALU.mult)
        P.dma("G", Brow2[0], V(dsB.ap[16:17, :], dsB.regs))
        P.t_mm(P.ps(6, 128, CK), C.onesrow[0:1, :], Brow2[0], True, True)
        for ct in range(16):
            P.s_act(dA_all[ct], delta[ct], AF.Exp, scale=-1.0)
        for s in range(L3A_SMAX):
            if s + 1 < L3A_SMAX:
                P.dma("G", Brow2[(s + 1) % 2],
                      V(dsB.ap[17 + s:18 + s, :], dsB.regs))
                P.t_mm(P.ps(6 + (s + 1) % 2, 128, CK), C.onesrow[0:1, :],
                       Brow2[(s + 1) % 2], True, True)
            psB = P.ps(6 + s % 2, 128, CK)
            for ct in range(16):
                pp = ct % 2
                P.v_tt(dBscr[pp], w[ct], psB, ALU.mult)
                hcol = V(hstate.ap[:, ct * 16 + s:ct * 16 + s + 1], hstate.regs)
                P.v_scan(hscr[pp], dA_all[ct], dBscr[pp], hcol)
                if s == 0:
                    P.v_copy(y[ct], hscr[pp])
                else:
                    P.v_tt(y[ct], y[ct], hscr[pp], ALU.add)
                P.v_copy(hcol, V(hscr[pp].ap[:, CK - 1:CK], hscr[pp].regs))
            if s + 1 < L3A_SMAX:
                for ct in range(16):
                    P.s_act(dA_all[ct], delta[ct], AF.Exp, scale=-(s + 2.0))
        for ct in range(16):
            P.v_stt(y[ct], u[ct], C.l3_D[:, ct:ct + 1], y[ct], ALU.mult, ALU.add)
            P.v_tt(yfin[ct], y[ct], z_s[ct], ALU.mult)
        for m in range(8):
            wb = BUF2[m % 2]
            P.dmaq(wb, C.dbf["l3_outproj"].ap[m].rearrange("p k j -> p (k j)"))
            psO = P.ps(P.take_bank(), 128, CK)
            for kt in range(16):
                P.t_mm(psO, V(wb.ap[:, kt * 128:(kt + 1) * 128], wb.regs),
                       yfin[kt], start=(kt == 0), stop=(kt == 15))
            P.v_stt(C.res[m][:, cs], psO, C.l3_outproj_b[:, m:m + 1],
                    C.res[m][:, cs], ALU.add, ALU.add)
    P.rr_banks = save_rr
    ar.consolidate()


def emit_l3f(C):
    P = C.P
    ar = C.ar
    nc = P.nc
    emit_ln(C)
    spill = V(nc.dram_tensor("spill_res", [D, T], F32).ap(), Reg(_Solo(), 0, 1))
    spill2 = V(nc.dram_tensor("spill_out", [D, T], F32).ap(), Reg(_Solo(), 0, 1))
    XH = [ar.v(i * 1024, 128, T, BF16) for i in range(8)]
    for c in range(4):
        emit_xhat_chunk(C, c * 512, 512,
                        [V(XH[i].ap[:, c * 512:(c + 1) * 512], XH[i].regs)
                         for i in range(8)],
                        P.take_bank(), P.take_bank())
    for i in range(8):
        P.dma("Q", spill[i * 128:(i + 1) * 128, :], C.res[i])
    hp = [C.resA.v(mi * 512, 128, 1024, BF16) for mi in range(32)]
    g1 = [ar.v(8192 + mi * 512, 128, 1024, BF16) for mi in range(32)]
    WBF = [ar.v(24576 + b * 512, 128, 1024, BF16) for b in range(2)]
    W2BF = [ar.v(25600 + b * 512, 128, 1024, BF16) for b in range(2)]
    HH = [ar.v(24576 + b * 1024, 128, 2048, BF16) for b in range(2)]
    for th in range(2):
        hs = [slice(th * 1024 + cc * 512, th * 1024 + cc * 512 + 512)
              for cc in range(2)]
        for mt in range(32):
            wb1 = WBF[mt % 2]
            wb2 = W2BF[mt % 2]
            P.dmaq(wb1, C.dbf["l3f_w1"].ap[mt].rearrange("p k j -> p (k j)"))
            P.dmaq(wb2, C.dbf["l3f_w2"].ap[mt].rearrange("p k j -> p (k j)"))
            for cc in range(2):
                psH = P.ps(P.take_bank(), 128, 512)
                psH2 = P.ps(P.take_bank(), 128, 512)
                for kt in range(8):
                    P.t_mm(psH, V(wb1.ap[:, kt * 128:(kt + 1) * 128], wb1.regs),
                           V(XH[kt].ap[:, hs[cc]], XH[kt].regs),
                           start=(kt == 0), stop=(kt == 7))
                for kt in range(8):
                    P.t_mm(psH2, V(wb2.ap[:, kt * 128:(kt + 1) * 128], wb2.regs),
                           V(XH[kt].ap[:, hs[cc]], XH[kt].regs),
                           start=(kt == 0), stop=(kt == 7))
                P.s_act(C.scr0, psH, AF.Silu, scale=C.l3f_s1[:, mt:mt + 1])
                P.v_ts(C.scr1, psH2, C.l3f_s2[:, mt:mt + 1], None, ALU.mult)
                P.v_tt(V(hp[mt].ap[:, cc * 512:(cc + 1) * 512], hp[mt].regs),
                       C.scr0, C.scr1, ALU.mult)
        for mt in range(32):
            bk0, bk1 = P.take_bank(), P.take_bank()
            for pc in range(2):
                wb = HH[pc % 2]
                P.dmaq(wb, C.dbf["l3f_h1"].ap[mt].rearrange("p k j -> p (k j)")
                       [:, pc * 2048:(pc + 1) * 2048])
                for cc in range(2):
                    psG = P.ps(bk0 if cc == 0 else bk1, 128, 512)
                    for kl in range(16):
                        kt = pc * 16 + kl
                        P.t_mm(psG, V(wb.ap[:, kl * 128:(kl + 1) * 128], wb.regs),
                               V(hp[kt].ap[:, cc * 512:(cc + 1) * 512], hp[kt].regs),
                               start=(kt == 0), stop=(kt == 31))
            for cc in range(2):
                psG = P.ps(bk0 if cc == 0 else bk1, 128, 512)
                P.s_act(V(g1[mt].ap[:, cc * 512:(cc + 1) * 512], g1[mt].regs),
                        psG, AF.Gelu, scale=C.l3f_sh1[:, mt:mt + 1])
        for mt in range(32):
            bk0, bk1 = P.take_bank(), P.take_bank()
            for pc in range(2):
                wb = HH[pc % 2]
                P.dmaq(wb, C.dbf["l3f_h2"].ap[mt].rearrange("p k j -> p (k j)")
                       [:, pc * 2048:(pc + 1) * 2048])
                for cc in range(2):
                    psG = P.ps(bk0 if cc == 0 else bk1, 128, 512)
                    for kl in range(16):
                        kt = pc * 16 + kl
                        P.t_mm(psG, V(wb.ap[:, kl * 128:(kl + 1) * 128], wb.regs),
                               V(g1[kt].ap[:, cc * 512:(cc + 1) * 512], g1[kt].regs),
                               start=(kt == 0), stop=(kt == 31))
            for cc in range(2):
                psG = P.ps(bk0 if cc == 0 else bk1, 128, 512)
                P.s_act(V(hp[mt].ap[:, cc * 512:(cc + 1) * 512], hp[mt].regs),
                        psG, AF.Gelu, scale=C.l3f_sh2[:, mt:mt + 1])
        for m in range(8):
            bk0, bk1 = P.take_bank(), P.take_bank()
            for pc in range(2):
                wb = HH[pc % 2]
                P.dmaq(wb, C.dbf["l3f_w3"].ap[m].rearrange("p k j -> p (k j)")
                       [:, pc * 2048:(pc + 1) * 2048])
                for cc in range(2):
                    psY = P.ps(bk0 if cc == 0 else bk1, 128, 512)
                    for kl in range(16):
                        kt = pc * 16 + kl
                        P.t_mm(psY, V(wb.ap[:, kl * 128:(kl + 1) * 128], wb.regs),
                               V(hp[kt].ap[:, cc * 512:(cc + 1) * 512], hp[kt].regs),
                               start=(kt == 0), stop=(kt == 31))
            for cc in range(2):
                psY = P.ps(bk0 if cc == 0 else bk1, 128, 512)
                P.v_ts(C.scr0, psY, C.l3f_s3[:, m:m + 1], None, ALU.mult)
                P.dma("G", C.scr1, spill[m * 128:(m + 1) * 128, hs[cc]])
                P.v_tt(C.scr0, C.scr0, C.scr1, ALU.add)
                P.dma("G", spill2[m * 128:(m + 1) * 128, hs[cc]], C.scr0)
    for i in range(8):
        P.dma("Q", C.res[i], spill2[i * 128:(i + 1) * 128, :])
    ar.consolidate()
    C.resA.consolidate()


EMITTERS = {"l0a": emit_l0a, "l0f": emit_l0f, "l1a": emit_l1a,
            "l1f": emit_l1f, "l2a": emit_l2a, "l2f": emit_l2f,
            "l3a": emit_l3a, "l3f": emit_l3f}

_PROG_CACHE = {}


def build_program(marrs, blocks):
    nc = bass.Bass("TRN2", target_bir_lowering=False)
    P = Prog(nc)
    C = setup_ctx(nc, P, marrs, blocks)
    load_block_statics(C, blocks)
    emit_convert(C, blocks)
    for i in range(8):
        P.dma("Q", C.res[i], C.xT[i * 128:(i + 1) * 128, :])
    mm_marks = [("start", 0)]
    for b in blocks:
        EMITTERS[b](C)
        C.resA.consolidate()
        C.ar.consolidate()
        mm_marks.append((b, P.mm_count))
    build_program.mm_marks = mm_marks
    for i in range(8):
        P.dma("Q", C.yT[i * 128:(i + 1) * 128, :], C.res[i])
    for qn in ("Q", "G", "S", "V"):
        for sl in range(P.NSLOT):
            key = f"{qn}#{sl}"
            P.raw_wait("Q", key, P.cnt[key])
    return P.build(), C.innames


def _get_prog(marrs, blocks):
    key = tuple(blocks)
    if key not in _PROG_CACHE:
        _PROG_CACHE[key] = build_program(marrs, blocks)
    return _PROG_CACHE[key]


def run_blocks(marrs, xts, blocks):
    nc, innames = _get_prog(marrs, blocks)
    in_maps = []
    for c in range(NCORE):
        d = {k: marrs[k] for k in innames if k != "xT"}
        d["xT"] = xts[c]
        in_maps.append(d)
    res = run_bass_kernel_spmd(nc, in_maps, core_ids=list(range(NCORE)))
    return [np.ascontiguousarray(np.asarray(res.results[c]["yT"]))
            for c in range(NCORE)]


def kernel(**inputs):
    m, xts = prep_inputs(inputs)
    groups = [ALL_BLOCKS] if FUSED else [[b] for b in ALL_BLOCKS]
    cur = xts
    for blocks in groups:
        cur = run_blocks(m, cur, blocks)
    out = np.stack([c.T for c in cur])
    return np.ascontiguousarray(out.astype(np.float32))
